# version 14
# speedup vs baseline: 1.5448x; 1.0048x over previous
import sys
sys.path.insert(0, '/opt/trn_rl_repo')
from contextlib import ExitStack
import numpy as np
import ml_dtypes
import concourse.bass as bass
import concourse.mybir as mybir
from concourse.bass_utils import run_bass_kernel_spmd

F32 = mybir.dt.float32
BF16 = mybir.dt.bfloat16
AF = mybir.ActivationFunctionType
ALU = mybir.AluOpType
AX = mybir.AxisListType

DM, DI, DS, DC, DTR = 1024, 2048, 16, 4, 64
T, NSEQ = 2048, 2
H, HD, LR, E = 16, 64, 64, 8
TCM, TCF = 256, 512
KD = DM // 128
ID = DI // 128

DEBUG_TAPS = False
TRUNC = None
STRICT_WAITS = False
ELIDE_ENGS = ()


class Prog:
    ENGS = ("sync", "tensor", "vector", "scalar", "gpsimd")
    NDMA = 32

    def __init__(self):
        self.ops = []
        self._barrier = {}
        self._dma_rr = 0
        self._dma_cum = [0] * self.NDMA

    def add(self, eng, fn, after=(), is_dma=False):
        deps = [d for d in after if d is not None]
        deps += [b for b in self._barrier.values() if b is not None]
        o = {"eng": eng, "fn": fn, "after": deps, "id": len(self.ops),
             "is_dma": is_dma, "inc": False, "cum": None, "dsem": None, "dcum": None}
        if is_dma:
            s = self._dma_rr % self.NDMA
            self._dma_rr += 1
            self._dma_cum[s] += 16
            o["dsem"] = s
            o["dcum"] = self._dma_cum[s]
        for d in deps:
            assert d["id"] < o["id"]
        self.ops.append(o)
        return o

    def dma(self, out, in_, after=()):
        return self.add("sync", lambda e, o=out, i=in_: e.dma_start(out=o, in_=i),
                        after=after, is_dma=True)

    def barrier(self):
        last = {}
        for o in self.ops:
            last[o["eng"]] = o
        self._barrier = last

    def emit(self, block, sems, dsems):
        if TRUNC is not None:
            self.ops = self.ops[:TRUNC]
            keep = set(id(o) for o in self.ops)
            for o in self.ops:
                o["after"] = [d for d in o["after"] if id(d) in keep]
        for o in self.ops:
            for d in o["after"]:
                if (not STRICT_WAITS and not d["is_dma"] and d["eng"] == o["eng"]
                        and o["eng"] in ELIDE_ENGS):
                    continue
                d["inc"] = True
        cum = {e: 0 for e in self.ENGS}
        for o in self.ops:
            if o["inc"] and not o["is_dma"]:
                cum[o["eng"]] += 1
                o["cum"] = cum[o["eng"]]

        def stream(eng_name, eng):
            waited = {e: 0 for e in self.ENGS}
            dwaited = [0] * self.NDMA
            for o in self.ops:
                if o["eng"] != eng_name:
                    continue
                for d in o["after"]:
                    if d["is_dma"]:
                        if d["dcum"] > dwaited[d["dsem"]]:
                            eng.wait_ge(dsems[d["dsem"]], d["dcum"])
                            dwaited[d["dsem"]] = d["dcum"]
                    else:
                        if (not STRICT_WAITS and d["eng"] == eng_name
                                and eng_name in ELIDE_ENGS):
                            continue
                        if d["cum"] is not None and d["cum"] > waited[d["eng"]]:
                            eng.wait_ge(sems[d["eng"]], d["cum"])
                            waited[d["eng"]] = d["cum"]
                if o["is_dma"]:
                    if o["dcum"] > 16 and o["dcum"] - 16 > dwaited[o["dsem"]]:
                        eng.wait_ge(dsems[o["dsem"]], o["dcum"] - 16)
                        dwaited[o["dsem"]] = o["dcum"] - 16
                    instr = o["fn"](eng)
                    instr.then_inc(dsems[o["dsem"]], 16)
                else:
                    instr = o["fn"](eng)
                    if o["inc"]:
                        instr.then_inc(sems[eng_name], 1)

        @block.sync
        def _(e):
            stream("sync", e)

        @block.tensor
        def _(e):
            stream("tensor", e)

        @block.vector
        def _(e):
            stream("vector", e)

        @block.scalar
        def _(e):
            stream("scalar", e)

        @block.gpsimd
        def _(e):
            stream("gpsimd", e)


def bcast_view(row_ap, width):
    return row_ap.unsqueeze(1).broadcast_to([1, 128, width])


def build_nc():
    nc = bass.Bass("TRN2", target_bir_lowering=False, debug=False,
                   detect_race_conditions=STRICT_WAITS)
    P = Prog()

    din = lambda n, sh, dt=F32: nc.dram_tensor(n, sh, dt, kind="ExternalInput").ap()
    x_h = din("x", [NSEQ, T, DM])
    winT_h = din("winT", [DM, 2 * DI])
    wxpT_h = din("wxpT", [DI, 96])
    wdtT_h = din("wdtT", [DTR, DI])
    woutT_h = din("woutT", [DI, DM])
    wq_h = din("wq", [DM, DM])
    wdown_h = din("wdown", [DM, LR])
    wupk_h = din("wupk", [LR, DM])
    wupv_h = din("wupv", [LR, DM])
    wo_h = din("wo", [DM, DM])
    wg_h = din("wg", [DM, E])
    we_h = din("we", [E, DM, DM])
    alog_h = din("alog", [DI, DS])
    dvec_h = din("dvec", [DI])
    convw_h = din("convw", [DI, DC])
    convb_h = din("convb", [DI])
    bdt_h = din("bdt", [DI])
    lng_h = din("lng", [DM])
    lnb_h = din("lnb", [DM])
    g1_h = din("g1", [DM])
    g2_h = din("g2", [DM])
    bq_h = din("bq", [DM])
    bdown_h = din("bdown", [LR])
    bupk_h = din("bupk", [DM])
    bupv_h = din("bupv", [DM])
    bo_h = din("bo", [DM])
    bg_h = din("bg", [E])
    eye32_h = din("eye32", [128, 128])
    eyebf_h = din("eyebf", [128, 128], BF16)

    out_h = nc.dram_tensor("out", [NSEQ, T, DM], F32, kind="ExternalOutput").ap()
    skind = "ExternalOutput" if DEBUG_TAPS else "Internal"
    xn_scr = nc.dram_tensor("xn_scr", [NSEQ, T, DM], F32, kind=skind).ap()
    s1_scr = nc.dram_tensor("s1_scr", [NSEQ, T, DM], F32, kind=skind).ap()
    dbg_h = {}
    if DEBUG_TAPS:
        dbg_h["o2sl"] = nc.dram_tensor("dbg_o2sl", [64, H, TCF], BF16, kind="ExternalOutput").ap()
        dbg_h["x1"] = nc.dram_tensor("dbg_x1", [128, 4, DM], F32, kind="ExternalOutput").ap()
        dbg_h["q2"] = nc.dram_tensor("dbg_q2", [128, H, 256], BF16, kind="ExternalOutput").ap()
        dbg_h["k2"] = nc.dram_tensor("dbg_k2", [128, H, 256], BF16, kind="ExternalOutput").ap()
        dbg_h["attn"] = nc.dram_tensor("dbg_attn", [128, 512], BF16, kind="ExternalOutput").ap()
        dbg_h["vg"] = nc.dram_tensor("dbg_vg", [16, 64, 128], BF16, kind="ExternalOutput").ap()
        dbg_h["attT"] = nc.dram_tensor("dbg_attT", [16, 4096], BF16, kind="ExternalOutput").ap()
    winT_scr = nc.dram_tensor("winT_b", [KD, 128, 2 * DI], BF16, kind="Internal").ap()
    wq_scr = nc.dram_tensor("wq_b", [KD, 128, DM], BF16, kind="Internal").ap()
    wo2_scr = nc.dram_tensor("wo2_b", [H, 64, DM], BF16, kind="Internal").ap()
    we_scr = nc.dram_tensor("we_b", [E, 2, 128, KD, 512], BF16, kind="Internal").ap()

    out_stores = []

    with ExitStack() as top:
        sems = {e: top.enter_context(nc.semaphore("sem_" + e)) for e in Prog.ENGS}
        dsems = [top.enter_context(nc.semaphore(f"dma{i}")) for i in range(Prog.NDMA)]
        sb = lambda n, sh, dt=F32: top.enter_context(nc.sbuf_tensor(n, sh, dt))

        eye32 = sb("s_eye32", [128, 128])
        eyebf = sb("s_eyebf", [128, 128], BF16)
        consts_sb = sb("consts_sb", [128, 4])
        cxz_sb = sb("cxz_sb", [128, 32])
        bq2_sb = sb("bq2_sb", [128, H])
        bdown_sb = sb("bdown_sb", [64, 1])
        bupk2_sb = sb("bupk2_sb", [128, H])
        wupv3_sb = sb("wupv3_sb", [65, 64 * H], BF16)
        bo_sb = sb("bo_sb", [128, KD])
        bg_sb = sb("bg_sb", [E, 1])
        wdown_sb = sb("wdown_sb", [128, KD * LR], BF16)
        wupk_sb = sb("wupk_sb", [64, DM], BF16)
        wg_sb = sb("wg_sb", [128, KD * E], BF16)

        pb = [top.enter_context(nc.psum_tensor(f"pb{i}", [128, 512], F32)) for i in range(6)]
        pbBF = top.enter_context(nc.psum_tensor("pbBF", [128, 1024], BF16))
        pbG = top.enter_context(nc.psum_tensor("pbG", [128, 512], F32))

        mres = ExitStack()
        msb0 = lambda n, sh, dt=F32: mres.enter_context(nc.sbuf_tensor(n, sh, dt))
        A_sb = msb0("A_sb", [128, ID * DS])
        D_sb = msb0("D_sb", [128, ID])
        convw_sb = msb0("convw_sb", [128, ID * DC])
        convb_sb = msb0("convb_sb", [128, ID])
        bdt_sb = msb0("bdt_sb", [128, ID])
        wxpT_sb = msb0("wxpT_sb", [128, ID * 96], BF16)
        wdtT_sb = msb0("wdtT_sb", [64, DI], BF16)
        woutT_sb = msb0("woutT_sb", [128, ID * DM], BF16)

        with ExitStack() as prep:
            psb = lambda n, sh, dt=F32: prep.enter_context(nc.sbuf_tensor(n, sh, dt))
            slab = psb("p_slab", [128, DM])
            slab_bf = psb("p_slab_bf", [128, DM], BF16)
            vcol = psb("p_vcol", [128, 16])
            lnb_bf = psb("p_lnb_bf", [128, KD], BF16)
            g18_sb = psb("p_g18", [128, KD])
            g1c_sb = psb("p_g1c", [128, KD])
            g2c_sb = psb("p_g2c", [128, KD])
            lng_c = psb("p_lng", [128, KD])

            P.add("vector", lambda e: e.memset(consts_sb[:, 0:1], 1e-5))
            P.add("vector", lambda e: e.memset(consts_sb[:, 1:2], 1e-6))
            P.add("vector", lambda e: e.memset(consts_sb[:, 2:3], 1.0))
            P.dma(eye32[:], eye32_h)
            P.dma(eyebf[:], eyebf_h)
            ld_a = P.dma(A_sb[:], alog_h.rearrange("(i p) n -> p i n", p=128))
            e_a = P.add("scalar", lambda e: e.activation(A_sb[:], A_sb[:], AF.Exp, bias=0.0, scale=1.0), after=[ld_a])
            P.add("vector", lambda e: e.tensor_scalar(A_sb[:], A_sb[:], -1.0, None, ALU.mult, ALU.bypass), after=[e_a])
            P.dma(D_sb[:], dvec_h.rearrange("(i p) -> p i", p=128))
            P.dma(convw_sb[:], convw_h.rearrange("(i p) c -> p i c", p=128))
            P.dma(convb_sb[:], convb_h.rearrange("(i p) -> p i", p=128))
            P.dma(bdt_sb[:], bdt_h.rearrange("(i p) -> p i", p=128))
            ld_g1 = P.dma(g1c_sb[:], g1_h.rearrange("(k p) -> p k", p=128))
            ld_g2 = P.dma(g2c_sb[:], g2_h.rearrange("(k p) -> p k", p=128))
            ld_lng = P.dma(lng_c[:], lng_h.rearrange("(k p) -> p k", p=128))
            ld_lnb = P.dma(vcol[:, 0:KD], lnb_h.rearrange("(k p) -> p k", p=128))
            cv_lnb = P.add("vector", lambda e: e.tensor_copy(lnb_bf[:], vcol[:, 0:KD]), after=[ld_lnb])
            lb1 = P.dma(bq2_sb[0:64, :], bq_h.rearrange("(h d) -> d h", d=HD))
            lb2 = P.dma(bq2_sb[64:128, :], bq_h.rearrange("(h d) -> d h", d=HD))
            P.add("vector", lambda e: e.tensor_scalar(bq2_sb[:], bq2_sb[:], 0.125, None, ALU.mult, ALU.bypass), after=[lb1, lb2])
            P.dma(bdown_sb[:], bdown_h.rearrange("(a p) -> p a", p=64))
            P.dma(bupk2_sb[0:64, :], bupk_h.rearrange("(h d) -> d h", d=HD))
            P.dma(bupk2_sb[64:128, :], bupk_h.rearrange("(h d) -> d h", d=HD))
            P.dma(bo_sb[:], bo_h.rearrange("(k p) -> p k", p=128))
            P.dma(bg_sb[:], bg_h.rearrange("(a p) -> p a", p=E))
            g18 = P.add("vector", lambda e: e.tensor_scalar(g18_sb[:], g1c_sb[:], 0.125, None, ALU.mult, ALU.bypass), after=[ld_g1])

            prev_st = None
            cmm = None
            for k in range(KD):
                for hh in range(4):
                    ldw = P.dma(slab[:], winT_h[k * 128:(k + 1) * 128, hh * 1024:(hh + 1) * 1024],
                                after=[prev_st])
                    cst = P.add("vector", lambda e: e.tensor_copy(slab_bf[:], slab[:]), after=[ldw, cmm])
                    for m8 in range(8):
                        m = hh * 8 + m8
                        cmm = P.add("tensor", lambda e, m=m, m8=m8, k=k, hh=hh: e.matmul(
                            pbG[:, m:m + 1], slab_bf[:, m8 * 128:(m8 + 1) * 128],
                            lnb_bf[:, k:k + 1],
                            start=(k == 0 and hh == 0 and m8 == 0),
                            stop=(k == KD - 1 and hh == 3 and m8 == 7)),
                            after=[cst, cv_lnb])
                    scl = P.add("scalar", lambda e, k=k: e.activation(
                        slab_bf[:], slab[:], AF.Copy, bias=0.0, scale=lng_c[:, k:k + 1]),
                        after=[cst, cmm, ld_lng])
                    prev_st = P.dma(winT_scr[k, :, hh * 1024:(hh + 1) * 1024], slab_bf[:], after=[scl])
            P.add("vector", lambda e: e.tensor_copy(cxz_sb[:], pbG[:, 0:32]), after=[cmm])

            for k in range(KD):
                ldw = P.dma(slab[:], wq_h[k * 128:(k + 1) * 128, :], after=[prev_st])
                scl = P.add("scalar", lambda e, k=k: e.activation(
                    slab_bf[:], slab[:], AF.Copy, bias=0.0, scale=g18_sb[:, k:k + 1]),
                    after=[ldw, g18])
                prev_st = P.dma(wq_scr[k], slab_bf[:], after=[scl])
            for k in range(KD):
                ldw = P.dma(slab[:], wo_h[k * 128:(k + 1) * 128, :], after=[prev_st])
                scl = P.add("scalar", lambda e: e.activation(
                    slab_bf[:], slab[:], AF.Copy, bias=0.0, scale=1.0), after=[ldw])
                st1 = P.dma(wo2_scr[2 * k], slab_bf[0:64, :], after=[scl])
                prev_st = P.dma(wo2_scr[2 * k + 1], slab_bf[64:128, :], after=[scl, st1])
            for ei in range(E):
                for k in range(KD):
                    ldw = P.dma(slab[:], we_h[ei, k * 128:(k + 1) * 128, :], after=[prev_st])
                    scl = P.add("scalar", lambda e, k=k: e.activation(
                        slab_bf[:], slab[:], AF.Copy, bias=0.0, scale=g2c_sb[:, k:k + 1]),
                        after=[ldw, ld_g2])
                    st1 = P.dma(we_scr[ei, 0, :, k, :], slab_bf[:, 0:512], after=[scl])
                    prev_st = P.dma(we_scr[ei, 1, :, k, :], slab_bf[:, 512:1024], after=[scl, st1])
            prev_wd = None
            prev_wg = None
            for k in range(KD):
                ldw = P.dma(slab[:, 0:LR], wdown_h[k * 128:(k + 1) * 128, :],
                            after=[prev_st, prev_wd])
                prev_st = None
                prev_wd = P.add("scalar", lambda e, k=k: e.activation(
                    wdown_sb[:, k * LR:(k + 1) * LR], slab[:, 0:LR], AF.Copy, bias=0.0,
                    scale=g1c_sb[:, k:k + 1]), after=[ldw, ld_g1])
                ldg = P.dma(slab[:, 512:512 + E], wg_h[k * 128:(k + 1) * 128, :],
                            after=[ldw, prev_wg])
                prev_wg = P.add("scalar", lambda e, k=k: e.activation(
                    wg_sb[:, k * E:(k + 1) * E], slab[:, 512:512 + E], AF.Copy, bias=0.0,
                    scale=g2c_sb[:, k:k + 1]), after=[ldg, ld_g2])
            l1 = P.dma(slab[0:64, :], wupk_h, after=[prev_wd, prev_wg])
            c1 = P.add("vector", lambda e: e.tensor_copy(wupk_sb[:], slab[0:64, :]), after=[l1])
            l2 = P.dma(slab[0:64, :], wupv_h, after=[l1, c1])
            c2 = P.add("vector", lambda e: e.tensor_copy(
                wupv3_sb[0:64, :].rearrange("p (d g) -> p d g", d=HD),
                slab[0:64, :].rearrange("p (g d) -> p g d", g=H).transpose([0, 2, 1])),
                after=[l2])
            l2b = P.dma(slab[64:65, 0:DM], bupv_h.unsqueeze(0), after=[l2])
            c2b = P.add("vector", lambda e: e.tensor_copy(
                wupv3_sb[64:65, :].rearrange("p (d g) -> p d g", d=HD),
                slab[64:65, 0:DM].rearrange("p (g d) -> p g d", g=H).transpose([0, 2, 1])),
                after=[l2b])
            prev_cp = c2b
            for i in range(ID):
                ld3 = P.dma(slab[:, 0:96], wxpT_h[i * 128:(i + 1) * 128, :], after=[prev_cp])
                prev_cp = P.add("vector", lambda e, i=i: e.tensor_copy(
                    wxpT_sb[:, i * 96:(i + 1) * 96], slab[:, 0:96]), after=[ld3])
            l4 = P.dma(slab[0:64, 0:DI // 2], wdtT_h[:, 0:DI // 2], after=[prev_cp])
            c4 = P.add("vector", lambda e: e.tensor_copy(wdtT_sb[:, 0:DI // 2], slab[0:64, 0:DI // 2]), after=[l4])
            l5 = P.dma(slab[0:64, 0:DI // 2], wdtT_h[:, DI // 2:], after=[l4, c4])
            prev_cp = P.add("vector", lambda e: e.tensor_copy(wdtT_sb[:, DI // 2:], slab[0:64, 0:DI // 2]), after=[l5])
            for i in range(ID):
                ldw = P.dma(slab[:], woutT_h[i * 128:(i + 1) * 128, :], after=[prev_cp])
                prev_cp = P.add("vector", lambda e, i=i: e.tensor_copy(
                    woutT_sb[:, i * DM:(i + 1) * DM], slab[:]), after=[ldw])
            P.barrier()

        with ExitStack() as pm:
            msb = lambda n, sh, dt=F32: pm.enter_context(nc.sbuf_tensor(n, sh, dt))
            x_tm = msb("x_tm", [128, 2, DM])
            xn_tm = msb("xn_tm", [128, 2, DM])
            sx_tm = msb("sx_tm", [128, 2, DM])
            xnA_fm = msb("xnA_fm", [128, KD, TCM], BF16)
            stat = msb("stat", [128, 32])
            xs_scr = [msb(f"xs_scr{i}", [128, DC - 1 + TCM]) for i in range(2)]
            ct_a = msb("ct_a", [128, TCM])
            ct_b = msb("ct_b", [128, TCM])
            halo_sb = msb("halo_sb", [128, ID * (DC - 1)])
            xc_bf = msb("xc_bf", [128, ID, TCM], BF16)
            zsil = msb("zsil", [128, ID, TCM], BF16)
            dt_fm = msb("dt_fm", [128, ID, TCM])
            udt_fm = msb("udt_fm", [128, ID, TCM], BF16)
            dblBC_bf = msb("dblBC_bf", [96, TCM], BF16)
            dtlo_bf = msb("dtlo_bf", [64, TCM], BF16)
            e1_scr = msb("e1_scr", [128, TCM])
            Bbc = msb("Bbc", [128, DS, TCM], BF16)
            Cbc = msb("Cbc", [128, DS, TCM], BF16)
            dA_s = [msb(f"dA{i}", [128, TCM]) for i in range(2)]
            U_all = msb("U_all", [128, DS, TCM], BF16)
            h_i = msb("h_i", [128, DS, TCM], BF16)
            pr_s = msb("pr_s", [128, TCM])
            prod_s = msb("prod_s", [128, DS, TCM], BF16)
            red_a = msb("red_a", [128, 8, TCM])
            red_b = msb("red_b", [128, 4, TCM])
            red_c = msb("red_c", [128, 2, TCM])
            y_s = msb("y_s", [128, TCM])
            hstate = msb("hstate", [128, ID * DS])
            ytot_bf = msb("ytot_bf", [128, ID, TCM], BF16)
            outA_sb = [msb(f"outA{i}", [128, TCM]) for i in range(2)]
            S1_tm = xn_tm
            winbuf = [msb(f"winbuf{i}", [128, KD, 256], BF16) for i in range(2)]

            NCH = T // TCM
            for s in range(NSEQ):
                P.barrier()
                P.add("vector", lambda e: e.memset(hstate[:], 0.0))
                P.add("vector", lambda e: e.memset(halo_sb[:], 0.0))
                for ci in range(NCH):
                    P.barrier()
                    t0 = ci * TCM
                    ld_x = P.dma(x_tm[:], x_h[s, t0:t0 + TCM, :].rearrange("(a p) d -> p a d", p=128))

                    sq_v = red_a[:, 0:4, :].rearrange("p a b -> p (a b)")
                    sq_prev = [None]

                    def stats_multi(entries, eps):
                        a1, a2, m1, m2, vr, sd, res = {}, {}, {}, {}, {}, {}, []
                        for k, (src, c0, deps) in enumerate(entries):
                            a1[k] = P.add("scalar", lambda e, src=src, c0=c0: e.activation(
                                sq_v, src, AF.Copy, bias=0.0, scale=1.0,
                                accum_out=stat[:, c0:c0 + 1]), after=deps + [sq_prev[0]])
                            sq_prev[0] = a1[k]
                        for k, (src, c0, deps) in enumerate(entries):
                            a2[k] = P.add("scalar", lambda e, src=src, c0=c0: e.activation(
                                sq_v, src, AF.Square, bias=0.0, scale=1.0,
                                accum_out=stat[:, c0 + 1:c0 + 2]), after=deps + [sq_prev[0]])
                            sq_prev[0] = a2[k]
                        for k, (src, c0, deps) in enumerate(entries):
                            m1[k] = P.add("vector", lambda e, c0=c0: e.tensor_scalar(
                                stat[:, c0 + 2:c0 + 3], stat[:, c0:c0 + 1], 1.0 / DM, None,
                                ALU.mult, ALU.bypass), after=[a1[k]])
                        for k, (src, c0, deps) in enumerate(entries):
                            m2[k] = P.add("vector", lambda e, c0=c0: e.tensor_scalar(
                                stat[:, c0 + 3:c0 + 4], stat[:, c0 + 2:c0 + 3],
                                stat[:, c0 + 2:c0 + 3], None, ALU.mult, ALU.bypass),
                                after=[m1[k]])
                        for k, (src, c0, deps) in enumerate(entries):
                            vr[k] = P.add("vector", lambda e, c0=c0: e.scalar_tensor_tensor(
                                stat[:, c0 + 4:c0 + 5], stat[:, c0 + 1:c0 + 2], 1.0 / DM,
                                stat[:, c0 + 3:c0 + 4], ALU.mult, ALU.subtract),
                                after=[a2[k], m2[k]])
                        for k, (src, c0, deps) in enumerate(entries):
                            sd[k] = P.add("scalar", lambda e, c0=c0: e.activation(
                                stat[:, c0 + 5:c0 + 6], stat[:, c0 + 4:c0 + 5], AF.Sqrt,
                                bias=eps, scale=1.0), after=[vr[k]])
                        for k, (src, c0, deps) in enumerate(entries):
                            res.append(P.add("vector", lambda e, c0=c0: e.reciprocal(
                                stat[:, c0 + 6:c0 + 7], stat[:, c0 + 5:c0 + 6]),
                                after=[sd[k]]))
                        return res

                    ivs = stats_multi([(x_tm[:, j, :], 16 * j, [ld_x]) for j in range(2)],
                                      consts_sb[:, 0:1])
                    xn_ops, xnA_ops, sx_ops = [], [], []
                    for j in range(2):
                        xn_ops.append(P.add("vector", lambda e, j=j, a=16 * j + 2, b=16 * j + 6:
                                            e.tensor_scalar(
                            xn_tm[:, j, :], x_tm[:, j, :], stat[:, a:a + 1], stat[:, b:b + 1],
                            ALU.subtract, ALU.mult), after=[ivs[j]]))
                    for j in range(2):
                        sx_ops.append(P.add("vector", lambda e, j=j: e.tensor_tensor(
                            sx_tm[:, j, :], x_tm[:, j, :], xn_tm[:, j, :], ALU.add),
                            after=[xn_ops[j]]))
                    ivs2 = stats_multi([(xn_tm[:, j, :], 16 * j + 8, [xn_ops[j]])
                                        for j in range(2)], consts_sb[:, 0:1])
                    for j in range(2):
                        xnA_ops.append(P.add("vector", lambda e, j=j, a=16 * j + 10,
                                             b=16 * j + 14: e.tensor_scalar(
                            x_tm[:, j, :], xn_tm[:, j, :], stat[:, a:a + 1], stat[:, b:b + 1],
                            ALU.subtract, ALU.mult), after=[ivs2[j], sx_ops[j]]))
                    st_xn = P.dma(xn_scr[s, t0:t0 + TCM, :].rearrange("(a p) d -> p a d", p=128),
                                  xn_tm[:], after=xn_ops)

                    fm_ev = []
                    for k in range(KD):
                        bank = pb[2 + (k % 2)]
                        tj = []
                        for j in range(2):
                            tj.append(P.add("tensor", lambda e, j=j, k=k, b=bank: e.transpose(
                                b[:, j * 128:(j + 1) * 128],
                                x_tm[:, j, k * 128:(k + 1) * 128], eye32[:]),
                                after=[xnA_ops[j], fm_ev[k - 2] if k >= 2 else None]))
                        fm_ev.append(P.add("scalar", lambda e, k=k, b=bank: e.activation(
                            xnA_fm[:, k, :], b[:, 0:TCM], AF.Copy, bias=0.0, scale=1.0), after=tj))

                    conv_done = [None] * ID
                    xcbf_done = [None] * ID
                    zs_done = [None] * ID
                    ev_m = [None] * 32
                    mm_last_of_g = [None] * 16
                    for g in range(16):
                        wld = []
                        war = [mm_last_of_g[g - 2]] if g >= 2 else []
                        for k in range(KD):
                            wld.append(P.dma(winbuf[g % 2][:, k, :],
                                             winT_scr[k, :, g * 256:(g + 1) * 256], after=war))
                        for m in range(g * 2, g * 2 + 2):
                            bank = pb[m % 2]
                            mm = None
                            for k in range(KD):
                                mm = P.add("tensor", lambda e, m=m, k=k, g=g, b=bank: e.matmul(
                                    b[:, 0:TCM], winbuf[g % 2][:, k, (m % 2) * 128:(m % 2 + 1) * 128],
                                    xnA_fm[:, k, :], start=(k == 0), stop=(k == KD - 1)),
                                    after=[wld[k], fm_ev[k], ev_m[m - 2] if m >= 2 else None])
                            mm_last_of_g[g] = mm
                            if m < 16:
                                i = m
                                ev = P.add("scalar", lambda e, m=m, i=i, b=bank: e.activation(
                                    xs_scr[i % 2][:, DC - 1:], b[:, 0:TCM], AF.Identity,
                                    bias=cxz_sb[:, m:m + 1], scale=1.0),
                                    after=[mm, conv_done[i - 2] if i >= 2 else None])
                                ev_m[m] = ev
                                hl = P.add("vector", lambda e, i=i: e.tensor_copy(
                                    xs_scr[i % 2][:, 0:DC - 1], halo_sb[:, i * 3:(i + 1) * 3]),
                                    after=[ev])
                                c1 = P.add("vector", lambda e, i=i: e.tensor_scalar(
                                    ct_a[:], xs_scr[i % 2][:, 0:TCM], convw_sb[:, 4 * i:4 * i + 1],
                                    None, ALU.mult, ALU.bypass), after=[ev, hl])
                                c2 = P.add("vector", lambda e, i=i: e.scalar_tensor_tensor(
                                    ct_b[:], xs_scr[i % 2][:, 1:1 + TCM], convw_sb[:, 4 * i + 1:4 * i + 2],
                                    ct_a[:], ALU.mult, ALU.add), after=[c1])
                                c3 = P.add("vector", lambda e, i=i: e.scalar_tensor_tensor(
                                    ct_a[:], xs_scr[i % 2][:, 2:2 + TCM], convw_sb[:, 4 * i + 2:4 * i + 3],
                                    ct_b[:], ALU.mult, ALU.add), after=[c2])
                                c4 = P.add("vector", lambda e, i=i: e.scalar_tensor_tensor(
                                    ct_b[:], xs_scr[i % 2][:, 3:3 + TCM], convw_sb[:, 4 * i + 3:4 * i + 4],
                                    ct_a[:], ALU.mult, ALU.add), after=[c3])
                                sv = P.add("vector", lambda e, i=i: e.tensor_copy(
                                    halo_sb[:, i * 3:(i + 1) * 3], xs_scr[i % 2][:, TCM:TCM + 3]),
                                    after=[ev])
                                sil = P.add("scalar", lambda e, i=i: e.activation(
                                    xc_bf[:, i, :], ct_b[:], AF.Silu, bias=convb_sb[:, i:i + 1],
                                    scale=1.0), after=[c4])
                                conv_done[i] = P.add("vector", lambda e: e.nop(), after=[c4, sv, sil])
                                xcbf_done[i] = sil
                            else:
                                i = m - 16
                                ev = P.add("scalar", lambda e, m=m, i=i, b=bank: e.activation(
                                    zsil[:, i, :], b[:, 0:TCM], AF.Silu, bias=cxz_sb[:, m:m + 1],
                                    scale=1.0), after=[mm])
                                ev_m[m] = ev
                                zs_done[i] = ev

                    mm = None
                    for i in range(ID):
                        mm = P.add("tensor", lambda e, i=i: e.matmul(
                            pb[0][0:96, 0:TCM], wxpT_sb[:, i * 96:(i + 1) * 96],
                            xc_bf[:, i, :], start=(i == 0), stop=(i == ID - 1)),
                            after=[xcbf_done[i], ev_m[30]])
                    ev_dtlo = P.add("scalar", lambda e: e.activation(
                        dtlo_bf[:], pb[0][0:64, 0:TCM], AF.Copy, bias=0.0, scale=1.0), after=[mm])
                    ev_bc = P.add("vector", lambda e: e.tensor_copy(
                        dblBC_bf[64:96, :], pb[0][64:96, 0:TCM]), after=[mm])
                    bc_ops = []
                    for n in range(DS):
                        bc_ops.append(P.dma(Bbc[:, n, :], bcast_view(dblBC_bf[64 + n:65 + n, :], TCM),
                                            after=[ev_bc]))
                        bc_ops.append(P.dma(Cbc[:, n, :], bcast_view(dblBC_bf[80 + n:81 + n, :], TCM),
                                            after=[ev_bc]))

                    dt_done = [None] * ID
                    udt_done = [None] * ID
                    ev_prev = None
                    for i in range(ID):
                        bank = pb[1] if i % 2 == 0 else pb[4]
                        mmdt = P.add("tensor", lambda e, i=i, b=bank: e.matmul(
                            b[:, 0:TCM], wdtT_sb[:, i * 128:(i + 1) * 128], dtlo_bf[:],
                            start=True, stop=True),
                            after=[ev_dtlo, ev_m[31] if i == 0 else None,
                                   dt_done[i - 2] if i >= 2 else None])
                        ex1 = P.add("scalar", lambda e, i=i, b=bank: e.activation(
                            e1_scr[:], b[:, 0:TCM], AF.Exp, bias=bdt_sb[:, i:i + 1], scale=1.0),
                            after=[mmdt, ev_prev])
                        ex2 = P.add("scalar", lambda e, i=i: e.activation(
                            dt_fm[:, i, :], e1_scr[:], AF.Ln, bias=consts_sb[:, 2:3], scale=1.0), after=[ex1])
                        ev_prev = ex2
                        dt_done[i] = ex2
                        udt_done[i] = P.add("vector", lambda e, i=i: e.tensor_tensor(
                            udt_fm[:, i, :], dt_fm[:, i, :], xc_bf[:, i, :], ALU.mult), after=[ex2])

                    fin = [None] * ID
                    tts_hist = {}
                    prev_f1 = None
                    hsave_prev = None
                    prev_reda = None
                    pend = None

                    def emit_f1(iq, yf):
                        nonlocal prev_f1
                        f1 = P.add("vector", lambda e, i=iq: e.scalar_tensor_tensor(
                            pr_s[:], xc_bf[:, i, :], D_sb[:, i:i + 1], y_s[:],
                            ALU.mult, ALU.add), after=[yf, fin[iq - 1] if iq > 0 else None])
                        prev_f1 = f1
                        return f1

                    def emit_fin(iq, f1):
                        fin[iq] = P.add("vector", lambda e, i=iq: e.tensor_tensor(
                            ytot_bf[:, i, :], pr_s[:], zsil[:, i, :], ALU.mult),
                            after=[f1, zs_done[iq]])
                    for i in range(ID):
                        uall = P.add("vector", lambda e, i=i: e.tensor_tensor(
                            U_all[:, 0:DS, :], Bbc[:, 0:DS, :],
                            udt_fm[:, i, :].unsqueeze(1).broadcast_to([128, DS, TCM]),
                            ALU.mult),
                            after=[udt_done[i]] + ([bc_ops[2 * n] for n in range(DS)]
                                                   if i == 0 else []))
                        for n in range(DS):
                            idx = i * DS + n
                            ex = P.add("scalar", lambda e, i=i, n=n, sl=idx % 2: e.activation(
                                dA_s[sl][:], dt_fm[:, i, :], AF.Exp, bias=0.0,
                                scale=A_sb[:, i * DS + n:i * DS + n + 1]),
                                after=[dt_done[i], tts_hist.get(idx - 2)])
                            tts = P.add("vector", lambda e, i=i, n=n, idx=idx: e.tensor_tensor_scan(
                                h_i[:, n, :], dA_s[idx % 2][:],
                                U_all[:, n, :],
                                hstate[:, i * DS + n:i * DS + n + 1], ALU.mult, ALU.add),
                                after=[ex, uall, hsave_prev if n == 0 else None])
                            tts_hist[idx] = tts
                        pf1 = emit_f1(*pend) if pend is not None else None
                        hsave_prev = P.add("vector", lambda e, i=i: e.tensor_copy(
                            hstate[:, i * DS:(i + 1) * DS],
                            h_i[:, 0:DS, TCM - 1:TCM].squeeze(2)),
                            after=[tts_hist[i * DS + DS - 1]])
                        prodall = P.add("vector", lambda e, i=i: e.tensor_tensor(
                            prod_s[:, 0:DS, :], h_i[:, 0:DS, :], Cbc[:, 0:DS, :], ALU.mult),
                            after=[tts_hist[i * DS + DS - 1], prev_reda]
                            + ([bc_ops[2 * n + 1] for n in range(DS)] if i == 0 else []))
                        reda = P.add("gpsimd", lambda e: e.tensor_tensor(
                            red_a[:, 0:8, :].rearrange("p a b -> p (a b)"),
                            prod_s[:, 0:8, :].rearrange("p a b -> p (a b)"),
                            prod_s[:, 8:16, :].rearrange("p a b -> p (a b)"), ALU.add),
                            after=[prodall])
                        prev_reda = reda
                        redb = P.add("gpsimd", lambda e: e.tensor_tensor(
                            red_b[:, 0:4, :].rearrange("p a b -> p (a b)"),
                            red_a[:, 0:4, :].rearrange("p a b -> p (a b)"),
                            red_a[:, 4:8, :].rearrange("p a b -> p (a b)"), ALU.add),
                            after=[reda])
                        redc = P.add("gpsimd", lambda e: e.tensor_tensor(
                            red_c[:, 0:2, :].rearrange("p a b -> p (a b)"),
                            red_b[:, 0:2, :].rearrange("p a b -> p (a b)"),
                            red_b[:, 2:4, :].rearrange("p a b -> p (a b)"), ALU.add),
                            after=[redb])
                        if pend is not None:
                            emit_fin(pend[0], pf1)
                            pend = None
                        yfin = P.add("gpsimd", lambda e: e.tensor_tensor(
                            y_s[:], red_c[:, 0, :], red_c[:, 1, :], ALU.add),
                            after=[redc, prev_f1])
                        pend = (i, yfin)
                    pf1 = emit_f1(*pend)
                    emit_fin(pend[0], pf1)
                    pend = None

                    s1_parts = []
                    prev_ads = [None, None]
                    wout_free = [None, None]
                    for mo in range(KD):
                        bank = pb[1] if mo % 2 == 0 else pb[5]
                        mm = None
                        for i in range(ID):
                            mm = P.add("tensor", lambda e, mo=mo, i=i, b=bank: e.matmul(
                                b[:, 0:TCM], woutT_sb[:, i * DM + mo * 128:i * DM + (mo + 1) * 128],
                                ytot_bf[:, i, :], start=(i == 0), stop=(i == ID - 1)),
                                after=[fin[i], wout_free[mo % 2],
                                       dt_done[ID - 1] if mo == 0 else None])
                            if i == 0:
                                pass
                        ev = P.add("scalar", lambda e, mo=mo, b=bank: e.activation(
                            outA_sb[mo % 2][:], b[:, 0:TCM], AF.Copy, bias=0.0, scale=1.0), after=[mm])
                        wout_free[mo % 2] = ev
                        gbank = pbG if mo % 2 == 0 else pb[4]
                        tts_ = []
                        for j in range(2):
                            tts_.append(P.add("tensor", lambda e, mo=mo, j=j, gb=gbank: e.transpose(
                                gb[:, j * 128:(j + 1) * 128],
                                outA_sb[mo % 2][:, j * 128:(j + 1) * 128], eye32[:]),
                                after=[ev, prev_ads[mo % 2]]))
                        wout_free[mo % 2] = tts_[-1]
                        ad = P.add("vector", lambda e, mo=mo, gb=gbank: e.tensor_tensor(
                            S1_tm[:, 0:2, mo * 128:(mo + 1) * 128],
                            gb[:, 0:256].rearrange("p (a b) -> p a b", a=2),
                            sx_tm[:, 0:2, mo * 128:(mo + 1) * 128], ALU.add),
                            after=tts_ + ([st_xn] if mo == 0 else []))
                        prev_ads[mo % 2] = ad
                        s1_parts.append(ad)
                    st_s1 = P.dma(s1_scr[s, t0:t0 + TCM, :].rearrange("(a p) d -> p a d", p=128),
                                  S1_tm[:], after=s1_parts)
            P.barrier()
        mres.close()

        with ExitStack() as pf:
            fsb = lambda n, sh, dt=F32: pf.enter_context(nc.sbuf_tensor(n, sh, dt))
            fxn_tm = fsb("f_xn", [128, 4, DM])
            x1_tm = fsb("f_x1", [128, 4, DM])
            out_tm = fsb("f_out", [128, 4, DM])
            h_scr = fsb("f_hs", [128, DM])
            fsq_scr = fsb("f_sq", [128, DM])
            fstat = fsb("f_stat", [128, 80])
            h1_fm = fsb("f_h1fm", [128, KD, TCF], BF16)
            h2_fm = fsb("f_h2fm", [128, KD, TCF], BF16)
            h2e_fm = [fsb(f"f_h2e{i}", [128, KD, TCF], BF16) for i in range(2)]
            q2ARR = fsb("f_q2", [128, H, 256], BF16)
            k2ARR = fsb("f_k2", [128, H, 256], BF16)
            qb1_scr = fsb("f_qb1", [64, H, 64], BF16)
            kb1_scr = fsb("f_kb1", [64, H, 64], BF16)
            vg_sb = fsb("f_vg", [16, 64, 128], BF16)
            r1_bf = fsb("f_rbf", [65, TCF], BF16)
            attn_b = fsb("f_attnb", [128, 512], BF16)
            attT_sb = fsb("f_attT", [16, 4096], BF16)
            out2slabs = fsb("f_o2sl", [64, H, TCF], BF16)
            aproj_sb = [fsb(f"f_ap{i}", [128, TCF]) for i in range(2)]
            g_tm = fsb("f_gtm", [128, 4 * E])
            gn_tm = fsb("f_gntm", [128, 4 * E])
            gn_bf = fsb("f_gnbf", [8, TCF], BF16)
            gbc = [fsb(f"f_gbc{i}", [128, TCF], BF16) for i in range(2)]
            moe_sb = [fsb(f"f_moe{i}", [128, TCF]) for i in range(2)]
            webuf = [fsb(f"f_we{i}", [128, KD, 512], BF16) for i in range(2)]
            wo2_sb = fsb("f_wo2", [64, H * 512], BF16)

            def wq_slice(k, h):
                base = (k % 4) * 2
                off = h * HD
                return webuf[k // 4][:, base + off // 512, off % 512:off % 512 + HD]

            def wo_load(half, war):
                return [P.dma(wo2_sb[:, h * 512:(h + 1) * 512],
                              wo2_scr[h, :, half * 512:(half + 1) * 512], after=war)
                        for h in range(H)]

            def rms_to_fm(src_tm, dst_fm, scol, eps, dep0):
                evs = []
                hprev = None
                prev_sq = None
                for j in range(4):
                    sq = P.add("scalar", lambda e, j=j: e.activation(
                        fsq_scr[:], src_tm[:, j, :], AF.Square, bias=0.0, scale=1.0,
                        accum_out=fstat[:, scol + j:scol + j + 1]), after=dep0 + [prev_sq])
                    prev_sq = sq
                    sd = P.add("scalar", lambda e, j=j: e.activation(
                        fstat[:, scol + 8 + j:scol + 9 + j], fstat[:, scol + j:scol + j + 1],
                        AF.Sqrt, bias=eps, scale=1.0 / DM), after=[sq])
                    iv = P.add("vector", lambda e, j=j: e.reciprocal(
                        fstat[:, scol + 16 + j:scol + 17 + j], fstat[:, scol + 8 + j:scol + 9 + j]),
                        after=[sd])
                    hh = P.add("vector", lambda e, j=j: e.tensor_scalar(
                        h_scr[:], src_tm[:, j, :], fstat[:, scol + 16 + j:scol + 17 + j], None,
                        ALU.mult, ALU.bypass), after=[iv, hprev])
                    tts_ = []
                    for k in range(KD):
                        bnk = pb[2 + (k // 4)]
                        tts_.append(P.add("tensor", lambda e, j=j, k=k, b=bnk: e.transpose(
                            b[:, (k % 4) * 128:(k % 4 + 1) * 128], h_scr[:, k * 128:(k + 1) * 128],
                            eye32[:]), after=[hh] + ([evs[-2], evs[-1]] if evs else [])))
                    e1 = P.add("scalar", lambda e, j=j: e.activation(
                        dst_fm[:, 0:4, j * 128:(j + 1) * 128], pb[2][:, 0:512].rearrange(
                            "p (a b) -> p a b", a=4), AF.Copy, bias=0.0, scale=1.0),
                        after=tts_[0:4])
                    e2 = P.add("scalar", lambda e, j=j: e.activation(
                        dst_fm[:, 4:8, j * 128:(j + 1) * 128], pb[3][:, 0:512].rearrange(
                            "p (a b) -> p a b", a=4), AF.Copy, bias=0.0, scale=1.0),
                        after=tts_[4:8])
                    evs += [e1, e2]
                    hprev = e2
                return evs

            NCF = T // TCF
            for s in range(NSEQ):
                for ci in range(NCF):
                    P.barrier()
                    t0 = ci * TCF
                    prev_wo_free = None
                    prev_r_user = None
                    wql = [P.dma(webuf[k // 4][:, (k % 4) * 2:(k % 4) * 2 + 2, :]
                                 .rearrange("p a b -> p (a b)"), wq_scr[k])
                           for k in range(KD)]
                    wolA = wo_load(0, [])
                    ld_xn = P.dma(fxn_tm[:], xn_scr[s, t0:t0 + TCF, :].rearrange("(a p) d -> p a d", p=128))
                    ld_s1 = P.dma(out_tm[:], s1_scr[s, t0:t0 + TCF, :].rearrange("(a p) d -> p a d", p=128))

                    h1_evs = rms_to_fm(fxn_tm, h1_fm, 0, consts_sb[:, 1:2], [ld_xn])

                    mm = None
                    for k in range(KD):
                        mm = P.add("tensor", lambda e, k=k: e.matmul(
                            pb[0][0:64, 0:TCF], wdown_sb[:, k * LR:(k + 1) * LR], h1_fm[:, k, :],
                            start=(k == 0), stop=(k == KD - 1)),
                            after=[h1_evs[-1], prev_wo_free])
                    ev_r = P.add("scalar", lambda e: e.activation(
                        r1_bf[0:64, :], pb[0][0:64, 0:TCF], AF.Identity, bias=bdown_sb[:], scale=1.0),
                        after=[mm, prev_r_user])
                    ones_r = P.add("vector", lambda e: e.memset(r1_bf[64:65, :], 1.0),
                                   after=[prev_r_user])

                    ev2prev = [ev_r, prev_wo_free]
                    q2_ev = []
                    for h in range(H):
                        bank = pb[h % 2]
                        mm = None
                        for b in range(2):
                            for k in range(KD):
                                mm = P.add("tensor", lambda e, h=h, k=k, b=b, bk=bank: e.matmul(
                                    bk[64 * b:64 * b + 64, 0:256],
                                    wq_slice(k, h),
                                    h1_fm[:, k, :].rearrange("p (t two) -> p two t", two=2)[:, b, :],
                                    start=(k == 0), stop=(k == KD - 1),
                                    tile_position=(0, 64 * b)),
                                    after=[h1_evs[-1], ev2prev[h % 2]] + (wql if h == 0 else []))
                        ev = P.add("scalar", lambda e, h=h, bk=bank: e.activation(
                            q2ARR[:, h, :], bk[:, 0:256], AF.Identity, bias=bq2_sb[:, h:h + 1],
                            scale=1.0), after=[mm])
                        ev2prev[h % 2] = ev
                        q2_ev.append(ev)
                    k2_ev = []
                    for g in range(H):
                        bank = pb[g % 2]
                        mm = None
                        for b in range(2):
                            mm = P.add("tensor", lambda e, g=g, b=b, bk=bank: e.matmul(
                                bk[64 * b:64 * b + 64, 0:256],
                                wupk_sb[:, g * HD:(g + 1) * HD],
                                r1_bf[0:64, :].rearrange("p (t two) -> p two t", two=2)[:, b, :],
                                start=True, stop=True,
                                tile_position=(0, 64 * b)),
                                after=[ev_r, ev2prev[g % 2]])
                        ev = P.add("scalar", lambda e, g=g, bk=bank: e.activation(
                            k2ARR[:, g, :], bk[:, 0:256], AF.Identity, bias=bupk2_sb[:, g:g + 1],
                            scale=1.0), after=[mm])
                        ev2prev[g % 2] = ev
                        k2_ev.append(ev)

                    slab_evs = []
                    prev_vg_user = None
                    prev_qk_free = [k2_ev[-2], k2_ev[-1]]
                    prev_av_free = [None, None]
                    prev_attn_user = None
                    prev_attT_user = None
                    prev_b1_user = None
                    for bi in range(4):
                        vg_evs = []
                        for hq in range(16):
                            bank = pb[hq % 2]
                            mms = []
                            for hh in range(4):
                                hd = hq * 4 + hh
                                mms.append(P.add("tensor", lambda e, hd=hd, hh=hh, bk=bank, bi=bi: e.matmul(
                                    bk[0:16, hh * 128:(hh + 1) * 128],
                                    wupv3_sb[:, hd * H:(hd + 1) * H],
                                    r1_bf[:, bi * 128:(bi + 1) * 128],
                                    start=True, stop=True),
                                    after=[ev_r, ones_r, ev2prev[hq % 2] if hq < 2 else None,
                                           vg_evs[hq - 2] if hq >= 2 else None,
                                           prev_vg_user if hq < 2 else None]))
                            vg_evs.append(P.add("vector", lambda e, hq=hq, bk=bank: e.tensor_copy(
                                vg_sb[:, hq * 4:hq * 4 + 4, :].rearrange("p a b -> p (a b)"),
                                bk[0:16, 0:512]), after=mms))
                        qbank = pb[2 + (bi % 2)]
                        cpq = P.add("vector", lambda e, bi=bi: e.tensor_copy(
                            qb1_scr[:], q2ARR[64:128, 0:H, bi * 64:(bi + 1) * 64]),
                            after=[q2_ev[-1], prev_b1_user])
                        cpk = P.add("vector", lambda e, bi=bi: e.tensor_copy(
                            kb1_scr[:], k2ARR[64:128, 0:H, bi * 64:(bi + 1) * 64]),
                            after=[k2_ev[-1], prev_b1_user])
                        qk_last = None
                        for q in range(64):
                            pidx = bi * 64 + q
                            j, slot = q % 4, q // 4
                            for b in range(2):
                                if b == 0:
                                    sq_ = q2ARR[0:64, 0:H, pidx:pidx + 1].rearrange("p a b -> p (a b)")
                                    sk_ = k2ARR[0:64, 0:H, pidx:pidx + 1].rearrange("p a b -> p (a b)")
                                else:
                                    sq_ = qb1_scr[:, 0:H, q:q + 1].rearrange("p a b -> p (a b)")
                                    sk_ = kb1_scr[:, 0:H, q:q + 1].rearrange("p a b -> p (a b)")
                                qk_last = P.add("tensor", lambda e, sq_=sq_, sk_=sk_, j=j, slot=slot, b=b, bk=qbank: e.matmul(
                                    bk[32 * j:32 * j + 16, slot * 32 + b * 16:slot * 32 + b * 16 + 16],
                                    sq_, sk_,
                                    start=True, stop=True,
                                    tile_position=(0, 32 * j), skip_group_check=True),
                                    after=[prev_qk_free[bi % 2], q2_ev[-1], k2_ev[-1], cpq, cpk] if q == 0 and b == 0 else [])
                        prev_b1_user = qk_last
                        nm = P.add("vector", lambda e, bk=qbank: e.tensor_reduce(
                            fstat[:, 0:32], bk[:, 0:512].rearrange("p (a b) -> p a b", a=32),
                            AX.X, ALU.max, negate=True), after=[qk_last])
                        sub = P.add("vector", lambda e, bk=qbank: e.tensor_tensor(
                            h_scr[:, 0:512].rearrange("p (a b) -> p a b", a=32),
                            bk[:, 0:512].rearrange("p (a b) -> p a b", a=32),
                            fstat[:, 0:32].unsqueeze(2).broadcast_to([128, 32, 16]), ALU.add),
                            after=[nm, h1_evs[-1]])
                        exp_ = P.add("scalar", lambda e: e.activation(
                            fsq_scr[:, 0:512], h_scr[:, 0:512], AF.Exp, bias=0.0, scale=1.0),
                            after=[sub])
                        smr = P.add("vector", lambda e: e.tensor_reduce(
                            fstat[:, 32:64], fsq_scr[:, 0:512].rearrange("p (a b) -> p a b", a=32),
                            AX.X, ALU.add, negate=False), after=[exp_])
                        rc = P.add("vector", lambda e: e.reciprocal(
                            fstat[:, 0:32], fstat[:, 32:64]), after=[smr])
                        nrm = P.add("vector", lambda e: e.tensor_tensor(
                            attn_b[:].rearrange("p (a b) -> p a b", a=32),
                            fsq_scr[:, 0:512].rearrange("p (a b) -> p a b", a=32),
                            fstat[:, 0:32].unsqueeze(2).broadcast_to([128, 32, 16]), ALU.mult),
                            after=[rc, prev_attn_user])
                        at_evs = []
                        tts_ = []
                        for gi in range(32):
                            slot, b = gi // 2, gi % 2
                            tts_.append(P.add("tensor", lambda e, gi=gi, slot=slot, b=b: e.transpose(
                                pbBF[0:16, (gi % 8) * 128:(gi % 8 + 1) * 128],
                                attn_b[:, slot * 32 + b * 16:slot * 32 + b * 16 + 16], eyebf[:]),
                                after=[nrm, at_evs[-1] if gi % 8 == 0 and at_evs else None,
                                       prev_attT_user if gi == 0 else None]))
                            if gi % 8 == 7:
                                at_evs.append(P.add("vector", lambda e, gi=gi: e.tensor_copy(
                                    attT_sb[:, (gi // 8) * 1024:(gi // 8 + 1) * 1024],
                                    pbBF[0:16, 0:1024]), after=tts_[-8:]))
                        for a in range(4):
                            abank = pb[4 + (a % 2)]
                            av_last = None
                            for ti in range(32):
                                tt = a * 32 + ti
                                t = bi * 128 + tt
                                qq = t // 2
                                j, slot, b = qq % 4, (qq // 4) % 16, t % 2
                                gi = slot * 2 + b
                                av_last = P.add("tensor", lambda e, tt=tt, ti=ti, j=j, gi=gi, bk=abank: e.matmul(
                                    bk[0:64, ti * 16:(ti + 1) * 16],
                                    vg_sb[:, 0:64, tt:tt + 1].rearrange("p a b -> p (a b)"),
                                    attT_sb[:, gi * 128 + 32 * j:gi * 128 + 32 * j + 16],
                                    start=True, stop=True),
                                    after=[at_evs[-1], vg_evs[-1], prev_av_free[a % 2]] if ti == 0 else [])
                            sev = P.add("vector", lambda e, bi=bi, a=a, bk=abank: e.tensor_copy(
                                out2slabs[:, :, bi * 128 + a * 32:bi * 128 + (a + 1) * 32].transpose([0, 2, 1]),
                                bk[0:64, 0:512].rearrange("p (a b) -> p a b", a=32)),
                                after=[av_last])
                            prev_av_free[a % 2] = sev
                            slab_evs.append(sev)
                        prev_qk_free[bi % 2] = sub
                        prev_attn_user = tts_[-1]
                        prev_attT_user = av_last
                        prev_vg_user = av_last
                        if DEBUG_TAPS and s == 0 and ci == 0 and bi == 0:
                            prev_attn_user = P.dma(dbg_h["attn"], attn_b[:], after=[tts_[-1]])
                            prev_attT_user = P.dma(dbg_h["attT"], attT_sb[:], after=[av_last])
                            prev_vg_user = P.dma(dbg_h["vg"], vg_sb[:], after=[av_last])
                    prev_r_user = av_last

                    if DEBUG_TAPS and s == 0 and ci == 0:
                        P.dma(dbg_h["o2sl"], out2slabs[:], after=slab_evs)
                        P.dma(dbg_h["q2"], q2ARR[:], after=[q2_ev[-1]])
                        P.dma(dbg_h["k2"], k2ARR[:], after=[k2_ev[-1]])
                    x1_ops = []
                    ap_free = [vg_evs[-2], vg_evs[-1]]
                    prev_tt2 = [None, None]
                    wolB = None
                    for mo in range(KD):
                        bank = pb[mo % 2]
                        mm = None
                        for h in range(H):
                            mm = P.add("tensor", lambda e, mo=mo, h=h, b=bank: e.matmul(
                                b[:, 0:TCF],
                                wo2_sb[:, h * 512 + (mo % 4) * 128:h * 512 + (mo % 4) * 128 + 128],
                                out2slabs[:, h, :], start=(h == 0), stop=(h == H - 1)),
                                after=[slab_evs[-1], ap_free[mo % 2]]
                                + (wolA if mo == 0 else [])
                                + (wolB if mo == 4 else []))
                        if mo == 3:
                            wolB = wo_load(1, [mm])
                        ev = P.add("scalar", lambda e, mo=mo, b=bank: e.activation(
                            aproj_sb[mo % 2][:], b[:, 0:TCF], AF.Identity, bias=bo_sb[:, mo:mo + 1],
                            scale=1.0), after=[mm])
                        ap_free[mo % 2] = ev
                        gbank = pbG if mo % 2 == 0 else pb[2]
                        tts_ = []
                        for j in range(4):
                            tts_.append(P.add("tensor", lambda e, mo=mo, j=j, gb=gbank: e.transpose(
                                gb[:, j * 128:(j + 1) * 128],
                                aproj_sb[mo % 2][:, j * 128:(j + 1) * 128], eye32[:]),
                                after=[ev, prev_tt2[mo % 2] if j == 0 else None]))
                        ad = P.add("vector", lambda e, mo=mo, gb=gbank: e.tensor_tensor(
                            x1_tm[:, 0:4, mo * 128:(mo + 1) * 128],
                            gb[:, 0:512].rearrange("p (a b) -> p a b", a=4),
                            fxn_tm[:, 0:4, mo * 128:(mo + 1) * 128], ALU.add), after=tts_)
                        prev_tt2[mo % 2] = ad
                        ap_free[mo % 2] = ad
                        x1_ops.append(ad)

                    if DEBUG_TAPS and s == 0 and ci == 0:
                        P.dma(dbg_h["x1"], x1_tm[:], after=[x1_ops[-1]])
                    h2_evs = rms_to_fm(x1_tm, h2_fm, 24, consts_sb[:, 1:2], [x1_ops[-1]])

                    mm = None
                    for k in range(KD):
                        mm = P.add("tensor", lambda e, k=k: e.matmul(
                            pb[0][0:E, 0:TCF], wg_sb[:, k * E:(k + 1) * E], h2_fm[:, k, :],
                            start=(k == 0), stop=(k == KD - 1)),
                            after=[h2_evs[-1], ap_free[0]])
                    gev = P.add("scalar", lambda e: e.activation(
                        h_scr[0:E, 0:TCF], pb[0][0:E, 0:TCF], AF.Identity, bias=bg_sb[:], scale=1.0),
                        after=[mm, x1_ops[-1]])
                    gt = []
                    for j in range(4):
                        gt.append(P.add("tensor", lambda e, j=j: e.transpose(
                            pbG[:, j * E:(j + 1) * E], h_scr[0:E, j * 128:(j + 1) * 128],
                            eye32[0:E, 0:E]),
                            after=[gev] + ([prev_tt2[0]] if j == 0 else [])))
                    gcp = P.add("vector", lambda e: e.tensor_copy(g_tm[:], pbG[:, 0:4 * E]), after=gt)
                    gsm = []
                    for j in range(4):
                        nm = P.add("vector", lambda e, j=j: e.tensor_reduce(
                            fstat[:, 40 + j:41 + j], g_tm[:, j * E:(j + 1) * E], AX.X, ALU.max,
                            negate=True), after=[gcp])
                        sb_ = P.add("vector", lambda e, j=j: e.tensor_tensor(
                            gn_tm[:, j * E:(j + 1) * E], g_tm[:, j * E:(j + 1) * E],
                            fstat[:, 40 + j:41 + j].broadcast_to([128, E]), ALU.add), after=[nm])
                        ex_ = P.add("scalar", lambda e, j=j: e.activation(
                            g_tm[:, j * E:(j + 1) * E], gn_tm[:, j * E:(j + 1) * E], AF.Exp,
                            bias=0.0, scale=1.0), after=[sb_])
                        sm_ = P.add("vector", lambda e, j=j: e.tensor_reduce(
                            fstat[:, 44 + j:45 + j], g_tm[:, j * E:(j + 1) * E], AX.X, ALU.add,
                            negate=False), after=[ex_])
                        rc_ = P.add("vector", lambda e, j=j: e.reciprocal(
                            fstat[:, 40 + j:41 + j], fstat[:, 44 + j:45 + j]), after=[sm_])
                        mu_ = P.add("vector", lambda e, j=j: e.tensor_scalar(
                            gn_tm[:, j * E:(j + 1) * E], g_tm[:, j * E:(j + 1) * E],
                            fstat[:, 40 + j:41 + j], None, ALU.mult, ALU.bypass), after=[rc_])
                        gsm.append(mu_)
                    gbt = []
                    for j in range(4):
                        gbt.append(P.add("tensor", lambda e, j=j: e.transpose(
                            pbG[0:E, j * 128:(j + 1) * 128], gn_tm[:, j * E:(j + 1) * E], eye32[:]),
                            after=[gsm[j]] + ([gcp] if j == 0 else [])))
                    gnf = P.add("vector", lambda e: e.tensor_copy(gn_bf[:], pbG[0:E, 0:TCF]), after=gbt)

                    wel = {}
                    moe_ads = []
                    for half in range(2):
                        mm_last = None
                        h2e_done = [None, None]
                        mm_of_e = [None] * E
                        for ei in range(E):
                            war = [mm_of_e[ei - 2]] if ei >= 2 else ([h2_evs[-1]] if half == 0 else [moe_ads[-1]])
                            wel[(half, ei)] = P.dma(webuf[ei % 2][:],
                                                    we_scr[ei, half], after=war)
                            gb = P.dma(gbc[ei % 2][:], bcast_view(gn_bf[ei:ei + 1, :], TCF),
                                       after=[gnf] + ([mm_of_e[ei - 2]] if ei >= 2 else []))
                            h2e = P.add("vector", lambda e, ei=ei: e.tensor_tensor(
                                h2e_fm[ei % 2][:], h2_fm[:],
                                gbc[ei % 2][:].unsqueeze(1).broadcast_to([128, KD, TCF]),
                                ALU.mult), after=[gb, h2_evs[-1], mm_of_e[ei - 2] if ei >= 2 else None])
                            mm = None
                            for mo4 in range(4):
                                for k in range(KD):
                                    mm = P.add("tensor", lambda e, ei=ei, mo4=mo4, k=k, half=half: e.matmul(
                                        pb[2 + mo4][:, 0:TCF],
                                        webuf[ei % 2][:, k, mo4 * 128:(mo4 + 1) * 128],
                                        h2e_fm[ei % 2][:, k, :],
                                        start=(ei == 0 and k == 0), stop=(ei == E - 1 and k == KD - 1)),
                                        after=[wel[(half, ei)], h2e,
                                               (h2_evs[-1] if half == 0 else moe_ads[-1]) if ei == 0 and mo4 == 0 and k == 0 else None])
                            mm_of_e[ei] = mm
                            mm_last = mm
                        for mo4 in range(4):
                            mo = half * 4 + mo4
                            ev = P.add("scalar", lambda e, mo4=mo4: e.activation(
                                moe_sb[mo4 % 2][:], pb[2 + mo4][:, 0:TCF], AF.Copy, bias=0.0,
                                scale=1.0), after=[mm_last, moe_ads[-2] if len(moe_ads) >= 2 else None])
                            tts_ = []
                            for j in range(4):
                                tts_.append(P.add("tensor", lambda e, j=j, mo4=mo4: e.transpose(
                                    pbG[:, j * 128:(j + 1) * 128],
                                    moe_sb[mo4 % 2][:, j * 128:(j + 1) * 128], eye32[:]),
                                    after=[ev, moe_ads[-1] if moe_ads else gnf]))
                            if half == 0 and mo4 == 0:
                                sx1 = []
                                for j in range(4):
                                    sx1.append(P.add("vector", lambda e, j=j: e.tensor_tensor(
                                        fxn_tm[:, j, :], out_tm[:, j, :], x1_tm[:, j, :], ALU.add),
                                        after=[ld_s1, x1_ops[-1], h2_evs[-1]]))
                                sx1_done = sx1[-1]
                            ad = P.add("vector", lambda e, mo=mo: e.tensor_tensor(
                                out_tm[:, 0:4, mo * 128:(mo + 1) * 128],
                                pbG[:, 0:512].rearrange("p (a b) -> p a b", a=4),
                                fxn_tm[:, 0:4, mo * 128:(mo + 1) * 128], ALU.add),
                                after=tts_ + [sx1_done])
                            moe_ads.append(ad)
                    st_out = P.dma(out_h[s, t0:t0 + TCF, :].rearrange("(a p) d -> p a d", p=128),
                                   out_tm[:], after=moe_ads)
                    out_stores.append(st_out)

        P.barrier()
        P.add("sync", lambda e: e.nop(), after=out_stores[-1:])

        blk = top.enter_context(nc.Block())
        with nc.allow_non_contiguous_dma(reason="small one-time vector layout loads"):
            P.emit(blk, sems, dsems)
    return nc


_NC_CACHE = None


def _get_nc():
    global _NC_CACHE
    if _NC_CACHE is None:
        _NC_CACHE = build_nc()
    return _NC_CACHE


def host_common(L0_attn_ln_g, L0_attn_ln_b, L0_attn_W_in, L0_attn_conv_w,
                L0_attn_conv_b, L0_attn_W_xp, L0_attn_W_dt, L0_attn_b_dt, L0_attn_A_log,
                L0_attn_D, L0_attn_W_out, L0_ffn_gamma1, L0_ffn_Wq, L0_ffn_bq,
                L0_ffn_Wdown, L0_ffn_bdown, L0_ffn_Wup, L0_ffn_bup, L0_ffn_Wo,
                L0_ffn_bo, L0_ffn_gamma2, L0_ffn_Wg, L0_ffn_bg, L0_ffn_We):
    f32 = lambda a: np.ascontiguousarray(np.asarray(a, dtype=np.float32))
    perm_k = np.array([h * 128 + hd for h in range(H) for hd in range(HD)])
    perm_v = perm_k + HD
    Wup = f32(L0_ffn_Wup)
    bup = f32(L0_ffn_bup)
    common = {
        "winT": f32(np.asarray(L0_attn_W_in).T),
        "wxpT": f32(np.asarray(L0_attn_W_xp).T),
        "wdtT": f32(np.asarray(L0_attn_W_dt).T),
        "woutT": f32(np.asarray(L0_attn_W_out).T),
        "wq": f32(L0_ffn_Wq),
        "wdown": f32(L0_ffn_Wdown),
        "wupk": np.ascontiguousarray(Wup[:, perm_k]),
        "wupv": np.ascontiguousarray(Wup[:, perm_v]),
        "wo": f32(L0_ffn_Wo),
        "wg": f32(L0_ffn_Wg),
        "we": f32(L0_ffn_We),
        "alog": f32(L0_attn_A_log),
        "dvec": f32(L0_attn_D),
        "convw": f32(L0_attn_conv_w),
        "convb": f32(L0_attn_conv_b),
        "bdt": f32(L0_attn_b_dt),
        "lng": f32(L0_attn_ln_g),
        "lnb": f32(L0_attn_ln_b),
        "g1": f32(L0_ffn_gamma1),
        "g2": f32(L0_ffn_gamma2),
        "bq": f32(L0_ffn_bq),
        "bdown": f32(L0_ffn_bdown),
        "bupk": np.ascontiguousarray(bup[perm_k]),
        "bupv": np.ascontiguousarray(bup[perm_v]),
        "bo": f32(L0_ffn_bo),
        "bg": f32(L0_ffn_bg),
        "eye32": np.eye(128, dtype=np.float32),
        "eyebf": np.eye(128, dtype=ml_dtypes.bfloat16),
    }
    return common


def kernel(L0_attn_x, **kw):
    x = np.ascontiguousarray(np.asarray(L0_attn_x, dtype=np.float32))
    common = host_common(**kw)
    in_maps = []
    for c in range(8):
        m = dict(common)
        m["x"] = np.ascontiguousarray(x[2 * c:2 * c + 2])
        in_maps.append(m)
    nc = _get_nc()
    res = run_bass_kernel_spmd(nc, in_maps, core_ids=list(range(8)))
    return np.concatenate([res.results[c]["out"] for c in range(8)], axis=0)


# version 15
# speedup vs baseline: 1.5607x; 1.0103x over previous
import sys
sys.path.insert(0, '/opt/trn_rl_repo')
from contextlib import ExitStack
import numpy as np
import ml_dtypes
import concourse.bass as bass
import concourse.mybir as mybir
from concourse.bass_utils import run_bass_kernel_spmd

F32 = mybir.dt.float32
BF16 = mybir.dt.bfloat16
AF = mybir.ActivationFunctionType
ALU = mybir.AluOpType
AX = mybir.AxisListType

DM, DI, DS, DC, DTR = 1024, 2048, 16, 4, 64
T, NSEQ = 2048, 2
H, HD, LR, E = 16, 64, 64, 8
TCM, TCF = 256, 512
KD = DM // 128
ID = DI // 128

DEBUG_TAPS = False
TRUNC = None
STRICT_WAITS = False
ELIDE_ENGS = ()


class Prog:
    ENGS = ("sync", "tensor", "vector", "scalar", "gpsimd")
    NDMA = 32

    def __init__(self):
        self.ops = []
        self._barrier = {}
        self._dma_rr = 0
        self._dma_cum = [0] * self.NDMA

    def add(self, eng, fn, after=(), is_dma=False):
        deps = [d for d in after if d is not None]
        deps += [b for b in self._barrier.values() if b is not None]
        o = {"eng": eng, "fn": fn, "after": deps, "id": len(self.ops),
             "is_dma": is_dma, "inc": False, "cum": None, "dsem": None, "dcum": None}
        if is_dma:
            s = self._dma_rr % self.NDMA
            self._dma_rr += 1
            self._dma_cum[s] += 16
            o["dsem"] = s
            o["dcum"] = self._dma_cum[s]
        for d in deps:
            assert d["id"] < o["id"]
        self.ops.append(o)
        return o

    def dma(self, out, in_, after=()):
        return self.add("sync", lambda e, o=out, i=in_: e.dma_start(out=o, in_=i),
                        after=after, is_dma=True)

    def barrier(self):
        last = {}
        for o in self.ops:
            last[o["eng"]] = o
        self._barrier = last

    def emit(self, block, sems, dsems):
        if TRUNC is not None:
            self.ops = self.ops[:TRUNC]
            keep = set(id(o) for o in self.ops)
            for o in self.ops:
                o["after"] = [d for d in o["after"] if id(d) in keep]
        for o in self.ops:
            for d in o["after"]:
                if (not STRICT_WAITS and not d["is_dma"] and d["eng"] == o["eng"]
                        and o["eng"] in ELIDE_ENGS):
                    continue
                d["inc"] = True
        cum = {e: 0 for e in self.ENGS}
        for o in self.ops:
            if o["inc"] and not o["is_dma"]:
                cum[o["eng"]] += 1
                o["cum"] = cum[o["eng"]]

        def stream(eng_name, eng):
            waited = {e: 0 for e in self.ENGS}
            dwaited = [0] * self.NDMA
            for o in self.ops:
                if o["eng"] != eng_name:
                    continue
                for d in o["after"]:
                    if d["is_dma"]:
                        if d["dcum"] > dwaited[d["dsem"]]:
                            eng.wait_ge(dsems[d["dsem"]], d["dcum"])
                            dwaited[d["dsem"]] = d["dcum"]
                    else:
                        if (not STRICT_WAITS and d["eng"] == eng_name
                                and eng_name in ELIDE_ENGS):
                            continue
                        if d["cum"] is not None and d["cum"] > waited[d["eng"]]:
                            eng.wait_ge(sems[d["eng"]], d["cum"])
                            waited[d["eng"]] = d["cum"]
                if o["is_dma"]:
                    if o["dcum"] > 16 and o["dcum"] - 16 > dwaited[o["dsem"]]:
                        eng.wait_ge(dsems[o["dsem"]], o["dcum"] - 16)
                        dwaited[o["dsem"]] = o["dcum"] - 16
                    instr = o["fn"](eng)
                    instr.then_inc(dsems[o["dsem"]], 16)
                else:
                    instr = o["fn"](eng)
                    if o["inc"]:
                        instr.then_inc(sems[eng_name], 1)

        @block.sync
        def _(e):
            stream("sync", e)

        @block.tensor
        def _(e):
            stream("tensor", e)

        @block.vector
        def _(e):
            stream("vector", e)

        @block.scalar
        def _(e):
            stream("scalar", e)

        @block.gpsimd
        def _(e):
            stream("gpsimd", e)


def bcast_view(row_ap, width):
    return row_ap.unsqueeze(1).broadcast_to([1, 128, width])


def build_nc():
    nc = bass.Bass("TRN2", target_bir_lowering=False, debug=False,
                   detect_race_conditions=STRICT_WAITS)
    P = Prog()

    din = lambda n, sh, dt=F32: nc.dram_tensor(n, sh, dt, kind="ExternalInput").ap()
    x_h = din("x", [NSEQ, T, DM])
    winT_h = din("winT", [DM, 2 * DI])
    wxpT_h = din("wxpT", [DI, 96])
    wdtT_h = din("wdtT", [DTR, DI])
    woutT_h = din("woutT", [DI, DM])
    wq_h = din("wq", [DM, DM])
    wdown_h = din("wdown", [DM, LR])
    wupk_h = din("wupk", [LR, DM])
    wupv_h = din("wupv", [LR, DM])
    wo_h = din("wo", [DM, DM])
    wg_h = din("wg", [DM, E])
    we_h = din("we", [E, DM, DM])
    alog_h = din("alog", [DI, DS])
    dvec_h = din("dvec", [DI])
    convw_h = din("convw", [DI, DC])
    convb_h = din("convb", [DI])
    bdt_h = din("bdt", [DI])
    lng_h = din("lng", [DM])
    lnb_h = din("lnb", [DM])
    g1_h = din("g1", [DM])
    g2_h = din("g2", [DM])
    bq_h = din("bq", [DM])
    bdown_h = din("bdown", [LR])
    bupk_h = din("bupk", [DM])
    bupv_h = din("bupv", [DM])
    bo_h = din("bo", [DM])
    bg_h = din("bg", [E])
    eye32_h = din("eye32", [128, 128])
    eyebf_h = din("eyebf", [128, 128], BF16)

    out_h = nc.dram_tensor("out", [NSEQ, T, DM], F32, kind="ExternalOutput").ap()
    skind = "ExternalOutput" if DEBUG_TAPS else "Internal"
    xn_scr = nc.dram_tensor("xn_scr", [NSEQ, T, DM], F32, kind=skind).ap()
    s1_scr = nc.dram_tensor("s1_scr", [NSEQ, T, DM], F32, kind=skind).ap()
    dbg_h = {}
    if DEBUG_TAPS:
        dbg_h["o2sl"] = nc.dram_tensor("dbg_o2sl", [64, H, TCF], BF16, kind="ExternalOutput").ap()
        dbg_h["x1"] = nc.dram_tensor("dbg_x1", [128, 4, DM], F32, kind="ExternalOutput").ap()
        dbg_h["q2"] = nc.dram_tensor("dbg_q2", [128, H, 256], BF16, kind="ExternalOutput").ap()
        dbg_h["k2"] = nc.dram_tensor("dbg_k2", [128, H, 256], BF16, kind="ExternalOutput").ap()
        dbg_h["attn"] = nc.dram_tensor("dbg_attn", [128, 512], BF16, kind="ExternalOutput").ap()
        dbg_h["vg"] = nc.dram_tensor("dbg_vg", [16, 64, 128], BF16, kind="ExternalOutput").ap()
        dbg_h["attT"] = nc.dram_tensor("dbg_attT", [16, 4096], BF16, kind="ExternalOutput").ap()
    winT_scr = nc.dram_tensor("winT_b", [KD, 128, 2 * DI], BF16, kind="Internal").ap()
    wq_scr = nc.dram_tensor("wq_b", [KD, 128, DM], BF16, kind="Internal").ap()
    wo2_scr = nc.dram_tensor("wo2_b", [H, 64, DM], BF16, kind="Internal").ap()
    we_scr = nc.dram_tensor("we_b", [E, 2, 128, KD, 512], BF16, kind="Internal").ap()

    out_stores = []

    with ExitStack() as top:
        sems = {e: top.enter_context(nc.semaphore("sem_" + e)) for e in Prog.ENGS}
        dsems = [top.enter_context(nc.semaphore(f"dma{i}")) for i in range(Prog.NDMA)]
        sb = lambda n, sh, dt=F32: top.enter_context(nc.sbuf_tensor(n, sh, dt))

        eye32 = sb("s_eye32", [128, 128])
        eyebf = sb("s_eyebf", [128, 128], BF16)
        consts_sb = sb("consts_sb", [128, 4])
        cxz_sb = sb("cxz_sb", [128, 32])
        bq2_sb = sb("bq2_sb", [128, H])
        bdown_sb = sb("bdown_sb", [64, 1])
        bupk2_sb = sb("bupk2_sb", [128, H])
        wupv3_sb = sb("wupv3_sb", [65, 64 * H], BF16)
        bo_sb = sb("bo_sb", [128, KD])
        bg_sb = sb("bg_sb", [E, 1])
        wdown_sb = sb("wdown_sb", [128, KD * LR], BF16)
        wupk_sb = sb("wupk_sb", [64, DM], BF16)
        wg_sb = sb("wg_sb", [128, KD * E], BF16)

        pb = [top.enter_context(nc.psum_tensor(f"pb{i}", [128, 512], F32)) for i in range(6)]
        pbBF = top.enter_context(nc.psum_tensor("pbBF", [128, 1024], BF16))
        pbG = top.enter_context(nc.psum_tensor("pbG", [128, 512], F32))

        mres = ExitStack()
        msb0 = lambda n, sh, dt=F32: mres.enter_context(nc.sbuf_tensor(n, sh, dt))
        A_sb = msb0("A_sb", [128, ID * DS])
        D_sb = msb0("D_sb", [128, ID])
        convw_sb = msb0("convw_sb", [128, ID * DC])
        convb_sb = msb0("convb_sb", [128, ID])
        bdt_sb = msb0("bdt_sb", [128, ID])
        wxpT_sb = msb0("wxpT_sb", [128, ID * 96], BF16)
        wdtT_sb = msb0("wdtT_sb", [64, DI], BF16)
        woutT_sb = msb0("woutT_sb", [128, ID * DM], BF16)

        with ExitStack() as prep:
            psb = lambda n, sh, dt=F32: prep.enter_context(nc.sbuf_tensor(n, sh, dt))
            slab = psb("p_slab", [128, DM])
            slab_bf = psb("p_slab_bf", [128, DM], BF16)
            vcol = psb("p_vcol", [128, 16])
            lnb_bf = psb("p_lnb_bf", [128, KD], BF16)
            g18_sb = psb("p_g18", [128, KD])
            g1c_sb = psb("p_g1c", [128, KD])
            g2c_sb = psb("p_g2c", [128, KD])
            lng_c = psb("p_lng", [128, KD])

            P.add("vector", lambda e: e.memset(consts_sb[:, 0:1], 1e-5))
            P.add("vector", lambda e: e.memset(consts_sb[:, 1:2], 1e-6))
            P.add("vector", lambda e: e.memset(consts_sb[:, 2:3], 1.0))
            P.dma(eye32[:], eye32_h)
            P.dma(eyebf[:], eyebf_h)
            ld_a = P.dma(A_sb[:], alog_h.rearrange("(i p) n -> p i n", p=128))
            e_a = P.add("scalar", lambda e: e.activation(A_sb[:], A_sb[:], AF.Exp, bias=0.0, scale=1.0), after=[ld_a])
            P.add("vector", lambda e: e.tensor_scalar(A_sb[:], A_sb[:], -1.0, None, ALU.mult, ALU.bypass), after=[e_a])
            P.dma(D_sb[:], dvec_h.rearrange("(i p) -> p i", p=128))
            P.dma(convw_sb[:], convw_h.rearrange("(i p) c -> p i c", p=128))
            P.dma(convb_sb[:], convb_h.rearrange("(i p) -> p i", p=128))
            P.dma(bdt_sb[:], bdt_h.rearrange("(i p) -> p i", p=128))
            ld_g1 = P.dma(g1c_sb[:], g1_h.rearrange("(k p) -> p k", p=128))
            ld_g2 = P.dma(g2c_sb[:], g2_h.rearrange("(k p) -> p k", p=128))
            ld_lng = P.dma(lng_c[:], lng_h.rearrange("(k p) -> p k", p=128))
            ld_lnb = P.dma(vcol[:, 0:KD], lnb_h.rearrange("(k p) -> p k", p=128))
            cv_lnb = P.add("vector", lambda e: e.tensor_copy(lnb_bf[:], vcol[:, 0:KD]), after=[ld_lnb])
            lb1 = P.dma(bq2_sb[0:64, :], bq_h.rearrange("(h d) -> d h", d=HD))
            lb2 = P.dma(bq2_sb[64:128, :], bq_h.rearrange("(h d) -> d h", d=HD))
            P.add("vector", lambda e: e.tensor_scalar(bq2_sb[:], bq2_sb[:], 0.125, None, ALU.mult, ALU.bypass), after=[lb1, lb2])
            P.dma(bdown_sb[:], bdown_h.rearrange("(a p) -> p a", p=64))
            P.dma(bupk2_sb[0:64, :], bupk_h.rearrange("(h d) -> d h", d=HD))
            P.dma(bupk2_sb[64:128, :], bupk_h.rearrange("(h d) -> d h", d=HD))
            P.dma(bo_sb[:], bo_h.rearrange("(k p) -> p k", p=128))
            P.dma(bg_sb[:], bg_h.rearrange("(a p) -> p a", p=E))
            g18 = P.add("vector", lambda e: e.tensor_scalar(g18_sb[:], g1c_sb[:], 0.125, None, ALU.mult, ALU.bypass), after=[ld_g1])

            prev_st = None
            cmm = None
            for k in range(KD):
                for hh in range(4):
                    ldw = P.dma(slab[:], winT_h[k * 128:(k + 1) * 128, hh * 1024:(hh + 1) * 1024],
                                after=[prev_st])
                    cst = P.add("vector", lambda e: e.tensor_copy(slab_bf[:], slab[:]), after=[ldw, cmm])
                    for m8 in range(8):
                        m = hh * 8 + m8
                        cmm = P.add("tensor", lambda e, m=m, m8=m8, k=k, hh=hh: e.matmul(
                            pbG[:, m:m + 1], slab_bf[:, m8 * 128:(m8 + 1) * 128],
                            lnb_bf[:, k:k + 1],
                            start=(k == 0 and hh == 0 and m8 == 0),
                            stop=(k == KD - 1 and hh == 3 and m8 == 7)),
                            after=[cst, cv_lnb])
                    scl = P.add("scalar", lambda e, k=k: e.activation(
                        slab_bf[:], slab[:], AF.Copy, bias=0.0, scale=lng_c[:, k:k + 1]),
                        after=[cst, cmm, ld_lng])
                    prev_st = P.dma(winT_scr[k, :, hh * 1024:(hh + 1) * 1024], slab_bf[:], after=[scl])
            P.add("vector", lambda e: e.tensor_copy(cxz_sb[:], pbG[:, 0:32]), after=[cmm])

            for k in range(KD):
                ldw = P.dma(slab[:], wq_h[k * 128:(k + 1) * 128, :], after=[prev_st])
                scl = P.add("scalar", lambda e, k=k: e.activation(
                    slab_bf[:], slab[:], AF.Copy, bias=0.0, scale=g18_sb[:, k:k + 1]),
                    after=[ldw, g18])
                prev_st = P.dma(wq_scr[k], slab_bf[:], after=[scl])
            for k in range(KD):
                ldw = P.dma(slab[:], wo_h[k * 128:(k + 1) * 128, :], after=[prev_st])
                scl = P.add("scalar", lambda e: e.activation(
                    slab_bf[:], slab[:], AF.Copy, bias=0.0, scale=1.0), after=[ldw])
                st1 = P.dma(wo2_scr[2 * k], slab_bf[0:64, :], after=[scl])
                prev_st = P.dma(wo2_scr[2 * k + 1], slab_bf[64:128, :], after=[scl, st1])
            for ei in range(E):
                for k in range(KD):
                    ldw = P.dma(slab[:], we_h[ei, k * 128:(k + 1) * 128, :], after=[prev_st])
                    scl = P.add("scalar", lambda e, k=k: e.activation(
                        slab_bf[:], slab[:], AF.Copy, bias=0.0, scale=g2c_sb[:, k:k + 1]),
                        after=[ldw, ld_g2])
                    st1 = P.dma(we_scr[ei, 0, :, k, :], slab_bf[:, 0:512], after=[scl])
                    prev_st = P.dma(we_scr[ei, 1, :, k, :], slab_bf[:, 512:1024], after=[scl, st1])
            prev_wd = None
            prev_wg = None
            for k in range(KD):
                ldw = P.dma(slab[:, 0:LR], wdown_h[k * 128:(k + 1) * 128, :],
                            after=[prev_st, prev_wd])
                prev_st = None
                prev_wd = P.add("scalar", lambda e, k=k: e.activation(
                    wdown_sb[:, k * LR:(k + 1) * LR], slab[:, 0:LR], AF.Copy, bias=0.0,
                    scale=g1c_sb[:, k:k + 1]), after=[ldw, ld_g1])
                ldg = P.dma(slab[:, 512:512 + E], wg_h[k * 128:(k + 1) * 128, :],
                            after=[ldw, prev_wg])
                prev_wg = P.add("scalar", lambda e, k=k: e.activation(
                    wg_sb[:, k * E:(k + 1) * E], slab[:, 512:512 + E], AF.Copy, bias=0.0,
                    scale=g2c_sb[:, k:k + 1]), after=[ldg, ld_g2])
            l1 = P.dma(slab[0:64, :], wupk_h, after=[prev_wd, prev_wg])
            c1 = P.add("vector", lambda e: e.tensor_copy(wupk_sb[:], slab[0:64, :]), after=[l1])
            l2 = P.dma(slab[0:64, :], wupv_h, after=[l1, c1])
            c2 = P.add("vector", lambda e: e.tensor_copy(
                wupv3_sb[0:64, :].rearrange("p (d g) -> p d g", d=HD),
                slab[0:64, :].rearrange("p (g d) -> p g d", g=H).transpose([0, 2, 1])),
                after=[l2])
            l2b = P.dma(slab[64:65, 0:DM], bupv_h.unsqueeze(0), after=[l2])
            c2b = P.add("vector", lambda e: e.tensor_copy(
                wupv3_sb[64:65, :].rearrange("p (d g) -> p d g", d=HD),
                slab[64:65, 0:DM].rearrange("p (g d) -> p g d", g=H).transpose([0, 2, 1])),
                after=[l2b])
            prev_cp = c2b
            for i in range(ID):
                ld3 = P.dma(slab[:, 0:96], wxpT_h[i * 128:(i + 1) * 128, :], after=[prev_cp])
                prev_cp = P.add("vector", lambda e, i=i: e.tensor_copy(
                    wxpT_sb[:, i * 96:(i + 1) * 96], slab[:, 0:96]), after=[ld3])
            l4 = P.dma(slab[0:64, 0:DI // 2], wdtT_h[:, 0:DI // 2], after=[prev_cp])
            c4 = P.add("vector", lambda e: e.tensor_copy(wdtT_sb[:, 0:DI // 2], slab[0:64, 0:DI // 2]), after=[l4])
            l5 = P.dma(slab[0:64, 0:DI // 2], wdtT_h[:, DI // 2:], after=[l4, c4])
            prev_cp = P.add("vector", lambda e: e.tensor_copy(wdtT_sb[:, DI // 2:], slab[0:64, 0:DI // 2]), after=[l5])
            for i in range(ID):
                ldw = P.dma(slab[:], woutT_h[i * 128:(i + 1) * 128, :], after=[prev_cp])
                prev_cp = P.add("vector", lambda e, i=i: e.tensor_copy(
                    woutT_sb[:, i * DM:(i + 1) * DM], slab[:]), after=[ldw])
            P.barrier()

        with ExitStack() as pm:
            msb = lambda n, sh, dt=F32: pm.enter_context(nc.sbuf_tensor(n, sh, dt))
            x_tm = msb("x_tm", [128, 2, DM])
            xn_tm = msb("xn_tm", [128, 2, DM])
            sx_tm = msb("sx_tm", [128, 2, DM])
            xnA_fm = msb("xnA_fm", [128, KD, TCM], BF16)
            stat = msb("stat", [128, 32])
            xs_scr = [msb(f"xs_scr{i}", [128, DC - 1 + TCM]) for i in range(2)]
            ct_a = msb("ct_a", [128, TCM])
            ct_b = msb("ct_b", [128, TCM])
            halo_sb = msb("halo_sb", [128, ID * (DC - 1)])
            xc_bf = msb("xc_bf", [128, ID, TCM], BF16)
            zsil = msb("zsil", [128, ID, TCM], BF16)
            dt_fm = msb("dt_fm", [128, ID, TCM])
            udt_fm = msb("udt_fm", [128, ID, TCM], BF16)
            dblBC_bf = msb("dblBC_bf", [96, TCM], BF16)
            dtlo_bf = msb("dtlo_bf", [64, TCM], BF16)
            e1_scr = msb("e1_scr", [128, TCM])
            Bbc = msb("Bbc", [128, DS, TCM], BF16)
            Cbc = msb("Cbc", [128, DS, TCM], BF16)
            dA_s = [msb(f"dA{i}", [128, TCM]) for i in range(2)]
            U_all = msb("U_all", [128, DS, TCM], BF16)
            h_i = msb("h_i", [128, DS, TCM], BF16)
            pr_s = msb("pr_s", [128, TCM])
            prod_s = msb("prod_s", [128, DS, TCM], BF16)
            red_a = msb("red_a", [128, 8, TCM])
            red_b = msb("red_b", [128, 4, TCM])
            red_c = msb("red_c", [128, 2, TCM])
            y_s = msb("y_s", [128, TCM])
            hstate = msb("hstate", [128, ID * DS])
            ytot_bf = msb("ytot_bf", [128, ID, TCM], BF16)
            outA_sb = [msb(f"outA{i}", [128, TCM]) for i in range(2)]
            S1_tm = msb("S1_tm", [128, 2, DM])
            winbuf = [msb(f"winbuf{i}", [128, KD, 256], BF16) for i in range(2)]

            NCH = T // TCM
            pend_w = [None]
            st_s1_h = [None]
            wout_guard = [None]
            pb4_guard = [None]

            def emit_wout():
                pw = pend_w[0]
                if pw is None:
                    return None
                pend_w[0] = None
                fin_l, dt_last, ws, wt0 = pw
                s1_parts = []
                prev_ads = [None, None]
                wout_free = [None, None]
                last_mm = None
                ads_last = None
                for mo in range(KD):
                    bank = pb[1] if mo % 2 == 0 else pb[5]
                    mm = None
                    for i in range(ID):
                        mm = P.add("tensor", lambda e, mo=mo, i=i, b=bank: e.matmul(
                            b[:, 0:TCM], woutT_sb[:, i * DM + mo * 128:i * DM + (mo + 1) * 128],
                            ytot_bf[:, i, :], start=(i == 0), stop=(i == ID - 1)),
                            after=[fin_l[i], wout_free[mo % 2],
                                   dt_last if mo == 0 else None])
                    last_mm = mm
                    ev = P.add("scalar", lambda e, mo=mo, b=bank: e.activation(
                        outA_sb[mo % 2][:], b[:, 0:TCM], AF.Copy, bias=0.0, scale=1.0),
                        after=[mm])
                    wout_free[mo % 2] = ev
                    gbank = pbG if mo % 2 == 0 else pb[4]
                    tts_ = []
                    for j in range(2):
                        tts_.append(P.add("tensor", lambda e, mo=mo, j=j, gb=gbank: e.transpose(
                            gb[:, j * 128:(j + 1) * 128],
                            outA_sb[mo % 2][:, j * 128:(j + 1) * 128], eye32[:]),
                            after=[ev, prev_ads[mo % 2]]))
                    wout_free[mo % 2] = tts_[-1]
                    ad = P.add("vector", lambda e, mo=mo, gb=gbank: e.tensor_tensor(
                        S1_tm[:, 0:2, mo * 128:(mo + 1) * 128],
                        gb[:, 0:256].rearrange("p (a b) -> p a b", a=2),
                        sx_tm[:, 0:2, mo * 128:(mo + 1) * 128], ALU.add),
                        after=tts_ + ([st_s1_h[0]] if mo == 0 else []))
                    prev_ads[mo % 2] = ad
                    ads_last = ad
                    s1_parts.append(ad)
                st_s1_h[0] = P.dma(
                    s1_scr[ws, wt0:wt0 + TCM, :].rearrange("(a p) d -> p a d", p=128),
                    S1_tm[:], after=s1_parts)
                wout_guard[0] = last_mm
                pb4_guard[0] = ads_last
                return ads_last

            for s in range(NSEQ):
                P.barrier()
                P.add("vector", lambda e: e.memset(hstate[:], 0.0))
                P.add("vector", lambda e: e.memset(halo_sb[:], 0.0))
                for ci in range(NCH):
                    P.barrier()
                    t0 = ci * TCM
                    ld_x = P.dma(x_tm[:], x_h[s, t0:t0 + TCM, :].rearrange("(a p) d -> p a d", p=128))

                    sq_v = red_a[:, 0:4, :].rearrange("p a b -> p (a b)")
                    sq_prev = [None]

                    def stats_multi(entries, eps):
                        a1, a2, m1, m2, vr, sd, res = {}, {}, {}, {}, {}, {}, []
                        for k, (src, c0, deps) in enumerate(entries):
                            a1[k] = P.add("scalar", lambda e, src=src, c0=c0: e.activation(
                                sq_v, src, AF.Copy, bias=0.0, scale=1.0,
                                accum_out=stat[:, c0:c0 + 1]), after=deps + [sq_prev[0]])
                            sq_prev[0] = a1[k]
                        for k, (src, c0, deps) in enumerate(entries):
                            a2[k] = P.add("scalar", lambda e, src=src, c0=c0: e.activation(
                                sq_v, src, AF.Square, bias=0.0, scale=1.0,
                                accum_out=stat[:, c0 + 1:c0 + 2]), after=deps + [sq_prev[0]])
                            sq_prev[0] = a2[k]
                        for k, (src, c0, deps) in enumerate(entries):
                            m1[k] = P.add("vector", lambda e, c0=c0: e.tensor_scalar(
                                stat[:, c0 + 2:c0 + 3], stat[:, c0:c0 + 1], 1.0 / DM, None,
                                ALU.mult, ALU.bypass), after=[a1[k]])
                        for k, (src, c0, deps) in enumerate(entries):
                            m2[k] = P.add("vector", lambda e, c0=c0: e.tensor_scalar(
                                stat[:, c0 + 3:c0 + 4], stat[:, c0 + 2:c0 + 3],
                                stat[:, c0 + 2:c0 + 3], None, ALU.mult, ALU.bypass),
                                after=[m1[k]])
                        for k, (src, c0, deps) in enumerate(entries):
                            vr[k] = P.add("vector", lambda e, c0=c0: e.scalar_tensor_tensor(
                                stat[:, c0 + 4:c0 + 5], stat[:, c0 + 1:c0 + 2], 1.0 / DM,
                                stat[:, c0 + 3:c0 + 4], ALU.mult, ALU.subtract),
                                after=[a2[k], m2[k]])
                        for k, (src, c0, deps) in enumerate(entries):
                            sd[k] = P.add("scalar", lambda e, c0=c0: e.activation(
                                stat[:, c0 + 5:c0 + 6], stat[:, c0 + 4:c0 + 5], AF.Sqrt,
                                bias=eps, scale=1.0), after=[vr[k]])
                        for k, (src, c0, deps) in enumerate(entries):
                            res.append(P.add("vector", lambda e, c0=c0: e.reciprocal(
                                stat[:, c0 + 6:c0 + 7], stat[:, c0 + 5:c0 + 6]),
                                after=[sd[k]]))
                        return res

                    ivs = stats_multi([(x_tm[:, j, :], 16 * j, [ld_x]) for j in range(2)],
                                      consts_sb[:, 0:1])
                    xn_ops, xnA_ops, sx_ops = [], [], []
                    for j in range(2):
                        xn_ops.append(P.add("vector", lambda e, j=j, a=16 * j + 2, b=16 * j + 6:
                                            e.tensor_scalar(
                            xn_tm[:, j, :], x_tm[:, j, :], stat[:, a:a + 1], stat[:, b:b + 1],
                            ALU.subtract, ALU.mult), after=[ivs[j]]))
                    wads = emit_wout()
                    for j in range(2):
                        sx_ops.append(P.add("vector", lambda e, j=j: e.tensor_tensor(
                            sx_tm[:, j, :], x_tm[:, j, :], xn_tm[:, j, :], ALU.add),
                            after=[xn_ops[j], wads if j == 0 else None]))
                    ivs2 = stats_multi([(xn_tm[:, j, :], 16 * j + 8, [xn_ops[j]])
                                        for j in range(2)], consts_sb[:, 0:1])
                    for j in range(2):
                        xnA_ops.append(P.add("vector", lambda e, j=j, a=16 * j + 10,
                                             b=16 * j + 14: e.tensor_scalar(
                            x_tm[:, j, :], xn_tm[:, j, :], stat[:, a:a + 1], stat[:, b:b + 1],
                            ALU.subtract, ALU.mult), after=[ivs2[j], sx_ops[j]]))
                    st_xn = P.dma(xn_scr[s, t0:t0 + TCM, :].rearrange("(a p) d -> p a d", p=128),
                                  xn_tm[:], after=xn_ops)

                    fm_ev = []
                    for k in range(KD):
                        bank = pb[2 + (k % 2)]
                        tj = []
                        for j in range(2):
                            tj.append(P.add("tensor", lambda e, j=j, k=k, b=bank: e.transpose(
                                b[:, j * 128:(j + 1) * 128],
                                x_tm[:, j, k * 128:(k + 1) * 128], eye32[:]),
                                after=[xnA_ops[j], fm_ev[k - 2] if k >= 2 else None]))
                        fm_ev.append(P.add("scalar", lambda e, k=k, b=bank: e.activation(
                            xnA_fm[:, k, :], b[:, 0:TCM], AF.Copy, bias=0.0, scale=1.0), after=tj))

                    conv_done = [None] * ID
                    xcbf_done = [None] * ID
                    zs_done = [None] * ID
                    ev_m = [None] * 32
                    mm_last_of_g = [None] * 16
                    for g in range(16):
                        wld = []
                        war = [mm_last_of_g[g - 2]] if g >= 2 else []
                        for k in range(KD):
                            wld.append(P.dma(winbuf[g % 2][:, k, :],
                                             winT_scr[k, :, g * 256:(g + 1) * 256], after=war))
                        for m in range(g * 2, g * 2 + 2):
                            bank = pb[m % 2]
                            mm = None
                            for k in range(KD):
                                mm = P.add("tensor", lambda e, m=m, k=k, g=g, b=bank: e.matmul(
                                    b[:, 0:TCM], winbuf[g % 2][:, k, (m % 2) * 128:(m % 2 + 1) * 128],
                                    xnA_fm[:, k, :], start=(k == 0), stop=(k == KD - 1)),
                                    after=[wld[k], fm_ev[k], ev_m[m - 2] if m >= 2 else None])
                            mm_last_of_g[g] = mm
                            if m < 16:
                                i = m
                                ev = P.add("scalar", lambda e, m=m, i=i, b=bank: e.activation(
                                    xs_scr[i % 2][:, DC - 1:], b[:, 0:TCM], AF.Identity,
                                    bias=cxz_sb[:, m:m + 1], scale=1.0),
                                    after=[mm, conv_done[i - 2] if i >= 2 else None])
                                ev_m[m] = ev
                                hl = P.add("vector", lambda e, i=i: e.tensor_copy(
                                    xs_scr[i % 2][:, 0:DC - 1], halo_sb[:, i * 3:(i + 1) * 3]),
                                    after=[ev])
                                c1 = P.add("vector", lambda e, i=i: e.tensor_scalar(
                                    ct_a[:], xs_scr[i % 2][:, 0:TCM], convw_sb[:, 4 * i:4 * i + 1],
                                    None, ALU.mult, ALU.bypass), after=[ev, hl])
                                c2 = P.add("vector", lambda e, i=i: e.scalar_tensor_tensor(
                                    ct_b[:], xs_scr[i % 2][:, 1:1 + TCM], convw_sb[:, 4 * i + 1:4 * i + 2],
                                    ct_a[:], ALU.mult, ALU.add), after=[c1])
                                c3 = P.add("vector", lambda e, i=i: e.scalar_tensor_tensor(
                                    ct_a[:], xs_scr[i % 2][:, 2:2 + TCM], convw_sb[:, 4 * i + 2:4 * i + 3],
                                    ct_b[:], ALU.mult, ALU.add), after=[c2])
                                c4 = P.add("vector", lambda e, i=i: e.scalar_tensor_tensor(
                                    ct_b[:], xs_scr[i % 2][:, 3:3 + TCM], convw_sb[:, 4 * i + 3:4 * i + 4],
                                    ct_a[:], ALU.mult, ALU.add), after=[c3])
                                sv = P.add("vector", lambda e, i=i: e.tensor_copy(
                                    halo_sb[:, i * 3:(i + 1) * 3], xs_scr[i % 2][:, TCM:TCM + 3]),
                                    after=[ev])
                                sil = P.add("scalar", lambda e, i=i: e.activation(
                                    xc_bf[:, i, :], ct_b[:], AF.Silu, bias=convb_sb[:, i:i + 1],
                                    scale=1.0), after=[c4])
                                conv_done[i] = P.add("vector", lambda e: e.nop(), after=[c4, sv, sil])
                                xcbf_done[i] = sil
                            else:
                                i = m - 16
                                ev = P.add("scalar", lambda e, m=m, i=i, b=bank: e.activation(
                                    zsil[:, i, :], b[:, 0:TCM], AF.Silu, bias=cxz_sb[:, m:m + 1],
                                    scale=1.0), after=[mm])
                                ev_m[m] = ev
                                zs_done[i] = ev

                    mm = None
                    for i in range(ID):
                        mm = P.add("tensor", lambda e, i=i: e.matmul(
                            pb[0][0:96, 0:TCM], wxpT_sb[:, i * 96:(i + 1) * 96],
                            xc_bf[:, i, :], start=(i == 0), stop=(i == ID - 1)),
                            after=[xcbf_done[i], ev_m[30]])
                    ev_dtlo = P.add("scalar", lambda e: e.activation(
                        dtlo_bf[:], pb[0][0:64, 0:TCM], AF.Copy, bias=0.0, scale=1.0), after=[mm])
                    ev_bc = P.add("vector", lambda e: e.tensor_copy(
                        dblBC_bf[64:96, :], pb[0][64:96, 0:TCM]), after=[mm])
                    bc_ops = []
                    for n in range(DS):
                        bc_ops.append(P.dma(Bbc[:, n, :], bcast_view(dblBC_bf[64 + n:65 + n, :], TCM),
                                            after=[ev_bc]))
                        bc_ops.append(P.dma(Cbc[:, n, :], bcast_view(dblBC_bf[80 + n:81 + n, :], TCM),
                                            after=[ev_bc]))

                    dt_done = [None] * ID
                    udt_done = [None] * ID
                    ev_prev = None
                    for i in range(ID):
                        bank = pb[1] if i % 2 == 0 else pb[4]
                        mmdt = P.add("tensor", lambda e, i=i, b=bank: e.matmul(
                            b[:, 0:TCM], wdtT_sb[:, i * 128:(i + 1) * 128], dtlo_bf[:],
                            start=True, stop=True),
                            after=[ev_dtlo, ev_m[31] if i == 0 else None,
                                   dt_done[i - 2] if i >= 2 else None,
                                   pb4_guard[0] if i == 1 else None])
                        ex1 = P.add("scalar", lambda e, i=i, b=bank: e.activation(
                            e1_scr[:], b[:, 0:TCM], AF.Exp, bias=bdt_sb[:, i:i + 1], scale=1.0),
                            after=[mmdt, ev_prev])
                        ex2 = P.add("scalar", lambda e, i=i: e.activation(
                            dt_fm[:, i, :], e1_scr[:], AF.Ln, bias=consts_sb[:, 2:3], scale=1.0), after=[ex1])
                        ev_prev = ex2
                        dt_done[i] = ex2
                        udt_done[i] = P.add("vector", lambda e, i=i: e.tensor_tensor(
                            udt_fm[:, i, :], dt_fm[:, i, :], xc_bf[:, i, :], ALU.mult), after=[ex2])

                    fin = [None] * ID
                    tts_hist = {}
                    prev_f1 = None
                    hsave_prev = None
                    prev_reda = None
                    pend = None

                    def emit_f1(iq, yf):
                        nonlocal prev_f1
                        f1 = P.add("vector", lambda e, i=iq: e.scalar_tensor_tensor(
                            pr_s[:], xc_bf[:, i, :], D_sb[:, i:i + 1], y_s[:],
                            ALU.mult, ALU.add), after=[yf, fin[iq - 1] if iq > 0 else None])
                        prev_f1 = f1
                        return f1

                    def emit_fin(iq, f1):
                        fin[iq] = P.add("vector", lambda e, i=iq: e.tensor_tensor(
                            ytot_bf[:, i, :], pr_s[:], zsil[:, i, :], ALU.mult),
                            after=[f1, zs_done[iq]]
                            + ([wout_guard[0]] if iq == 0 else []))
                    for i in range(ID):
                        uall = P.add("vector", lambda e, i=i: e.tensor_tensor(
                            U_all[:, 0:DS, :], Bbc[:, 0:DS, :],
                            udt_fm[:, i, :].unsqueeze(1).broadcast_to([128, DS, TCM]),
                            ALU.mult),
                            after=[udt_done[i]] + ([bc_ops[2 * n] for n in range(DS)]
                                                   if i == 0 else []))
                        for n in range(DS):
                            idx = i * DS + n
                            ex = P.add("scalar", lambda e, i=i, n=n, sl=idx % 2: e.activation(
                                dA_s[sl][:], dt_fm[:, i, :], AF.Exp, bias=0.0,
                                scale=A_sb[:, i * DS + n:i * DS + n + 1]),
                                after=[dt_done[i], tts_hist.get(idx - 2)])
                            tts = P.add("vector", lambda e, i=i, n=n, idx=idx: e.tensor_tensor_scan(
                                h_i[:, n, :], dA_s[idx % 2][:],
                                U_all[:, n, :],
                                hstate[:, i * DS + n:i * DS + n + 1], ALU.mult, ALU.add),
                                after=[ex, uall, hsave_prev if n == 0 else None])
                            tts_hist[idx] = tts
                        pf1 = emit_f1(*pend) if pend is not None else None
                        hsave_prev = P.add("vector", lambda e, i=i: e.tensor_copy(
                            hstate[:, i * DS:(i + 1) * DS],
                            h_i[:, 0:DS, TCM - 1:TCM].squeeze(2)),
                            after=[tts_hist[i * DS + DS - 1]])
                        prodall = P.add("vector", lambda e, i=i: e.tensor_tensor(
                            prod_s[:, 0:DS, :], h_i[:, 0:DS, :], Cbc[:, 0:DS, :], ALU.mult),
                            after=[tts_hist[i * DS + DS - 1], prev_reda]
                            + ([bc_ops[2 * n + 1] for n in range(DS)] if i == 0 else []))
                        reda = P.add("gpsimd", lambda e: e.tensor_tensor(
                            red_a[:, 0:8, :].rearrange("p a b -> p (a b)"),
                            prod_s[:, 0:8, :].rearrange("p a b -> p (a b)"),
                            prod_s[:, 8:16, :].rearrange("p a b -> p (a b)"), ALU.add),
                            after=[prodall])
                        prev_reda = reda
                        redb = P.add("gpsimd", lambda e: e.tensor_tensor(
                            red_b[:, 0:4, :].rearrange("p a b -> p (a b)"),
                            red_a[:, 0:4, :].rearrange("p a b -> p (a b)"),
                            red_a[:, 4:8, :].rearrange("p a b -> p (a b)"), ALU.add),
                            after=[reda])
                        redc = P.add("gpsimd", lambda e: e.tensor_tensor(
                            red_c[:, 0:2, :].rearrange("p a b -> p (a b)"),
                            red_b[:, 0:2, :].rearrange("p a b -> p (a b)"),
                            red_b[:, 2:4, :].rearrange("p a b -> p (a b)"), ALU.add),
                            after=[redb])
                        if pend is not None:
                            emit_fin(pend[0], pf1)
                            pend = None
                        yfin = P.add("gpsimd", lambda e: e.tensor_tensor(
                            y_s[:], red_c[:, 0, :], red_c[:, 1, :], ALU.add),
                            after=[redc, prev_f1])
                        pend = (i, yfin)
                    pf1 = emit_f1(*pend)
                    emit_fin(pend[0], pf1)
                    pend = None

                    pend_w[0] = (list(fin), dt_done[ID - 1], s, t0)
                emit_wout()
            P.barrier()
        mres.close()

        with ExitStack() as pf:
            fsb = lambda n, sh, dt=F32: pf.enter_context(nc.sbuf_tensor(n, sh, dt))
            fxn_tm = fsb("f_xn", [128, 4, DM])
            x1_tm = fsb("f_x1", [128, 4, DM])
            out_tm = fsb("f_out", [128, 4, DM])
            h_scr = fsb("f_hs", [128, DM])
            fsq_scr = fsb("f_sq", [128, DM])
            fstat = fsb("f_stat", [128, 80])
            h1_fm = fsb("f_h1fm", [128, KD, TCF], BF16)
            h2_fm = fsb("f_h2fm", [128, KD, TCF], BF16)
            h2e_fm = [fsb(f"f_h2e{i}", [128, KD, TCF], BF16) for i in range(2)]
            q2ARR = fsb("f_q2", [128, H, 256], BF16)
            k2ARR = fsb("f_k2", [128, H, 256], BF16)
            qb1_scr = fsb("f_qb1", [64, H, 64], BF16)
            kb1_scr = fsb("f_kb1", [64, H, 64], BF16)
            vg_sb = fsb("f_vg", [16, 64, 128], BF16)
            r1_bf = fsb("f_rbf", [65, TCF], BF16)
            attn_b = fsb("f_attnb", [128, 512], BF16)
            attT_sb = fsb("f_attT", [16, 4096], BF16)
            out2slabs = fsb("f_o2sl", [64, H, TCF], BF16)
            aproj_sb = [fsb(f"f_ap{i}", [128, TCF]) for i in range(2)]
            g_tm = fsb("f_gtm", [128, 4 * E])
            gn_tm = fsb("f_gntm", [128, 4 * E])
            gn_bf = fsb("f_gnbf", [8, TCF], BF16)
            gbc = [fsb(f"f_gbc{i}", [128, TCF], BF16) for i in range(2)]
            moe_sb = [fsb(f"f_moe{i}", [128, TCF]) for i in range(2)]
            webuf = [fsb(f"f_we{i}", [128, KD, 512], BF16) for i in range(2)]
            wo2_sb = fsb("f_wo2", [64, H * 512], BF16)

            def wq_slice(k, h):
                base = (k % 4) * 2
                off = h * HD
                return webuf[k // 4][:, base + off // 512, off % 512:off % 512 + HD]

            def wo_load(half, war):
                return [P.dma(wo2_sb[:, h * 512:(h + 1) * 512],
                              wo2_scr[h, :, half * 512:(half + 1) * 512], after=war)
                        for h in range(H)]

            def rms_to_fm(src_tm, dst_fm, scol, eps, dep0):
                evs = []
                hprev = None
                prev_sq = None
                for j in range(4):
                    sq = P.add("scalar", lambda e, j=j: e.activation(
                        fsq_scr[:], src_tm[:, j, :], AF.Square, bias=0.0, scale=1.0,
                        accum_out=fstat[:, scol + j:scol + j + 1]), after=dep0 + [prev_sq])
                    prev_sq = sq
                    sd = P.add("scalar", lambda e, j=j: e.activation(
                        fstat[:, scol + 8 + j:scol + 9 + j], fstat[:, scol + j:scol + j + 1],
                        AF.Sqrt, bias=eps, scale=1.0 / DM), after=[sq])
                    iv = P.add("vector", lambda e, j=j: e.reciprocal(
                        fstat[:, scol + 16 + j:scol + 17 + j], fstat[:, scol + 8 + j:scol + 9 + j]),
                        after=[sd])
                    hh = P.add("vector", lambda e, j=j: e.tensor_scalar(
                        h_scr[:], src_tm[:, j, :], fstat[:, scol + 16 + j:scol + 17 + j], None,
                        ALU.mult, ALU.bypass), after=[iv, hprev])
                    tts_ = []
                    for k in range(KD):
                        bnk = pb[2 + (k // 4)]
                        tts_.append(P.add("tensor", lambda e, j=j, k=k, b=bnk: e.transpose(
                            b[:, (k % 4) * 128:(k % 4 + 1) * 128], h_scr[:, k * 128:(k + 1) * 128],
                            eye32[:]), after=[hh] + ([evs[-2], evs[-1]] if evs else [])))
                    e1 = P.add("scalar", lambda e, j=j: e.activation(
                        dst_fm[:, 0:4, j * 128:(j + 1) * 128], pb[2][:, 0:512].rearrange(
                            "p (a b) -> p a b", a=4), AF.Copy, bias=0.0, scale=1.0),
                        after=tts_[0:4])
                    e2 = P.add("scalar", lambda e, j=j: e.activation(
                        dst_fm[:, 4:8, j * 128:(j + 1) * 128], pb[3][:, 0:512].rearrange(
                            "p (a b) -> p a b", a=4), AF.Copy, bias=0.0, scale=1.0),
                        after=tts_[4:8])
                    evs += [e1, e2]
                    hprev = e2
                return evs

            NCF = T // TCF
            for s in range(NSEQ):
                for ci in range(NCF):
                    P.barrier()
                    t0 = ci * TCF
                    prev_wo_free = None
                    prev_r_user = None
                    wql = [P.dma(webuf[k // 4][:, (k % 4) * 2:(k % 4) * 2 + 2, :]
                                 .rearrange("p a b -> p (a b)"), wq_scr[k])
                           for k in range(KD)]
                    wolA = wo_load(0, [])
                    ld_xn = P.dma(fxn_tm[:], xn_scr[s, t0:t0 + TCF, :].rearrange("(a p) d -> p a d", p=128))
                    ld_s1 = P.dma(out_tm[:], s1_scr[s, t0:t0 + TCF, :].rearrange("(a p) d -> p a d", p=128))

                    h1_evs = rms_to_fm(fxn_tm, h1_fm, 0, consts_sb[:, 1:2], [ld_xn])

                    mm = None
                    for k in range(KD):
                        mm = P.add("tensor", lambda e, k=k: e.matmul(
                            pb[0][0:64, 0:TCF], wdown_sb[:, k * LR:(k + 1) * LR], h1_fm[:, k, :],
                            start=(k == 0), stop=(k == KD - 1)),
                            after=[h1_evs[-1], prev_wo_free])
                    ev_r = P.add("scalar", lambda e: e.activation(
                        r1_bf[0:64, :], pb[0][0:64, 0:TCF], AF.Identity, bias=bdown_sb[:], scale=1.0),
                        after=[mm, prev_r_user])
                    ones_r = P.add("vector", lambda e: e.memset(r1_bf[64:65, :], 1.0),
                                   after=[prev_r_user])

                    ev2prev = [ev_r, prev_wo_free]
                    q2_ev = []
                    for h in range(H):
                        bank = pb[h % 2]
                        mm = None
                        for b in range(2):
                            for k in range(KD):
                                mm = P.add("tensor", lambda e, h=h, k=k, b=b, bk=bank: e.matmul(
                                    bk[64 * b:64 * b + 64, 0:256],
                                    wq_slice(k, h),
                                    h1_fm[:, k, :].rearrange("p (t two) -> p two t", two=2)[:, b, :],
                                    start=(k == 0), stop=(k == KD - 1),
                                    tile_position=(0, 64 * b)),
                                    after=[h1_evs[-1], ev2prev[h % 2]] + (wql if h == 0 else []))
                        ev = P.add("scalar", lambda e, h=h, bk=bank: e.activation(
                            q2ARR[:, h, :], bk[:, 0:256], AF.Identity, bias=bq2_sb[:, h:h + 1],
                            scale=1.0), after=[mm])
                        ev2prev[h % 2] = ev
                        q2_ev.append(ev)
                    k2_ev = []
                    for g in range(H):
                        bank = pb[g % 2]
                        mm = None
                        for b in range(2):
                            mm = P.add("tensor", lambda e, g=g, b=b, bk=bank: e.matmul(
                                bk[64 * b:64 * b + 64, 0:256],
                                wupk_sb[:, g * HD:(g + 1) * HD],
                                r1_bf[0:64, :].rearrange("p (t two) -> p two t", two=2)[:, b, :],
                                start=True, stop=True,
                                tile_position=(0, 64 * b)),
                                after=[ev_r, ev2prev[g % 2]])
                        ev = P.add("scalar", lambda e, g=g, bk=bank: e.activation(
                            k2ARR[:, g, :], bk[:, 0:256], AF.Identity, bias=bupk2_sb[:, g:g + 1],
                            scale=1.0), after=[mm])
                        ev2prev[g % 2] = ev
                        k2_ev.append(ev)

                    slab_evs = []
                    prev_vg_user = None
                    prev_qk_free = [k2_ev[-2], k2_ev[-1]]
                    prev_av_free = [None, None]
                    prev_attn_user = None
                    prev_attT_user = None
                    prev_b1_user = None
                    for bi in range(4):
                        vg_evs = []
                        for hq in range(16):
                            bank = pb[hq % 2]
                            mms = []
                            for hh in range(4):
                                hd = hq * 4 + hh
                                mms.append(P.add("tensor", lambda e, hd=hd, hh=hh, bk=bank, bi=bi: e.matmul(
                                    bk[0:16, hh * 128:(hh + 1) * 128],
                                    wupv3_sb[:, hd * H:(hd + 1) * H],
                                    r1_bf[:, bi * 128:(bi + 1) * 128],
                                    start=True, stop=True),
                                    after=[ev_r, ones_r, ev2prev[hq % 2] if hq < 2 else None,
                                           vg_evs[hq - 2] if hq >= 2 else None,
                                           prev_vg_user if hq < 2 else None]))
                            vg_evs.append(P.add("vector", lambda e, hq=hq, bk=bank: e.tensor_copy(
                                vg_sb[:, hq * 4:hq * 4 + 4, :].rearrange("p a b -> p (a b)"),
                                bk[0:16, 0:512]), after=mms))
                        qbank = pb[2 + (bi % 2)]
                        cpq = P.add("vector", lambda e, bi=bi: e.tensor_copy(
                            qb1_scr[:], q2ARR[64:128, 0:H, bi * 64:(bi + 1) * 64]),
                            after=[q2_ev[-1], prev_b1_user])
                        cpk = P.add("vector", lambda e, bi=bi: e.tensor_copy(
                            kb1_scr[:], k2ARR[64:128, 0:H, bi * 64:(bi + 1) * 64]),
                            after=[k2_ev[-1], prev_b1_user])
                        qk_last = None
                        for q in range(64):
                            pidx = bi * 64 + q
                            j, slot = q % 4, q // 4
                            for b in range(2):
                                if b == 0:
                                    sq_ = q2ARR[0:64, 0:H, pidx:pidx + 1].rearrange("p a b -> p (a b)")
                                    sk_ = k2ARR[0:64, 0:H, pidx:pidx + 1].rearrange("p a b -> p (a b)")
                                else:
                                    sq_ = qb1_scr[:, 0:H, q:q + 1].rearrange("p a b -> p (a b)")
                                    sk_ = kb1_scr[:, 0:H, q:q + 1].rearrange("p a b -> p (a b)")
                                qk_last = P.add("tensor", lambda e, sq_=sq_, sk_=sk_, j=j, slot=slot, b=b, bk=qbank: e.matmul(
                                    bk[32 * j:32 * j + 16, slot * 32 + b * 16:slot * 32 + b * 16 + 16],
                                    sq_, sk_,
                                    start=True, stop=True,
                                    tile_position=(0, 32 * j), skip_group_check=True),
                                    after=[prev_qk_free[bi % 2], q2_ev[-1], k2_ev[-1], cpq, cpk] if q == 0 and b == 0 else [])
                        prev_b1_user = qk_last
                        nm = P.add("vector", lambda e, bk=qbank: e.tensor_reduce(
                            fstat[:, 0:32], bk[:, 0:512].rearrange("p (a b) -> p a b", a=32),
                            AX.X, ALU.max, negate=True), after=[qk_last])
                        sub = P.add("vector", lambda e, bk=qbank: e.tensor_tensor(
                            h_scr[:, 0:512].rearrange("p (a b) -> p a b", a=32),
                            bk[:, 0:512].rearrange("p (a b) -> p a b", a=32),
                            fstat[:, 0:32].unsqueeze(2).broadcast_to([128, 32, 16]), ALU.add),
                            after=[nm, h1_evs[-1]])
                        exp_ = P.add("scalar", lambda e: e.activation(
                            fsq_scr[:, 0:512], h_scr[:, 0:512], AF.Exp, bias=0.0, scale=1.0),
                            after=[sub])
                        smr = P.add("vector", lambda e: e.tensor_reduce(
                            fstat[:, 32:64], fsq_scr[:, 0:512].rearrange("p (a b) -> p a b", a=32),
                            AX.X, ALU.add, negate=False), after=[exp_])
                        rc = P.add("vector", lambda e: e.reciprocal(
                            fstat[:, 0:32], fstat[:, 32:64]), after=[smr])
                        nrm = P.add("vector", lambda e: e.tensor_tensor(
                            attn_b[:].rearrange("p (a b) -> p a b", a=32),
                            fsq_scr[:, 0:512].rearrange("p (a b) -> p a b", a=32),
                            fstat[:, 0:32].unsqueeze(2).broadcast_to([128, 32, 16]), ALU.mult),
                            after=[rc, prev_attn_user])
                        at_evs = []
                        tts_ = []
                        for gi in range(32):
                            slot, b = gi // 2, gi % 2
                            tts_.append(P.add("tensor", lambda e, gi=gi, slot=slot, b=b: e.transpose(
                                pbBF[0:16, (gi % 8) * 128:(gi % 8 + 1) * 128],
                                attn_b[:, slot * 32 + b * 16:slot * 32 + b * 16 + 16], eyebf[:]),
                                after=[nrm, at_evs[-1] if gi % 8 == 0 and at_evs else None,
                                       prev_attT_user if gi == 0 else None]))
                            if gi % 8 == 7:
                                at_evs.append(P.add("vector", lambda e, gi=gi: e.tensor_copy(
                                    attT_sb[:, (gi // 8) * 1024:(gi // 8 + 1) * 1024],
                                    pbBF[0:16, 0:1024]), after=tts_[-8:]))
                        for a in range(4):
                            abank = pb[4 + (a % 2)]
                            av_last = None
                            for ti in range(32):
                                tt = a * 32 + ti
                                t = bi * 128 + tt
                                qq = t // 2
                                j, slot, b = qq % 4, (qq // 4) % 16, t % 2
                                gi = slot * 2 + b
                                av_last = P.add("tensor", lambda e, tt=tt, ti=ti, j=j, gi=gi, bk=abank: e.matmul(
                                    bk[0:64, ti * 16:(ti + 1) * 16],
                                    vg_sb[:, 0:64, tt:tt + 1].rearrange("p a b -> p (a b)"),
                                    attT_sb[:, gi * 128 + 32 * j:gi * 128 + 32 * j + 16],
                                    start=True, stop=True),
                                    after=[at_evs[-1], vg_evs[-1], prev_av_free[a % 2]] if ti == 0 else [])
                            sev = P.add("vector", lambda e, bi=bi, a=a, bk=abank: e.tensor_copy(
                                out2slabs[:, :, bi * 128 + a * 32:bi * 128 + (a + 1) * 32].transpose([0, 2, 1]),
                                bk[0:64, 0:512].rearrange("p (a b) -> p a b", a=32)),
                                after=[av_last])
                            prev_av_free[a % 2] = sev
                            slab_evs.append(sev)
                        prev_qk_free[bi % 2] = sub
                        prev_attn_user = tts_[-1]
                        prev_attT_user = av_last
                        prev_vg_user = av_last
                        if DEBUG_TAPS and s == 0 and ci == 0 and bi == 0:
                            prev_attn_user = P.dma(dbg_h["attn"], attn_b[:], after=[tts_[-1]])
                            prev_attT_user = P.dma(dbg_h["attT"], attT_sb[:], after=[av_last])
                            prev_vg_user = P.dma(dbg_h["vg"], vg_sb[:], after=[av_last])
                    prev_r_user = av_last

                    if DEBUG_TAPS and s == 0 and ci == 0:
                        P.dma(dbg_h["o2sl"], out2slabs[:], after=slab_evs)
                        P.dma(dbg_h["q2"], q2ARR[:], after=[q2_ev[-1]])
                        P.dma(dbg_h["k2"], k2ARR[:], after=[k2_ev[-1]])
                    x1_ops = []
                    ap_free = [vg_evs[-2], vg_evs[-1]]
                    prev_tt2 = [None, None]
                    wolB = None
                    for mo in range(KD):
                        bank = pb[mo % 2]
                        mm = None
                        for h in range(H):
                            mm = P.add("tensor", lambda e, mo=mo, h=h, b=bank: e.matmul(
                                b[:, 0:TCF],
                                wo2_sb[:, h * 512 + (mo % 4) * 128:h * 512 + (mo % 4) * 128 + 128],
                                out2slabs[:, h, :], start=(h == 0), stop=(h == H - 1)),
                                after=[slab_evs[-1], ap_free[mo % 2]]
                                + (wolA if mo == 0 else [])
                                + (wolB if mo == 4 else []))
                        if mo == 3:
                            wolB = wo_load(1, [mm])
                        ev = P.add("scalar", lambda e, mo=mo, b=bank: e.activation(
                            aproj_sb[mo % 2][:], b[:, 0:TCF], AF.Identity, bias=bo_sb[:, mo:mo + 1],
                            scale=1.0), after=[mm])
                        ap_free[mo % 2] = ev
                        gbank = pbG if mo % 2 == 0 else pb[2]
                        tts_ = []
                        for j in range(4):
                            tts_.append(P.add("tensor", lambda e, mo=mo, j=j, gb=gbank: e.transpose(
                                gb[:, j * 128:(j + 1) * 128],
                                aproj_sb[mo % 2][:, j * 128:(j + 1) * 128], eye32[:]),
                                after=[ev, prev_tt2[mo % 2] if j == 0 else None]))
                        ad = P.add("vector", lambda e, mo=mo, gb=gbank: e.tensor_tensor(
                            x1_tm[:, 0:4, mo * 128:(mo + 1) * 128],
                            gb[:, 0:512].rearrange("p (a b) -> p a b", a=4),
                            fxn_tm[:, 0:4, mo * 128:(mo + 1) * 128], ALU.add), after=tts_)
                        prev_tt2[mo % 2] = ad
                        ap_free[mo % 2] = ad
                        x1_ops.append(ad)

                    if DEBUG_TAPS and s == 0 and ci == 0:
                        P.dma(dbg_h["x1"], x1_tm[:], after=[x1_ops[-1]])
                    h2_evs = rms_to_fm(x1_tm, h2_fm, 24, consts_sb[:, 1:2], [x1_ops[-1]])

                    mm = None
                    for k in range(KD):
                        mm = P.add("tensor", lambda e, k=k: e.matmul(
                            pb[0][0:E, 0:TCF], wg_sb[:, k * E:(k + 1) * E], h2_fm[:, k, :],
                            start=(k == 0), stop=(k == KD - 1)),
                            after=[h2_evs[-1], ap_free[0]])
                    gev = P.add("scalar", lambda e: e.activation(
                        h_scr[0:E, 0:TCF], pb[0][0:E, 0:TCF], AF.Identity, bias=bg_sb[:], scale=1.0),
                        after=[mm, x1_ops[-1]])
                    gt = []
                    for j in range(4):
                        gt.append(P.add("tensor", lambda e, j=j: e.transpose(
                            pbG[:, j * E:(j + 1) * E], h_scr[0:E, j * 128:(j + 1) * 128],
                            eye32[0:E, 0:E]),
                            after=[gev] + ([prev_tt2[0]] if j == 0 else [])))
                    gcp = P.add("vector", lambda e: e.tensor_copy(g_tm[:], pbG[:, 0:4 * E]), after=gt)
                    gsm = []
                    for j in range(4):
                        nm = P.add("vector", lambda e, j=j: e.tensor_reduce(
                            fstat[:, 40 + j:41 + j], g_tm[:, j * E:(j + 1) * E], AX.X, ALU.max,
                            negate=True), after=[gcp])
                        sb_ = P.add("vector", lambda e, j=j: e.tensor_tensor(
                            gn_tm[:, j * E:(j + 1) * E], g_tm[:, j * E:(j + 1) * E],
                            fstat[:, 40 + j:41 + j].broadcast_to([128, E]), ALU.add), after=[nm])
                        ex_ = P.add("scalar", lambda e, j=j: e.activation(
                            g_tm[:, j * E:(j + 1) * E], gn_tm[:, j * E:(j + 1) * E], AF.Exp,
                            bias=0.0, scale=1.0), after=[sb_])
                        sm_ = P.add("vector", lambda e, j=j: e.tensor_reduce(
                            fstat[:, 44 + j:45 + j], g_tm[:, j * E:(j + 1) * E], AX.X, ALU.add,
                            negate=False), after=[ex_])
                        rc_ = P.add("vector", lambda e, j=j: e.reciprocal(
                            fstat[:, 40 + j:41 + j], fstat[:, 44 + j:45 + j]), after=[sm_])
                        mu_ = P.add("vector", lambda e, j=j: e.tensor_scalar(
                            gn_tm[:, j * E:(j + 1) * E], g_tm[:, j * E:(j + 1) * E],
                            fstat[:, 40 + j:41 + j], None, ALU.mult, ALU.bypass), after=[rc_])
                        gsm.append(mu_)
                    gbt = []
                    for j in range(4):
                        gbt.append(P.add("tensor", lambda e, j=j: e.transpose(
                            pbG[0:E, j * 128:(j + 1) * 128], gn_tm[:, j * E:(j + 1) * E], eye32[:]),
                            after=[gsm[j]] + ([gcp] if j == 0 else [])))
                    gnf = P.add("vector", lambda e: e.tensor_copy(gn_bf[:], pbG[0:E, 0:TCF]), after=gbt)

                    wel = {}
                    moe_ads = []
                    for half in range(2):
                        mm_last = None
                        h2e_done = [None, None]
                        mm_of_e = [None] * E
                        for ei in range(E):
                            war = [mm_of_e[ei - 2]] if ei >= 2 else ([h2_evs[-1]] if half == 0 else [moe_ads[-1]])
                            wel[(half, ei)] = P.dma(webuf[ei % 2][:],
                                                    we_scr[ei, half], after=war)
                            gb = P.dma(gbc[ei % 2][:], bcast_view(gn_bf[ei:ei + 1, :], TCF),
                                       after=[gnf] + ([mm_of_e[ei - 2]] if ei >= 2 else []))
                            h2e = P.add("vector", lambda e, ei=ei: e.tensor_tensor(
                                h2e_fm[ei % 2][:], h2_fm[:],
                                gbc[ei % 2][:].unsqueeze(1).broadcast_to([128, KD, TCF]),
                                ALU.mult), after=[gb, h2_evs[-1], mm_of_e[ei - 2] if ei >= 2 else None])
                            mm = None
                            for mo4 in range(4):
                                for k in range(KD):
                                    mm = P.add("tensor", lambda e, ei=ei, mo4=mo4, k=k, half=half: e.matmul(
                                        pb[2 + mo4][:, 0:TCF],
                                        webuf[ei % 2][:, k, mo4 * 128:(mo4 + 1) * 128],
                                        h2e_fm[ei % 2][:, k, :],
                                        start=(ei == 0 and k == 0), stop=(ei == E - 1 and k == KD - 1)),
                                        after=[wel[(half, ei)], h2e,
                                               (h2_evs[-1] if half == 0 else moe_ads[-1]) if ei == 0 and mo4 == 0 and k == 0 else None])
                            mm_of_e[ei] = mm
                            mm_last = mm
                        for mo4 in range(4):
                            mo = half * 4 + mo4
                            ev = P.add("scalar", lambda e, mo4=mo4: e.activation(
                                moe_sb[mo4 % 2][:], pb[2 + mo4][:, 0:TCF], AF.Copy, bias=0.0,
                                scale=1.0), after=[mm_last, moe_ads[-2] if len(moe_ads) >= 2 else None])
                            tts_ = []
                            for j in range(4):
                                tts_.append(P.add("tensor", lambda e, j=j, mo4=mo4: e.transpose(
                                    pbG[:, j * 128:(j + 1) * 128],
                                    moe_sb[mo4 % 2][:, j * 128:(j + 1) * 128], eye32[:]),
                                    after=[ev, moe_ads[-1] if moe_ads else gnf]))
                            if half == 0 and mo4 == 0:
                                sx1 = []
                                for j in range(4):
                                    sx1.append(P.add("vector", lambda e, j=j: e.tensor_tensor(
                                        fxn_tm[:, j, :], out_tm[:, j, :], x1_tm[:, j, :], ALU.add),
                                        after=[ld_s1, x1_ops[-1], h2_evs[-1]]))
                                sx1_done = sx1[-1]
                            ad = P.add("vector", lambda e, mo=mo: e.tensor_tensor(
                                out_tm[:, 0:4, mo * 128:(mo + 1) * 128],
                                pbG[:, 0:512].rearrange("p (a b) -> p a b", a=4),
                                fxn_tm[:, 0:4, mo * 128:(mo + 1) * 128], ALU.add),
                                after=tts_ + [sx1_done])
                            moe_ads.append(ad)
                    st_out = P.dma(out_h[s, t0:t0 + TCF, :].rearrange("(a p) d -> p a d", p=128),
                                   out_tm[:], after=moe_ads)
                    out_stores.append(st_out)

        P.barrier()
        P.add("sync", lambda e: e.nop(), after=out_stores[-1:])

        blk = top.enter_context(nc.Block())
        with nc.allow_non_contiguous_dma(reason="small one-time vector layout loads"):
            P.emit(blk, sems, dsems)
    return nc


_NC_CACHE = None


def _get_nc():
    global _NC_CACHE
    if _NC_CACHE is None:
        _NC_CACHE = build_nc()
    return _NC_CACHE


def host_common(L0_attn_ln_g, L0_attn_ln_b, L0_attn_W_in, L0_attn_conv_w,
                L0_attn_conv_b, L0_attn_W_xp, L0_attn_W_dt, L0_attn_b_dt, L0_attn_A_log,
                L0_attn_D, L0_attn_W_out, L0_ffn_gamma1, L0_ffn_Wq, L0_ffn_bq,
                L0_ffn_Wdown, L0_ffn_bdown, L0_ffn_Wup, L0_ffn_bup, L0_ffn_Wo,
                L0_ffn_bo, L0_ffn_gamma2, L0_ffn_Wg, L0_ffn_bg, L0_ffn_We):
    f32 = lambda a: np.ascontiguousarray(np.asarray(a, dtype=np.float32))
    perm_k = np.array([h * 128 + hd for h in range(H) for hd in range(HD)])
    perm_v = perm_k + HD
    Wup = f32(L0_ffn_Wup)
    bup = f32(L0_ffn_bup)
    common = {
        "winT": f32(np.asarray(L0_attn_W_in).T),
        "wxpT": f32(np.asarray(L0_attn_W_xp).T),
        "wdtT": f32(np.asarray(L0_attn_W_dt).T),
        "woutT": f32(np.asarray(L0_attn_W_out).T),
        "wq": f32(L0_ffn_Wq),
        "wdown": f32(L0_ffn_Wdown),
        "wupk": np.ascontiguousarray(Wup[:, perm_k]),
        "wupv": np.ascontiguousarray(Wup[:, perm_v]),
        "wo": f32(L0_ffn_Wo),
        "wg": f32(L0_ffn_Wg),
        "we": f32(L0_ffn_We),
        "alog": f32(L0_attn_A_log),
        "dvec": f32(L0_attn_D),
        "convw": f32(L0_attn_conv_w),
        "convb": f32(L0_attn_conv_b),
        "bdt": f32(L0_attn_b_dt),
        "lng": f32(L0_attn_ln_g),
        "lnb": f32(L0_attn_ln_b),
        "g1": f32(L0_ffn_gamma1),
        "g2": f32(L0_ffn_gamma2),
        "bq": f32(L0_ffn_bq),
        "bdown": f32(L0_ffn_bdown),
        "bupk": np.ascontiguousarray(bup[perm_k]),
        "bupv": np.ascontiguousarray(bup[perm_v]),
        "bo": f32(L0_ffn_bo),
        "bg": f32(L0_ffn_bg),
        "eye32": np.eye(128, dtype=np.float32),
        "eyebf": np.eye(128, dtype=ml_dtypes.bfloat16),
    }
    return common


def kernel(L0_attn_x, **kw):
    x = np.ascontiguousarray(np.asarray(L0_attn_x, dtype=np.float32))
    common = host_common(**kw)
    in_maps = []
    for c in range(8):
        m = dict(common)
        m["x"] = np.ascontiguousarray(x[2 * c:2 * c + 2])
        in_maps.append(m)
    nc = _get_nc()
    res = run_bass_kernel_spmd(nc, in_maps, core_ids=list(range(8)))
    return np.concatenate([res.results[c]["out"] for c in range(8)], axis=0)


# version 16
# speedup vs baseline: 1.5720x; 1.0073x over previous
import sys
sys.path.insert(0, '/opt/trn_rl_repo')
from contextlib import ExitStack
import numpy as np
import ml_dtypes
import concourse.bass as bass
import concourse.mybir as mybir
from concourse.bass_utils import run_bass_kernel_spmd

F32 = mybir.dt.float32
BF16 = mybir.dt.bfloat16
AF = mybir.ActivationFunctionType
ALU = mybir.AluOpType
AX = mybir.AxisListType

DM, DI, DS, DC, DTR = 1024, 2048, 16, 4, 64
T, NSEQ = 2048, 2
H, HD, LR, E = 16, 64, 64, 8
TCM, TCF = 256, 512
KD = DM // 128
ID = DI // 128

DEBUG_TAPS = False
TRUNC = None
STRICT_WAITS = False
ELIDE_ENGS = ()


class Prog:
    ENGS = ("sync", "tensor", "vector", "scalar", "gpsimd")
    NDMA = 32

    def __init__(self):
        self.ops = []
        self._barrier = {}
        self._dma_rr = 0
        self._dma_cum = [0] * self.NDMA

    def add(self, eng, fn, after=(), is_dma=False):
        deps = [d for d in after if d is not None]
        deps += [b for b in self._barrier.values() if b is not None]
        o = {"eng": eng, "fn": fn, "after": deps, "id": len(self.ops),
             "is_dma": is_dma, "inc": False, "cum": None, "dsem": None, "dcum": None}
        if is_dma:
            s = self._dma_rr % self.NDMA
            self._dma_rr += 1
            self._dma_cum[s] += 16
            o["dsem"] = s
            o["dcum"] = self._dma_cum[s]
        for d in deps:
            assert d["id"] < o["id"]
        self.ops.append(o)
        return o

    def dma(self, out, in_, after=()):
        return self.add("sync", lambda e, o=out, i=in_: e.dma_start(out=o, in_=i),
                        after=after, is_dma=True)

    def barrier(self):
        last = {}
        for o in self.ops:
            last[o["eng"]] = o
        self._barrier = last

    def emit(self, block, sems, dsems):
        if TRUNC is not None:
            self.ops = self.ops[:TRUNC]
            keep = set(id(o) for o in self.ops)
            for o in self.ops:
                o["after"] = [d for d in o["after"] if id(d) in keep]
        for o in self.ops:
            for d in o["after"]:
                if (not STRICT_WAITS and not d["is_dma"] and d["eng"] == o["eng"]
                        and o["eng"] in ELIDE_ENGS):
                    continue
                d["inc"] = True
        cum = {e: 0 for e in self.ENGS}
        for o in self.ops:
            if o["inc"] and not o["is_dma"]:
                cum[o["eng"]] += 1
                o["cum"] = cum[o["eng"]]

        def stream(eng_name, eng):
            waited = {e: 0 for e in self.ENGS}
            dwaited = [0] * self.NDMA
            for o in self.ops:
                if o["eng"] != eng_name:
                    continue
                for d in o["after"]:
                    if d["is_dma"]:
                        if d["dcum"] > dwaited[d["dsem"]]:
                            eng.wait_ge(dsems[d["dsem"]], d["dcum"])
                            dwaited[d["dsem"]] = d["dcum"]
                    else:
                        if (not STRICT_WAITS and d["eng"] == eng_name
                                and eng_name in ELIDE_ENGS):
                            continue
                        if d["cum"] is not None and d["cum"] > waited[d["eng"]]:
                            eng.wait_ge(sems[d["eng"]], d["cum"])
                            waited[d["eng"]] = d["cum"]
                if o["is_dma"]:
                    if o["dcum"] > 16 and o["dcum"] - 16 > dwaited[o["dsem"]]:
                        eng.wait_ge(dsems[o["dsem"]], o["dcum"] - 16)
                        dwaited[o["dsem"]] = o["dcum"] - 16
                    instr = o["fn"](eng)
                    instr.then_inc(dsems[o["dsem"]], 16)
                else:
                    instr = o["fn"](eng)
                    if o["inc"]:
                        instr.then_inc(sems[eng_name], 1)

        @block.sync
        def _(e):
            stream("sync", e)

        @block.tensor
        def _(e):
            stream("tensor", e)

        @block.vector
        def _(e):
            stream("vector", e)

        @block.scalar
        def _(e):
            stream("scalar", e)

        @block.gpsimd
        def _(e):
            stream("gpsimd", e)


def bcast_view(row_ap, width):
    return row_ap.unsqueeze(1).broadcast_to([1, 128, width])


def build_nc():
    nc = bass.Bass("TRN2", target_bir_lowering=False, debug=False,
                   detect_race_conditions=STRICT_WAITS)
    P = Prog()

    din = lambda n, sh, dt=F32: nc.dram_tensor(n, sh, dt, kind="ExternalInput").ap()
    x_h = din("x", [NSEQ, T, DM])
    winT_h = din("winT", [DM, 2 * DI])
    wxpT_h = din("wxpT", [DI, 96])
    wdtT_h = din("wdtT", [DTR, DI])
    woutT_h = din("woutT", [DI, DM])
    wq_h = din("wq", [DM, DM])
    wdown_h = din("wdown", [DM, LR])
    wupk_h = din("wupk", [LR, DM])
    wupv_h = din("wupv", [LR, DM])
    wo_h = din("wo", [DM, DM])
    wg_h = din("wg", [DM, E])
    we_h = din("we", [E, DM, DM])
    alog_h = din("alog", [DI, DS])
    dvec_h = din("dvec", [DI])
    convw_h = din("convw", [DI, DC])
    convb_h = din("convb", [DI])
    bdt_h = din("bdt", [DI])
    lng_h = din("lng", [DM])
    lnb_h = din("lnb", [DM])
    g1_h = din("g1", [DM])
    g2_h = din("g2", [DM])
    bq_h = din("bq", [DM])
    bdown_h = din("bdown", [LR])
    bupk_h = din("bupk", [DM])
    bupv_h = din("bupv", [DM])
    bo_h = din("bo", [DM])
    bg_h = din("bg", [E])
    eye32_h = din("eye32", [128, 128])
    eyebf_h = din("eyebf", [128, 128], BF16)

    out_h = nc.dram_tensor("out", [NSEQ, T, DM], F32, kind="ExternalOutput").ap()
    skind = "ExternalOutput" if DEBUG_TAPS else "Internal"
    xn_scr = nc.dram_tensor("xn_scr", [NSEQ, T, DM], F32, kind=skind).ap()
    s1_scr = nc.dram_tensor("s1_scr", [NSEQ, T, DM], F32, kind=skind).ap()
    dbg_h = {}
    if DEBUG_TAPS:
        dbg_h["o2sl"] = nc.dram_tensor("dbg_o2sl", [64, H, TCF], BF16, kind="ExternalOutput").ap()
        dbg_h["x1"] = nc.dram_tensor("dbg_x1", [128, 4, DM], F32, kind="ExternalOutput").ap()
        dbg_h["q2"] = nc.dram_tensor("dbg_q2", [128, H, 256], BF16, kind="ExternalOutput").ap()
        dbg_h["k2"] = nc.dram_tensor("dbg_k2", [128, H, 256], BF16, kind="ExternalOutput").ap()
        dbg_h["attn"] = nc.dram_tensor("dbg_attn", [128, 512], BF16, kind="ExternalOutput").ap()
        dbg_h["vg"] = nc.dram_tensor("dbg_vg", [16, 64, 128], BF16, kind="ExternalOutput").ap()
        dbg_h["attT"] = nc.dram_tensor("dbg_attT", [16, 4096], BF16, kind="ExternalOutput").ap()
    winT_scr = nc.dram_tensor("winT_b", [KD, 128, 2 * DI], BF16, kind="Internal").ap()
    wq_scr = nc.dram_tensor("wq_b", [KD, 128, DM], BF16, kind="Internal").ap()
    wo2_scr = nc.dram_tensor("wo2_b", [H, 64, DM], BF16, kind="Internal").ap()
    we_scr = nc.dram_tensor("we_b", [E, 2, 128, KD, 512], BF16, kind="Internal").ap()

    out_stores = []

    with ExitStack() as top:
        sems = {e: top.enter_context(nc.semaphore("sem_" + e)) for e in Prog.ENGS}
        dsems = [top.enter_context(nc.semaphore(f"dma{i}")) for i in range(Prog.NDMA)]
        sb = lambda n, sh, dt=F32: top.enter_context(nc.sbuf_tensor(n, sh, dt))

        eye32 = sb("s_eye32", [128, 128])
        eyebf = sb("s_eyebf", [128, 128], BF16)
        consts_sb = sb("consts_sb", [128, 4])
        cxz_sb = sb("cxz_sb", [128, 32])
        bq2_sb = sb("bq2_sb", [128, H])
        bdown_sb = sb("bdown_sb", [64, 1])
        bupk2_sb = sb("bupk2_sb", [128, H])
        wupv3_sb = sb("wupv3_sb", [65, 64 * H], BF16)
        bo_sb = sb("bo_sb", [128, KD])
        bg_sb = sb("bg_sb", [E, 1])
        wdown_sb = sb("wdown_sb", [128, KD * LR], BF16)
        wupk_sb = sb("wupk_sb", [64, DM], BF16)
        wg_sb = sb("wg_sb", [128, KD * E], BF16)

        pb = [top.enter_context(nc.psum_tensor(f"pb{i}", [128, 512], F32)) for i in range(6)]
        pbBF = top.enter_context(nc.psum_tensor("pbBF", [128, 1024], BF16))
        pbG = top.enter_context(nc.psum_tensor("pbG", [128, 512], F32))

        mres = ExitStack()
        msb0 = lambda n, sh, dt=F32: mres.enter_context(nc.sbuf_tensor(n, sh, dt))
        A_sb = msb0("A_sb", [128, ID * DS])
        D_sb = msb0("D_sb", [128, ID])
        convw_sb = msb0("convw_sb", [128, ID * DC])
        convb_sb = msb0("convb_sb", [128, ID])
        bdt_sb = msb0("bdt_sb", [128, ID])
        wxpT_sb = msb0("wxpT_sb", [128, ID * 96], BF16)
        wdtT_sb = msb0("wdtT_sb", [64, DI], BF16)
        woutT_sb = msb0("woutT_sb", [128, ID * DM], BF16)

        with ExitStack() as prep:
            psb = lambda n, sh, dt=F32: prep.enter_context(nc.sbuf_tensor(n, sh, dt))
            slab = psb("p_slab", [128, DM])
            slab_bf = psb("p_slab_bf", [128, DM], BF16)
            vcol = psb("p_vcol", [128, 16])
            lnb_bf = psb("p_lnb_bf", [128, KD], BF16)
            g18_sb = psb("p_g18", [128, KD])
            g1c_sb = psb("p_g1c", [128, KD])
            g2c_sb = psb("p_g2c", [128, KD])
            lng_c = psb("p_lng", [128, KD])

            P.add("vector", lambda e: e.memset(consts_sb[:, 0:1], 1e-5))
            P.add("vector", lambda e: e.memset(consts_sb[:, 1:2], 1e-6))
            P.add("vector", lambda e: e.memset(consts_sb[:, 2:3], 1.0))
            P.dma(eye32[:], eye32_h)
            P.dma(eyebf[:], eyebf_h)
            ld_a = P.dma(A_sb[:], alog_h.rearrange("(i p) n -> p i n", p=128))
            e_a = P.add("scalar", lambda e: e.activation(A_sb[:], A_sb[:], AF.Exp, bias=0.0, scale=1.0), after=[ld_a])
            P.add("vector", lambda e: e.tensor_scalar(A_sb[:], A_sb[:], -1.0, None, ALU.mult, ALU.bypass), after=[e_a])
            P.dma(D_sb[:], dvec_h.rearrange("(i p) -> p i", p=128))
            P.dma(convw_sb[:], convw_h.rearrange("(i p) c -> p i c", p=128))
            P.dma(convb_sb[:], convb_h.rearrange("(i p) -> p i", p=128))
            P.dma(bdt_sb[:], bdt_h.rearrange("(i p) -> p i", p=128))
            ld_g1 = P.dma(g1c_sb[:], g1_h.rearrange("(k p) -> p k", p=128))
            ld_g2 = P.dma(g2c_sb[:], g2_h.rearrange("(k p) -> p k", p=128))
            ld_lng = P.dma(lng_c[:], lng_h.rearrange("(k p) -> p k", p=128))
            ld_lnb = P.dma(vcol[:, 0:KD], lnb_h.rearrange("(k p) -> p k", p=128))
            cv_lnb = P.add("vector", lambda e: e.tensor_copy(lnb_bf[:], vcol[:, 0:KD]), after=[ld_lnb])
            lb1 = P.dma(bq2_sb[0:64, :], bq_h.rearrange("(h d) -> d h", d=HD))
            lb2 = P.dma(bq2_sb[64:128, :], bq_h.rearrange("(h d) -> d h", d=HD))
            P.add("vector", lambda e: e.tensor_scalar(bq2_sb[:], bq2_sb[:], 0.125, None, ALU.mult, ALU.bypass), after=[lb1, lb2])
            P.dma(bdown_sb[:], bdown_h.rearrange("(a p) -> p a", p=64))
            P.dma(bupk2_sb[0:64, :], bupk_h.rearrange("(h d) -> d h", d=HD))
            P.dma(bupk2_sb[64:128, :], bupk_h.rearrange("(h d) -> d h", d=HD))
            P.dma(bo_sb[:], bo_h.rearrange("(k p) -> p k", p=128))
            P.dma(bg_sb[:], bg_h.rearrange("(a p) -> p a", p=E))
            g18 = P.add("vector", lambda e: e.tensor_scalar(g18_sb[:], g1c_sb[:], 0.125, None, ALU.mult, ALU.bypass), after=[ld_g1])

            prev_st = None
            cmm = None
            for k in range(KD):
                for hh in range(4):
                    ldw = P.dma(slab[:], winT_h[k * 128:(k + 1) * 128, hh * 1024:(hh + 1) * 1024],
                                after=[prev_st])
                    cst = P.add("vector", lambda e: e.tensor_copy(slab_bf[:], slab[:]), after=[ldw, cmm])
                    for m8 in range(8):
                        m = hh * 8 + m8
                        cmm = P.add("tensor", lambda e, m=m, m8=m8, k=k, hh=hh: e.matmul(
                            pbG[:, m:m + 1], slab_bf[:, m8 * 128:(m8 + 1) * 128],
                            lnb_bf[:, k:k + 1],
                            start=(k == 0 and hh == 0 and m8 == 0),
                            stop=(k == KD - 1 and hh == 3 and m8 == 7)),
                            after=[cst, cv_lnb])
                    scl = P.add("scalar", lambda e, k=k: e.activation(
                        slab_bf[:], slab[:], AF.Copy, bias=0.0, scale=lng_c[:, k:k + 1]),
                        after=[cst, cmm, ld_lng])
                    prev_st = P.dma(winT_scr[k, :, hh * 1024:(hh + 1) * 1024], slab_bf[:], after=[scl])
            P.add("vector", lambda e: e.tensor_copy(cxz_sb[:], pbG[:, 0:32]), after=[cmm])

            for k in range(KD):
                ldw = P.dma(slab[:], wq_h[k * 128:(k + 1) * 128, :], after=[prev_st])
                scl = P.add("scalar", lambda e, k=k: e.activation(
                    slab_bf[:], slab[:], AF.Copy, bias=0.0, scale=g18_sb[:, k:k + 1]),
                    after=[ldw, g18])
                prev_st = P.dma(wq_scr[k], slab_bf[:], after=[scl])
            for k in range(KD):
                ldw = P.dma(slab[:], wo_h[k * 128:(k + 1) * 128, :], after=[prev_st])
                scl = P.add("scalar", lambda e: e.activation(
                    slab_bf[:], slab[:], AF.Copy, bias=0.0, scale=1.0), after=[ldw])
                st1 = P.dma(wo2_scr[2 * k], slab_bf[0:64, :], after=[scl])
                prev_st = P.dma(wo2_scr[2 * k + 1], slab_bf[64:128, :], after=[scl, st1])
            for ei in range(E):
                for k in range(KD):
                    ldw = P.dma(slab[:], we_h[ei, k * 128:(k + 1) * 128, :], after=[prev_st])
                    scl = P.add("scalar", lambda e, k=k: e.activation(
                        slab_bf[:], slab[:], AF.Copy, bias=0.0, scale=g2c_sb[:, k:k + 1]),
                        after=[ldw, ld_g2])
                    st1 = P.dma(we_scr[ei, 0, :, k, :], slab_bf[:, 0:512], after=[scl])
                    prev_st = P.dma(we_scr[ei, 1, :, k, :], slab_bf[:, 512:1024], after=[scl, st1])
            prev_wd = None
            prev_wg = None
            for k in range(KD):
                ldw = P.dma(slab[:, 0:LR], wdown_h[k * 128:(k + 1) * 128, :],
                            after=[prev_st, prev_wd])
                prev_st = None
                prev_wd = P.add("scalar", lambda e, k=k: e.activation(
                    wdown_sb[:, k * LR:(k + 1) * LR], slab[:, 0:LR], AF.Copy, bias=0.0,
                    scale=g1c_sb[:, k:k + 1]), after=[ldw, ld_g1])
                ldg = P.dma(slab[:, 512:512 + E], wg_h[k * 128:(k + 1) * 128, :],
                            after=[ldw, prev_wg])
                prev_wg = P.add("scalar", lambda e, k=k: e.activation(
                    wg_sb[:, k * E:(k + 1) * E], slab[:, 512:512 + E], AF.Copy, bias=0.0,
                    scale=g2c_sb[:, k:k + 1]), after=[ldg, ld_g2])
            l1 = P.dma(slab[0:64, :], wupk_h, after=[prev_wd, prev_wg])
            c1 = P.add("vector", lambda e: e.tensor_copy(wupk_sb[:], slab[0:64, :]), after=[l1])
            l2 = P.dma(slab[0:64, :], wupv_h, after=[l1, c1])
            c2 = P.add("vector", lambda e: e.tensor_copy(
                wupv3_sb[0:64, :].rearrange("p (d g) -> p d g", d=HD),
                slab[0:64, :].rearrange("p (g d) -> p g d", g=H).transpose([0, 2, 1])),
                after=[l2])
            l2b = P.dma(slab[64:65, 0:DM], bupv_h.unsqueeze(0), after=[l2])
            c2b = P.add("vector", lambda e: e.tensor_copy(
                wupv3_sb[64:65, :].rearrange("p (d g) -> p d g", d=HD),
                slab[64:65, 0:DM].rearrange("p (g d) -> p g d", g=H).transpose([0, 2, 1])),
                after=[l2b])
            prev_cp = c2b
            for i in range(ID):
                ld3 = P.dma(slab[:, 0:96], wxpT_h[i * 128:(i + 1) * 128, :], after=[prev_cp])
                prev_cp = P.add("vector", lambda e, i=i: e.tensor_copy(
                    wxpT_sb[:, i * 96:(i + 1) * 96], slab[:, 0:96]), after=[ld3])
            l4 = P.dma(slab[0:64, 0:DI // 2], wdtT_h[:, 0:DI // 2], after=[prev_cp])
            c4 = P.add("vector", lambda e: e.tensor_copy(wdtT_sb[:, 0:DI // 2], slab[0:64, 0:DI // 2]), after=[l4])
            l5 = P.dma(slab[0:64, 0:DI // 2], wdtT_h[:, DI // 2:], after=[l4, c4])
            prev_cp = P.add("vector", lambda e: e.tensor_copy(wdtT_sb[:, DI // 2:], slab[0:64, 0:DI // 2]), after=[l5])
            for i in range(ID):
                ldw = P.dma(slab[:], woutT_h[i * 128:(i + 1) * 128, :], after=[prev_cp])
                prev_cp = P.add("vector", lambda e, i=i: e.tensor_copy(
                    woutT_sb[:, i * DM:(i + 1) * DM], slab[:]), after=[ldw])
            P.barrier()

        with ExitStack() as pm:
            msb = lambda n, sh, dt=F32: pm.enter_context(nc.sbuf_tensor(n, sh, dt))
            x_tm = msb("x_tm", [128, 2, DM])
            xn_tm = msb("xn_tm", [128, 2, DM])
            sx_tm = msb("sx_tm", [128, 2, DM])
            xnA_fm = msb("xnA_fm", [128, KD, TCM], BF16)
            stat = msb("stat", [128, 32])
            xs_scr = [msb(f"xs_scr{i}", [128, DC - 1 + TCM]) for i in range(2)]
            ct_a = msb("ct_a", [128, TCM])
            ct_b = msb("ct_b", [128, TCM])
            halo_sb = msb("halo_sb", [128, ID * (DC - 1)])
            xc_bf = msb("xc_bf", [128, ID, TCM], BF16)
            zsil = msb("zsil", [128, ID, TCM], BF16)
            dt_fm = msb("dt_fm", [128, ID, TCM])
            udt_fm = msb("udt_fm", [128, ID, TCM], BF16)
            dblBC_bf = msb("dblBC_bf", [96, TCM], BF16)
            dtlo_bf = msb("dtlo_bf", [64, TCM], BF16)
            e1_scr = msb("e1_scr", [128, TCM])
            Bbc = msb("Bbc", [128, DS, TCM], BF16)
            Cbc = msb("Cbc", [128, DS, TCM], BF16)
            dA_s = [msb(f"dA{i}", [128, TCM]) for i in range(2)]
            U_all = msb("U_all", [128, DS, TCM], BF16)
            h_i = msb("h_i", [128, DS, TCM], BF16)
            pr_s = msb("pr_s", [128, TCM])
            prod_s = msb("prod_s", [128, DS, TCM], BF16)
            red_a = msb("red_a", [128, 8, TCM])
            red_b = msb("red_b", [128, 4, TCM])
            red_c = msb("red_c", [128, 2, TCM])
            y_s = msb("y_s", [128, TCM])
            hstate = msb("hstate", [128, ID * DS])
            ytot_bf = msb("ytot_bf", [128, ID, TCM], BF16)
            outA_sb = [msb(f"outA{i}", [128, TCM]) for i in range(2)]
            S1_tm = msb("S1_tm", [128, 2, DM])
            winbuf = [msb(f"winbuf{i}", [128, KD, 256], BF16) for i in range(2)]

            NCH = T // TCM
            pend_w = [None]
            st_s1_h = [None]
            wout_guard = [None]
            pb4_guard = [None]

            def emit_wout():
                pw = pend_w[0]
                if pw is None:
                    return None
                pend_w[0] = None
                fin_l, dt_last, ws, wt0 = pw
                s1_parts = []
                prev_ads = [None, None]
                wout_free = [None, None]
                last_mm = None
                ads_last = None
                for mo in range(KD):
                    bank = pb[1] if mo % 2 == 0 else pb[5]
                    mm = None
                    for i in range(ID):
                        mm = P.add("tensor", lambda e, mo=mo, i=i, b=bank: e.matmul(
                            b[:, 0:TCM], woutT_sb[:, i * DM + mo * 128:i * DM + (mo + 1) * 128],
                            ytot_bf[:, i, :], start=(i == 0), stop=(i == ID - 1)),
                            after=[fin_l[i], wout_free[mo % 2],
                                   dt_last if mo == 0 else None])
                    last_mm = mm
                    ev = P.add("scalar", lambda e, mo=mo, b=bank: e.activation(
                        outA_sb[mo % 2][:], b[:, 0:TCM], AF.Copy, bias=0.0, scale=1.0),
                        after=[mm])
                    wout_free[mo % 2] = ev
                    gbank = pbG if mo % 2 == 0 else pb[4]
                    tts_ = []
                    for j in range(2):
                        tts_.append(P.add("tensor", lambda e, mo=mo, j=j, gb=gbank: e.transpose(
                            gb[:, j * 128:(j + 1) * 128],
                            outA_sb[mo % 2][:, j * 128:(j + 1) * 128], eye32[:]),
                            after=[ev, prev_ads[mo % 2]]))
                    wout_free[mo % 2] = tts_[-1]
                    ad = P.add("vector", lambda e, mo=mo, gb=gbank: e.tensor_tensor(
                        S1_tm[:, 0:2, mo * 128:(mo + 1) * 128],
                        gb[:, 0:256].rearrange("p (a b) -> p a b", a=2),
                        sx_tm[:, 0:2, mo * 128:(mo + 1) * 128], ALU.add),
                        after=tts_ + ([st_s1_h[0]] if mo == 0 else []))
                    prev_ads[mo % 2] = ad
                    ads_last = ad
                    s1_parts.append(ad)
                st_s1_h[0] = P.dma(
                    s1_scr[ws, wt0:wt0 + TCM, :].rearrange("(a p) d -> p a d", p=128),
                    S1_tm[:], after=s1_parts)
                wout_guard[0] = last_mm
                pb4_guard[0] = ads_last
                return ads_last

            for s in range(NSEQ):
                P.barrier()
                P.add("vector", lambda e: e.memset(hstate[:], 0.0))
                P.add("vector", lambda e: e.memset(halo_sb[:], 0.0))
                for ci in range(NCH):
                    P.barrier()
                    t0 = ci * TCM
                    ld_x = P.dma(x_tm[:], x_h[s, t0:t0 + TCM, :].rearrange("(a p) d -> p a d", p=128))

                    sq_v = red_a[:, 0:4, :].rearrange("p a b -> p (a b)")
                    sq_prev = [None]

                    def stats_multi(entries, eps):
                        a1, a2, m1, m2, vr, sd, res = {}, {}, {}, {}, {}, {}, []
                        for k, (src, c0, deps) in enumerate(entries):
                            a1[k] = P.add("scalar", lambda e, src=src, c0=c0: e.activation(
                                sq_v, src, AF.Copy, bias=0.0, scale=1.0,
                                accum_out=stat[:, c0:c0 + 1]), after=deps + [sq_prev[0]])
                            sq_prev[0] = a1[k]
                        for k, (src, c0, deps) in enumerate(entries):
                            a2[k] = P.add("scalar", lambda e, src=src, c0=c0: e.activation(
                                sq_v, src, AF.Square, bias=0.0, scale=1.0,
                                accum_out=stat[:, c0 + 1:c0 + 2]), after=deps + [sq_prev[0]])
                            sq_prev[0] = a2[k]
                        for k, (src, c0, deps) in enumerate(entries):
                            m1[k] = P.add("vector", lambda e, c0=c0: e.tensor_scalar(
                                stat[:, c0 + 2:c0 + 3], stat[:, c0:c0 + 1], 1.0 / DM, None,
                                ALU.mult, ALU.bypass), after=[a1[k]])
                        for k, (src, c0, deps) in enumerate(entries):
                            m2[k] = P.add("vector", lambda e, c0=c0: e.tensor_scalar(
                                stat[:, c0 + 3:c0 + 4], stat[:, c0 + 2:c0 + 3],
                                stat[:, c0 + 2:c0 + 3], None, ALU.mult, ALU.bypass),
                                after=[m1[k]])
                        for k, (src, c0, deps) in enumerate(entries):
                            vr[k] = P.add("vector", lambda e, c0=c0: e.scalar_tensor_tensor(
                                stat[:, c0 + 4:c0 + 5], stat[:, c0 + 1:c0 + 2], 1.0 / DM,
                                stat[:, c0 + 3:c0 + 4], ALU.mult, ALU.subtract),
                                after=[a2[k], m2[k]])
                        for k, (src, c0, deps) in enumerate(entries):
                            sd[k] = P.add("scalar", lambda e, c0=c0: e.activation(
                                stat[:, c0 + 5:c0 + 6], stat[:, c0 + 4:c0 + 5], AF.Sqrt,
                                bias=eps, scale=1.0), after=[vr[k]])
                        for k, (src, c0, deps) in enumerate(entries):
                            res.append(P.add("vector", lambda e, c0=c0: e.reciprocal(
                                stat[:, c0 + 6:c0 + 7], stat[:, c0 + 5:c0 + 6]),
                                after=[sd[k]]))
                        return res

                    ivs = stats_multi([(x_tm[:, j, :], 16 * j, [ld_x]) for j in range(2)],
                                      consts_sb[:, 0:1])
                    xn_ops, xnA_ops, sx_ops = [], [], []
                    for j in range(2):
                        xn_ops.append(P.add("vector", lambda e, j=j, a=16 * j + 2, b=16 * j + 6:
                                            e.tensor_scalar(
                            xn_tm[:, j, :], x_tm[:, j, :], stat[:, a:a + 1], stat[:, b:b + 1],
                            ALU.subtract, ALU.mult), after=[ivs[j]]))
                    wads = emit_wout()
                    for j in range(2):
                        sx_ops.append(P.add("vector", lambda e, j=j: e.tensor_tensor(
                            sx_tm[:, j, :], x_tm[:, j, :], xn_tm[:, j, :], ALU.add),
                            after=[xn_ops[j], wads if j == 0 else None]))
                    ivs2 = stats_multi([(xn_tm[:, j, :], 16 * j + 8, [xn_ops[j]])
                                        for j in range(2)], consts_sb[:, 0:1])
                    for j in range(2):
                        xnA_ops.append(P.add("vector", lambda e, j=j, a=16 * j + 10,
                                             b=16 * j + 14: e.tensor_scalar(
                            x_tm[:, j, :], xn_tm[:, j, :], stat[:, a:a + 1], stat[:, b:b + 1],
                            ALU.subtract, ALU.mult), after=[ivs2[j], sx_ops[j]]))
                    st_xn = P.dma(xn_scr[s, t0:t0 + TCM, :].rearrange("(a p) d -> p a d", p=128),
                                  xn_tm[:], after=xn_ops)

                    fm_ev = []
                    for k in range(KD):
                        bank = pb[2 + (k % 2)]
                        tj = []
                        for j in range(2):
                            tj.append(P.add("tensor", lambda e, j=j, k=k, b=bank: e.transpose(
                                b[:, j * 128:(j + 1) * 128],
                                x_tm[:, j, k * 128:(k + 1) * 128], eye32[:]),
                                after=[xnA_ops[j], fm_ev[k - 2] if k >= 2 else None]))
                        fm_ev.append(P.add("scalar", lambda e, k=k, b=bank: e.activation(
                            xnA_fm[:, k, :], b[:, 0:TCM], AF.Copy, bias=0.0, scale=1.0), after=tj))

                    conv_done = [None] * ID
                    xcbf_done = [None] * ID
                    zs_done = [None] * ID
                    ev_m = [None] * 32
                    mm_last_of_g = [None] * 16
                    for g in range(16):
                        wld = []
                        war = [mm_last_of_g[g - 2]] if g >= 2 else []
                        for k in range(KD):
                            wld.append(P.dma(winbuf[g % 2][:, k, :],
                                             winT_scr[k, :, g * 256:(g + 1) * 256], after=war))
                        for m in range(g * 2, g * 2 + 2):
                            bank = pb[m % 2]
                            mm = None
                            for k in range(KD):
                                mm = P.add("tensor", lambda e, m=m, k=k, g=g, b=bank: e.matmul(
                                    b[:, 0:TCM], winbuf[g % 2][:, k, (m % 2) * 128:(m % 2 + 1) * 128],
                                    xnA_fm[:, k, :], start=(k == 0), stop=(k == KD - 1)),
                                    after=[wld[k], fm_ev[k], ev_m[m - 2] if m >= 2 else None])
                            mm_last_of_g[g] = mm
                            if m < 16:
                                i = m
                                ev = P.add("scalar", lambda e, m=m, i=i, b=bank: e.activation(
                                    xs_scr[i % 2][:, DC - 1:], b[:, 0:TCM], AF.Identity,
                                    bias=cxz_sb[:, m:m + 1], scale=1.0),
                                    after=[mm, conv_done[i - 2] if i >= 2 else None])
                                ev_m[m] = ev
                                hl = P.add("vector", lambda e, i=i: e.tensor_copy(
                                    xs_scr[i % 2][:, 0:DC - 1], halo_sb[:, i * 3:(i + 1) * 3]),
                                    after=[ev])
                                c1 = P.add("vector", lambda e, i=i: e.tensor_scalar(
                                    ct_a[:], xs_scr[i % 2][:, 0:TCM], convw_sb[:, 4 * i:4 * i + 1],
                                    None, ALU.mult, ALU.bypass), after=[ev, hl])
                                c2 = P.add("vector", lambda e, i=i: e.scalar_tensor_tensor(
                                    ct_b[:], xs_scr[i % 2][:, 1:1 + TCM], convw_sb[:, 4 * i + 1:4 * i + 2],
                                    ct_a[:], ALU.mult, ALU.add), after=[c1])
                                c3 = P.add("vector", lambda e, i=i: e.scalar_tensor_tensor(
                                    ct_a[:], xs_scr[i % 2][:, 2:2 + TCM], convw_sb[:, 4 * i + 2:4 * i + 3],
                                    ct_b[:], ALU.mult, ALU.add), after=[c2])
                                c4 = P.add("vector", lambda e, i=i: e.scalar_tensor_tensor(
                                    ct_b[:], xs_scr[i % 2][:, 3:3 + TCM], convw_sb[:, 4 * i + 3:4 * i + 4],
                                    ct_a[:], ALU.mult, ALU.add), after=[c3])
                                sv = P.add("vector", lambda e, i=i: e.tensor_copy(
                                    halo_sb[:, i * 3:(i + 1) * 3], xs_scr[i % 2][:, TCM:TCM + 3]),
                                    after=[ev])
                                sil = P.add("scalar", lambda e, i=i: e.activation(
                                    xc_bf[:, i, :], ct_b[:], AF.Silu, bias=convb_sb[:, i:i + 1],
                                    scale=1.0), after=[c4])
                                conv_done[i] = P.add("vector", lambda e: e.nop(), after=[c4, sv, sil])
                                xcbf_done[i] = sil
                            else:
                                i = m - 16
                                ev = P.add("scalar", lambda e, m=m, i=i, b=bank: e.activation(
                                    zsil[:, i, :], b[:, 0:TCM], AF.Silu, bias=cxz_sb[:, m:m + 1],
                                    scale=1.0), after=[mm])
                                ev_m[m] = ev
                                zs_done[i] = ev

                    mm = None
                    for i in range(ID):
                        mm = P.add("tensor", lambda e, i=i: e.matmul(
                            pb[0][0:96, 0:TCM], wxpT_sb[:, i * 96:(i + 1) * 96],
                            xc_bf[:, i, :], start=(i == 0), stop=(i == ID - 1)),
                            after=[xcbf_done[i], ev_m[30]])
                    ev_dtlo = P.add("scalar", lambda e: e.activation(
                        dtlo_bf[:], pb[0][0:64, 0:TCM], AF.Copy, bias=0.0, scale=1.0), after=[mm])
                    ev_bc = P.add("vector", lambda e: e.tensor_copy(
                        dblBC_bf[64:96, :], pb[0][64:96, 0:TCM]), after=[mm])
                    bc_ops = []
                    for n in range(DS):
                        bc_ops.append(P.dma(Bbc[:, n, :], bcast_view(dblBC_bf[64 + n:65 + n, :], TCM),
                                            after=[ev_bc]))
                        bc_ops.append(P.dma(Cbc[:, n, :], bcast_view(dblBC_bf[80 + n:81 + n, :], TCM),
                                            after=[ev_bc]))

                    dt_done = [None] * ID
                    udt_done = [None] * ID
                    ev_prev = None
                    for i in range(ID):
                        bank = pb[1] if i % 2 == 0 else pb[4]
                        mmdt = P.add("tensor", lambda e, i=i, b=bank: e.matmul(
                            b[:, 0:TCM], wdtT_sb[:, i * 128:(i + 1) * 128], dtlo_bf[:],
                            start=True, stop=True),
                            after=[ev_dtlo, ev_m[31] if i == 0 else None,
                                   dt_done[i - 2] if i >= 2 else None,
                                   pb4_guard[0] if i == 1 else None])
                        ex1 = P.add("scalar", lambda e, i=i, b=bank: e.activation(
                            e1_scr[:], b[:, 0:TCM], AF.Exp, bias=bdt_sb[:, i:i + 1], scale=1.0),
                            after=[mmdt, ev_prev])
                        ex2 = P.add("scalar", lambda e, i=i: e.activation(
                            dt_fm[:, i, :], e1_scr[:], AF.Ln, bias=consts_sb[:, 2:3], scale=1.0), after=[ex1])
                        ev_prev = ex2
                        dt_done[i] = ex2
                        udt_done[i] = P.add("vector", lambda e, i=i: e.tensor_tensor(
                            udt_fm[:, i, :], dt_fm[:, i, :], xc_bf[:, i, :], ALU.mult), after=[ex2])

                    fin = [None] * ID
                    tts_hist = {}
                    prev_f1 = None
                    hsave_prev = None
                    prev_reda = None
                    pend = None

                    def emit_f1(iq, yf):
                        nonlocal prev_f1
                        f1 = P.add("vector", lambda e, i=iq: e.scalar_tensor_tensor(
                            pr_s[:], xc_bf[:, i, :], D_sb[:, i:i + 1], y_s[:],
                            ALU.mult, ALU.add), after=[yf, fin[iq - 1] if iq > 0 else None])
                        prev_f1 = f1
                        return f1

                    def emit_fin(iq, f1):
                        fin[iq] = P.add("vector", lambda e, i=iq: e.tensor_tensor(
                            ytot_bf[:, i, :], pr_s[:], zsil[:, i, :], ALU.mult),
                            after=[f1, zs_done[iq]]
                            + ([wout_guard[0]] if iq == 0 else []))
                    for i in range(ID):
                        uall = P.add("vector", lambda e, i=i: e.tensor_tensor(
                            U_all[:, 0:DS, :], Bbc[:, 0:DS, :],
                            udt_fm[:, i, :].unsqueeze(1).broadcast_to([128, DS, TCM]),
                            ALU.mult),
                            after=[udt_done[i]] + ([bc_ops[2 * n] for n in range(DS)]
                                                   if i == 0 else []))
                        for n in range(DS):
                            idx = i * DS + n
                            ex = P.add("scalar", lambda e, i=i, n=n, sl=idx % 2: e.activation(
                                dA_s[sl][:], dt_fm[:, i, :], AF.Exp, bias=0.0,
                                scale=A_sb[:, i * DS + n:i * DS + n + 1]),
                                after=[dt_done[i], tts_hist.get(idx - 2)])
                            tts = P.add("vector", lambda e, i=i, n=n, idx=idx: e.tensor_tensor_scan(
                                h_i[:, n, :], dA_s[idx % 2][:],
                                U_all[:, n, :],
                                hstate[:, i * DS + n:i * DS + n + 1], ALU.mult, ALU.add),
                                after=[ex, uall, hsave_prev if n == 0 else None])
                            tts_hist[idx] = tts
                        pf1 = emit_f1(*pend) if pend is not None else None
                        hsave_prev = P.add("vector", lambda e, i=i: e.tensor_copy(
                            hstate[:, i * DS:(i + 1) * DS],
                            h_i[:, 0:DS, TCM - 1:TCM].squeeze(2)),
                            after=[tts_hist[i * DS + DS - 1]])
                        prodall = P.add("vector", lambda e, i=i: e.tensor_tensor(
                            prod_s[:, 0:DS, :], h_i[:, 0:DS, :], Cbc[:, 0:DS, :], ALU.mult),
                            after=[tts_hist[i * DS + DS - 1], prev_reda]
                            + ([bc_ops[2 * n + 1] for n in range(DS)] if i == 0 else []))
                        reda = P.add("gpsimd", lambda e: e.tensor_tensor(
                            red_a[:, 0:8, :].rearrange("p a b -> p (a b)"),
                            prod_s[:, 0:8, :].rearrange("p a b -> p (a b)"),
                            prod_s[:, 8:16, :].rearrange("p a b -> p (a b)"), ALU.add),
                            after=[prodall])
                        prev_reda = reda
                        redb = P.add("gpsimd", lambda e: e.tensor_tensor(
                            red_b[:, 0:4, :].rearrange("p a b -> p (a b)"),
                            red_a[:, 0:4, :].rearrange("p a b -> p (a b)"),
                            red_a[:, 4:8, :].rearrange("p a b -> p (a b)"), ALU.add),
                            after=[reda])
                        redc = P.add("gpsimd", lambda e: e.tensor_tensor(
                            red_c[:, 0:2, :].rearrange("p a b -> p (a b)"),
                            red_b[:, 0:2, :].rearrange("p a b -> p (a b)"),
                            red_b[:, 2:4, :].rearrange("p a b -> p (a b)"), ALU.add),
                            after=[redb])
                        if pend is not None:
                            emit_fin(pend[0], pf1)
                            pend = None
                        yfin = P.add("gpsimd", lambda e: e.tensor_tensor(
                            y_s[:], red_c[:, 0, :], red_c[:, 1, :], ALU.add),
                            after=[redc, prev_f1])
                        pend = (i, yfin)
                    pf1 = emit_f1(*pend)
                    emit_fin(pend[0], pf1)
                    pend = None

                    pend_w[0] = (list(fin), dt_done[ID - 1], s, t0)
                emit_wout()
            P.barrier()
        mres.close()

        with ExitStack() as pf:
            fsb = lambda n, sh, dt=F32: pf.enter_context(nc.sbuf_tensor(n, sh, dt))
            fxn_tm = fsb("f_xn", [128, 4, DM])
            x1_tm = fsb("f_x1", [128, 4, DM])
            out_tm = fsb("f_out", [128, 4, DM])
            h_scr = fsb("f_hs", [128, DM])
            hsB = fsb("f_hsB", [128, DM])
            hs2 = [h_scr, hsB]
            pbj = [(pb[2], pb[3]), (pb[4], pb[5])]
            fsq_scr = fsb("f_sq", [128, DM])
            fstat = fsb("f_stat", [128, 80])
            h1_fm = fsb("f_h1fm", [128, KD, TCF], BF16)
            h2_fm = fsb("f_h2fm", [128, KD, TCF], BF16)
            h2e_fm = [fsb(f"f_h2e{i}", [128, KD, TCF], BF16) for i in range(2)]
            q2ARR = fsb("f_q2", [128, H, 256], BF16)
            k2ARR = fsb("f_k2", [128, H, 256], BF16)
            qb1_scr = fsb("f_qb1", [64, H, 64], BF16)
            kb1_scr = fsb("f_kb1", [64, H, 64], BF16)
            vg_sb = fsb("f_vg", [16, 64, 128], BF16)
            r1_bf = fsb("f_rbf", [65, TCF], BF16)
            attn_b = fsb("f_attnb", [128, 512], BF16)
            attT_sb = fsb("f_attT", [16, 4096], BF16)
            out2slabs = fsb("f_o2sl", [64, H, TCF], BF16)
            aproj_sb = [fsb(f"f_ap{i}", [128, TCF]) for i in range(2)]
            g_tm = fsb("f_gtm", [128, 4 * E])
            gn_tm = fsb("f_gntm", [128, 4 * E])
            gn_bf = fsb("f_gnbf", [8, TCF], BF16)
            gbc = [fsb(f"f_gbc{i}", [128, TCF], BF16) for i in range(2)]
            moe_sb = [fsb(f"f_moe{i}", [128, TCF]) for i in range(2)]
            webuf = [fsb(f"f_we{i}", [128, KD, 512], BF16) for i in range(2)]
            wo2_sb = fsb("f_wo2", [64, H * 512], BF16)

            def wq_slice(k, h):
                base = (k % 4) * 2
                off = h * HD
                return webuf[k // 4][:, base + off // 512, off % 512:off % 512 + HD]

            def wo_load(half, war):
                return [P.dma(wo2_sb[:, h * 512:(h + 1) * 512],
                              wo2_scr[h, :, half * 512:(half + 1) * 512], after=war)
                        for h in range(H)]

            def rms_to_fm(src_tm, dst_fm, scol, eps, dep0):
                evs = []
                sqs, sds, ivr, hhs = {}, {}, {}, {}
                prev_sq = None
                for j in range(4):
                    sqs[j] = P.add("scalar", lambda e, j=j: e.activation(
                        fsq_scr[:], src_tm[:, j, :], AF.Square, bias=0.0, scale=1.0,
                        accum_out=fstat[:, scol + j:scol + j + 1]), after=dep0 + [prev_sq])
                    prev_sq = sqs[j]
                    sds[j] = P.add("scalar", lambda e, j=j: e.activation(
                        fstat[:, scol + 8 + j:scol + 9 + j], fstat[:, scol + j:scol + j + 1],
                        AF.Sqrt, bias=eps, scale=1.0 / DM), after=[sqs[j]])
                for j in range(4):
                    ivr[j] = P.add("vector", lambda e, j=j: e.reciprocal(
                        fstat[:, scol + 16 + j:scol + 17 + j],
                        fstat[:, scol + 8 + j:scol + 9 + j]), after=[sds[j]])
                for j in range(4):
                    war = [evs[2 * (j - 2)], evs[2 * (j - 2) + 1]] if j >= 2 else []
                    hhs[j] = P.add("vector", lambda e, j=j: e.tensor_scalar(
                        hs2[j % 2][:], src_tm[:, j, :], fstat[:, scol + 16 + j:scol + 17 + j],
                        None, ALU.mult, ALU.bypass), after=[ivr[j]] + war)
                    tts_ = []
                    for k in range(KD):
                        bnk = pbj[j % 2][k // 4]
                        tts_.append(P.add("tensor", lambda e, j=j, k=k, b=bnk: e.transpose(
                            b[:, (k % 4) * 128:(k % 4 + 1) * 128],
                            hs2[j % 2][:, k * 128:(k + 1) * 128],
                            eye32[:]), after=[hhs[j]] + war))
                    e1 = P.add("scalar", lambda e, j=j: e.activation(
                        dst_fm[:, 0:4, j * 128:(j + 1) * 128], pbj[j % 2][0][:, 0:512].rearrange(
                            "p (a b) -> p a b", a=4), AF.Copy, bias=0.0, scale=1.0),
                        after=tts_[0:4])
                    e2 = P.add("scalar", lambda e, j=j: e.activation(
                        dst_fm[:, 4:8, j * 128:(j + 1) * 128], pbj[j % 2][1][:, 0:512].rearrange(
                            "p (a b) -> p a b", a=4), AF.Copy, bias=0.0, scale=1.0),
                        after=tts_[4:8])
                    evs += [e1, e2]
                return evs

            NCF = T // TCF
            for s in range(NSEQ):
                for ci in range(NCF):
                    P.barrier()
                    t0 = ci * TCF
                    prev_wo_free = None
                    prev_r_user = None
                    wql = [P.dma(webuf[k // 4][:, (k % 4) * 2:(k % 4) * 2 + 2, :]
                                 .rearrange("p a b -> p (a b)"), wq_scr[k])
                           for k in range(KD)]
                    wolA = wo_load(0, [])
                    ld_xn = P.dma(fxn_tm[:], xn_scr[s, t0:t0 + TCF, :].rearrange("(a p) d -> p a d", p=128))
                    ld_s1 = P.dma(out_tm[:], s1_scr[s, t0:t0 + TCF, :].rearrange("(a p) d -> p a d", p=128))

                    h1_evs = rms_to_fm(fxn_tm, h1_fm, 0, consts_sb[:, 1:2], [ld_xn])

                    mm = None
                    for k in range(KD):
                        mm = P.add("tensor", lambda e, k=k: e.matmul(
                            pb[0][0:64, 0:TCF], wdown_sb[:, k * LR:(k + 1) * LR], h1_fm[:, k, :],
                            start=(k == 0), stop=(k == KD - 1)),
                            after=[h1_evs[-1], prev_wo_free])
                    ev_r = P.add("scalar", lambda e: e.activation(
                        r1_bf[0:64, :], pb[0][0:64, 0:TCF], AF.Identity, bias=bdown_sb[:], scale=1.0),
                        after=[mm, prev_r_user])
                    ones_r = P.add("vector", lambda e: e.memset(r1_bf[64:65, :], 1.0),
                                   after=[prev_r_user])

                    ev2prev = [ev_r, prev_wo_free]
                    q2_ev = []
                    for h in range(H):
                        bank = pb[h % 2]
                        mm = None
                        for b in range(2):
                            for k in range(KD):
                                mm = P.add("tensor", lambda e, h=h, k=k, b=b, bk=bank: e.matmul(
                                    bk[64 * b:64 * b + 64, 0:256],
                                    wq_slice(k, h),
                                    h1_fm[:, k, :].rearrange("p (t two) -> p two t", two=2)[:, b, :],
                                    start=(k == 0), stop=(k == KD - 1),
                                    tile_position=(0, 64 * b)),
                                    after=[h1_evs[-1], ev2prev[h % 2]] + (wql if h == 0 else []))
                        ev = P.add("scalar", lambda e, h=h, bk=bank: e.activation(
                            q2ARR[:, h, :], bk[:, 0:256], AF.Identity, bias=bq2_sb[:, h:h + 1],
                            scale=1.0), after=[mm])
                        ev2prev[h % 2] = ev
                        q2_ev.append(ev)
                    k2_ev = []
                    for g in range(H):
                        bank = pb[g % 2]
                        mm = None
                        for b in range(2):
                            mm = P.add("tensor", lambda e, g=g, b=b, bk=bank: e.matmul(
                                bk[64 * b:64 * b + 64, 0:256],
                                wupk_sb[:, g * HD:(g + 1) * HD],
                                r1_bf[0:64, :].rearrange("p (t two) -> p two t", two=2)[:, b, :],
                                start=True, stop=True,
                                tile_position=(0, 64 * b)),
                                after=[ev_r, ev2prev[g % 2]])
                        ev = P.add("scalar", lambda e, g=g, bk=bank: e.activation(
                            k2ARR[:, g, :], bk[:, 0:256], AF.Identity, bias=bupk2_sb[:, g:g + 1],
                            scale=1.0), after=[mm])
                        ev2prev[g % 2] = ev
                        k2_ev.append(ev)

                    slab_evs = []
                    prev_vg_user = None
                    prev_qk_free = [k2_ev[-2], k2_ev[-1]]
                    prev_av_free = [None, None]
                    prev_attn_user = None
                    prev_attT_user = None
                    prev_b1_user = None
                    for bi in range(4):
                        vg_evs = []
                        for hq in range(16):
                            bank = pb[hq % 2]
                            mms = []
                            for hh in range(4):
                                hd = hq * 4 + hh
                                mms.append(P.add("tensor", lambda e, hd=hd, hh=hh, bk=bank, bi=bi: e.matmul(
                                    bk[0:16, hh * 128:(hh + 1) * 128],
                                    wupv3_sb[:, hd * H:(hd + 1) * H],
                                    r1_bf[:, bi * 128:(bi + 1) * 128],
                                    start=True, stop=True),
                                    after=[ev_r, ones_r, ev2prev[hq % 2] if hq < 2 else None,
                                           vg_evs[hq - 2] if hq >= 2 else None,
                                           prev_vg_user if hq < 2 else None]))
                            vg_evs.append(P.add("vector", lambda e, hq=hq, bk=bank: e.tensor_copy(
                                vg_sb[:, hq * 4:hq * 4 + 4, :].rearrange("p a b -> p (a b)"),
                                bk[0:16, 0:512]), after=mms))
                        qbank = pb[2 + (bi % 2)]
                        cpq = P.add("vector", lambda e, bi=bi: e.tensor_copy(
                            qb1_scr[:], q2ARR[64:128, 0:H, bi * 64:(bi + 1) * 64]),
                            after=[q2_ev[-1], prev_b1_user])
                        cpk = P.add("vector", lambda e, bi=bi: e.tensor_copy(
                            kb1_scr[:], k2ARR[64:128, 0:H, bi * 64:(bi + 1) * 64]),
                            after=[k2_ev[-1], prev_b1_user])
                        qk_last = None
                        for q in range(64):
                            pidx = bi * 64 + q
                            j, slot = q % 4, q // 4
                            for b in range(2):
                                if b == 0:
                                    sq_ = q2ARR[0:64, 0:H, pidx:pidx + 1].rearrange("p a b -> p (a b)")
                                    sk_ = k2ARR[0:64, 0:H, pidx:pidx + 1].rearrange("p a b -> p (a b)")
                                else:
                                    sq_ = qb1_scr[:, 0:H, q:q + 1].rearrange("p a b -> p (a b)")
                                    sk_ = kb1_scr[:, 0:H, q:q + 1].rearrange("p a b -> p (a b)")
                                qk_last = P.add("tensor", lambda e, sq_=sq_, sk_=sk_, j=j, slot=slot, b=b, bk=qbank: e.matmul(
                                    bk[32 * j:32 * j + 16, slot * 32 + b * 16:slot * 32 + b * 16 + 16],
                                    sq_, sk_,
                                    start=True, stop=True,
                                    tile_position=(0, 32 * j), skip_group_check=True),
                                    after=[prev_qk_free[bi % 2], q2_ev[-1], k2_ev[-1], cpq, cpk] if q == 0 and b == 0 else [])
                        prev_b1_user = qk_last
                        nm = P.add("vector", lambda e, bk=qbank: e.tensor_reduce(
                            fstat[:, 0:32], bk[:, 0:512].rearrange("p (a b) -> p a b", a=32),
                            AX.X, ALU.max, negate=True), after=[qk_last])
                        sub = P.add("vector", lambda e, bk=qbank: e.tensor_tensor(
                            h_scr[:, 0:512].rearrange("p (a b) -> p a b", a=32),
                            bk[:, 0:512].rearrange("p (a b) -> p a b", a=32),
                            fstat[:, 0:32].unsqueeze(2).broadcast_to([128, 32, 16]), ALU.add),
                            after=[nm, h1_evs[-1]])
                        exp_ = P.add("scalar", lambda e: e.activation(
                            fsq_scr[:, 0:512], h_scr[:, 0:512], AF.Exp, bias=0.0, scale=1.0),
                            after=[sub])
                        smr = P.add("vector", lambda e: e.tensor_reduce(
                            fstat[:, 32:64], fsq_scr[:, 0:512].rearrange("p (a b) -> p a b", a=32),
                            AX.X, ALU.add, negate=False), after=[exp_])
                        rc = P.add("vector", lambda e: e.reciprocal(
                            fstat[:, 0:32], fstat[:, 32:64]), after=[smr])
                        nrm = P.add("vector", lambda e: e.tensor_tensor(
                            attn_b[:].rearrange("p (a b) -> p a b", a=32),
                            fsq_scr[:, 0:512].rearrange("p (a b) -> p a b", a=32),
                            fstat[:, 0:32].unsqueeze(2).broadcast_to([128, 32, 16]), ALU.mult),
                            after=[rc, prev_attn_user])
                        at_evs = []
                        tts_ = []
                        for gi in range(32):
                            slot, b = gi // 2, gi % 2
                            tts_.append(P.add("tensor", lambda e, gi=gi, slot=slot, b=b: e.transpose(
                                pbBF[0:16, (gi % 8) * 128:(gi % 8 + 1) * 128],
                                attn_b[:, slot * 32 + b * 16:slot * 32 + b * 16 + 16], eyebf[:]),
                                after=[nrm, at_evs[-1] if gi % 8 == 0 and at_evs else None,
                                       prev_attT_user if gi == 0 else None]))
                            if gi % 8 == 7:
                                at_evs.append(P.add("vector", lambda e, gi=gi: e.tensor_copy(
                                    attT_sb[:, (gi // 8) * 1024:(gi // 8 + 1) * 1024],
                                    pbBF[0:16, 0:1024]), after=tts_[-8:]))
                        for a in range(4):
                            abank = pb[4 + (a % 2)]
                            av_last = None
                            for ti in range(32):
                                tt = a * 32 + ti
                                t = bi * 128 + tt
                                qq = t // 2
                                j, slot, b = qq % 4, (qq // 4) % 16, t % 2
                                gi = slot * 2 + b
                                av_last = P.add("tensor", lambda e, tt=tt, ti=ti, j=j, gi=gi, bk=abank: e.matmul(
                                    bk[0:64, ti * 16:(ti + 1) * 16],
                                    vg_sb[:, 0:64, tt:tt + 1].rearrange("p a b -> p (a b)"),
                                    attT_sb[:, gi * 128 + 32 * j:gi * 128 + 32 * j + 16],
                                    start=True, stop=True),
                                    after=[at_evs[-1], vg_evs[-1], prev_av_free[a % 2],
                                           h1_evs[-1]] if ti == 0 else [])
                            sev = P.add("vector", lambda e, bi=bi, a=a, bk=abank: e.tensor_copy(
                                out2slabs[:, :, bi * 128 + a * 32:bi * 128 + (a + 1) * 32].transpose([0, 2, 1]),
                                bk[0:64, 0:512].rearrange("p (a b) -> p a b", a=32)),
                                after=[av_last])
                            prev_av_free[a % 2] = sev
                            slab_evs.append(sev)
                        prev_qk_free[bi % 2] = sub
                        prev_attn_user = tts_[-1]
                        prev_attT_user = av_last
                        prev_vg_user = av_last
                        if DEBUG_TAPS and s == 0 and ci == 0 and bi == 0:
                            prev_attn_user = P.dma(dbg_h["attn"], attn_b[:], after=[tts_[-1]])
                            prev_attT_user = P.dma(dbg_h["attT"], attT_sb[:], after=[av_last])
                            prev_vg_user = P.dma(dbg_h["vg"], vg_sb[:], after=[av_last])
                    prev_r_user = av_last

                    if DEBUG_TAPS and s == 0 and ci == 0:
                        P.dma(dbg_h["o2sl"], out2slabs[:], after=slab_evs)
                        P.dma(dbg_h["q2"], q2ARR[:], after=[q2_ev[-1]])
                        P.dma(dbg_h["k2"], k2ARR[:], after=[k2_ev[-1]])
                    x1_ops = []
                    ap_free = [vg_evs[-2], vg_evs[-1]]
                    prev_tt2 = [None, None]
                    wolB = None
                    for mo in range(KD):
                        bank = pb[mo % 2]
                        mm = None
                        for h in range(H):
                            mm = P.add("tensor", lambda e, mo=mo, h=h, b=bank: e.matmul(
                                b[:, 0:TCF],
                                wo2_sb[:, h * 512 + (mo % 4) * 128:h * 512 + (mo % 4) * 128 + 128],
                                out2slabs[:, h, :], start=(h == 0), stop=(h == H - 1)),
                                after=[slab_evs[-1], ap_free[mo % 2]]
                                + (wolA if mo == 0 else [])
                                + (wolB if mo == 4 else []))
                        if mo == 3:
                            wolB = wo_load(1, [mm])
                        ev = P.add("scalar", lambda e, mo=mo, b=bank: e.activation(
                            aproj_sb[mo % 2][:], b[:, 0:TCF], AF.Identity, bias=bo_sb[:, mo:mo + 1],
                            scale=1.0), after=[mm])
                        ap_free[mo % 2] = ev
                        gbank = pbG if mo % 2 == 0 else pb[2]
                        tts_ = []
                        for j in range(4):
                            tts_.append(P.add("tensor", lambda e, mo=mo, j=j, gb=gbank: e.transpose(
                                gb[:, j * 128:(j + 1) * 128],
                                aproj_sb[mo % 2][:, j * 128:(j + 1) * 128], eye32[:]),
                                after=[ev, prev_tt2[mo % 2] if j == 0 else None]))
                        ad = P.add("vector", lambda e, mo=mo, gb=gbank: e.tensor_tensor(
                            x1_tm[:, 0:4, mo * 128:(mo + 1) * 128],
                            gb[:, 0:512].rearrange("p (a b) -> p a b", a=4),
                            fxn_tm[:, 0:4, mo * 128:(mo + 1) * 128], ALU.add), after=tts_)
                        prev_tt2[mo % 2] = ad
                        ap_free[mo % 2] = ad
                        x1_ops.append(ad)

                    if DEBUG_TAPS and s == 0 and ci == 0:
                        P.dma(dbg_h["x1"], x1_tm[:], after=[x1_ops[-1]])
                    h2_evs = rms_to_fm(x1_tm, h2_fm, 24, consts_sb[:, 1:2], [x1_ops[-1]])

                    mm = None
                    for k in range(KD):
                        mm = P.add("tensor", lambda e, k=k: e.matmul(
                            pb[0][0:E, 0:TCF], wg_sb[:, k * E:(k + 1) * E], h2_fm[:, k, :],
                            start=(k == 0), stop=(k == KD - 1)),
                            after=[h2_evs[-1], ap_free[0]])
                    gev = P.add("scalar", lambda e: e.activation(
                        h_scr[0:E, 0:TCF], pb[0][0:E, 0:TCF], AF.Identity, bias=bg_sb[:], scale=1.0),
                        after=[mm, x1_ops[-1]])
                    gt = []
                    for j in range(4):
                        gt.append(P.add("tensor", lambda e, j=j: e.transpose(
                            pbG[:, j * E:(j + 1) * E], h_scr[0:E, j * 128:(j + 1) * 128],
                            eye32[0:E, 0:E]),
                            after=[gev] + ([prev_tt2[0]] if j == 0 else [])))
                    gcp = P.add("vector", lambda e: e.tensor_copy(g_tm[:], pbG[:, 0:4 * E]), after=gt)
                    gsm = []
                    for j in range(4):
                        nm = P.add("vector", lambda e, j=j: e.tensor_reduce(
                            fstat[:, 40 + j:41 + j], g_tm[:, j * E:(j + 1) * E], AX.X, ALU.max,
                            negate=True), after=[gcp])
                        sb_ = P.add("vector", lambda e, j=j: e.tensor_tensor(
                            gn_tm[:, j * E:(j + 1) * E], g_tm[:, j * E:(j + 1) * E],
                            fstat[:, 40 + j:41 + j].broadcast_to([128, E]), ALU.add), after=[nm])
                        ex_ = P.add("scalar", lambda e, j=j: e.activation(
                            g_tm[:, j * E:(j + 1) * E], gn_tm[:, j * E:(j + 1) * E], AF.Exp,
                            bias=0.0, scale=1.0), after=[sb_])
                        sm_ = P.add("vector", lambda e, j=j: e.tensor_reduce(
                            fstat[:, 44 + j:45 + j], g_tm[:, j * E:(j + 1) * E], AX.X, ALU.add,
                            negate=False), after=[ex_])
                        rc_ = P.add("vector", lambda e, j=j: e.reciprocal(
                            fstat[:, 40 + j:41 + j], fstat[:, 44 + j:45 + j]), after=[sm_])
                        mu_ = P.add("vector", lambda e, j=j: e.tensor_scalar(
                            gn_tm[:, j * E:(j + 1) * E], g_tm[:, j * E:(j + 1) * E],
                            fstat[:, 40 + j:41 + j], None, ALU.mult, ALU.bypass), after=[rc_])
                        gsm.append(mu_)
                    gbt = []
                    for j in range(4):
                        gbt.append(P.add("tensor", lambda e, j=j: e.transpose(
                            pbG[0:E, j * 128:(j + 1) * 128], gn_tm[:, j * E:(j + 1) * E], eye32[:]),
                            after=[gsm[j]] + ([gcp] if j == 0 else [])))
                    gnf = P.add("vector", lambda e: e.tensor_copy(gn_bf[:], pbG[0:E, 0:TCF]), after=gbt)

                    wel = {}
                    moe_ads = []
                    for half in range(2):
                        mm_last = None
                        h2e_done = [None, None]
                        mm_of_e = [None] * E
                        for ei in range(E):
                            war = [mm_of_e[ei - 2]] if ei >= 2 else ([h2_evs[-1]] if half == 0 else [moe_ads[-1]])
                            wel[(half, ei)] = P.dma(webuf[ei % 2][:],
                                                    we_scr[ei, half], after=war)
                            gb = P.dma(gbc[ei % 2][:], bcast_view(gn_bf[ei:ei + 1, :], TCF),
                                       after=[gnf] + ([mm_of_e[ei - 2]] if ei >= 2 else []))
                            h2e = P.add("vector", lambda e, ei=ei: e.tensor_tensor(
                                h2e_fm[ei % 2][:], h2_fm[:],
                                gbc[ei % 2][:].unsqueeze(1).broadcast_to([128, KD, TCF]),
                                ALU.mult), after=[gb, h2_evs[-1], mm_of_e[ei - 2] if ei >= 2 else None])
                            mm = None
                            for mo4 in range(4):
                                for k in range(KD):
                                    mm = P.add("tensor", lambda e, ei=ei, mo4=mo4, k=k, half=half: e.matmul(
                                        pb[2 + mo4][:, 0:TCF],
                                        webuf[ei % 2][:, k, mo4 * 128:(mo4 + 1) * 128],
                                        h2e_fm[ei % 2][:, k, :],
                                        start=(ei == 0 and k == 0), stop=(ei == E - 1 and k == KD - 1)),
                                        after=[wel[(half, ei)], h2e,
                                               (h2_evs[-1] if half == 0 else moe_ads[-1]) if ei == 0 and mo4 == 0 and k == 0 else None])
                            mm_of_e[ei] = mm
                            mm_last = mm
                        for mo4 in range(4):
                            mo = half * 4 + mo4
                            ev = P.add("scalar", lambda e, mo4=mo4: e.activation(
                                moe_sb[mo4 % 2][:], pb[2 + mo4][:, 0:TCF], AF.Copy, bias=0.0,
                                scale=1.0), after=[mm_last, moe_ads[-2] if len(moe_ads) >= 2 else None])
                            tts_ = []
                            for j in range(4):
                                tts_.append(P.add("tensor", lambda e, j=j, mo4=mo4: e.transpose(
                                    pbG[:, j * 128:(j + 1) * 128],
                                    moe_sb[mo4 % 2][:, j * 128:(j + 1) * 128], eye32[:]),
                                    after=[ev, moe_ads[-1] if moe_ads else gnf]))
                            if half == 0 and mo4 == 0:
                                sx1 = []
                                for j in range(4):
                                    sx1.append(P.add("vector", lambda e, j=j: e.tensor_tensor(
                                        fxn_tm[:, j, :], out_tm[:, j, :], x1_tm[:, j, :], ALU.add),
                                        after=[ld_s1, x1_ops[-1], h2_evs[-1]]))
                                sx1_done = sx1[-1]
                            ad = P.add("vector", lambda e, mo=mo: e.tensor_tensor(
                                out_tm[:, 0:4, mo * 128:(mo + 1) * 128],
                                pbG[:, 0:512].rearrange("p (a b) -> p a b", a=4),
                                fxn_tm[:, 0:4, mo * 128:(mo + 1) * 128], ALU.add),
                                after=tts_ + [sx1_done])
                            moe_ads.append(ad)
                    st_out = P.dma(out_h[s, t0:t0 + TCF, :].rearrange("(a p) d -> p a d", p=128),
                                   out_tm[:], after=moe_ads)
                    out_stores.append(st_out)

        P.barrier()
        P.add("sync", lambda e: e.nop(), after=out_stores[-1:])

        blk = top.enter_context(nc.Block())
        with nc.allow_non_contiguous_dma(reason="small one-time vector layout loads"):
            P.emit(blk, sems, dsems)
    return nc


_NC_CACHE = None


def _get_nc():
    global _NC_CACHE
    if _NC_CACHE is None:
        _NC_CACHE = build_nc()
    return _NC_CACHE


def host_common(L0_attn_ln_g, L0_attn_ln_b, L0_attn_W_in, L0_attn_conv_w,
                L0_attn_conv_b, L0_attn_W_xp, L0_attn_W_dt, L0_attn_b_dt, L0_attn_A_log,
                L0_attn_D, L0_attn_W_out, L0_ffn_gamma1, L0_ffn_Wq, L0_ffn_bq,
                L0_ffn_Wdown, L0_ffn_bdown, L0_ffn_Wup, L0_ffn_bup, L0_ffn_Wo,
                L0_ffn_bo, L0_ffn_gamma2, L0_ffn_Wg, L0_ffn_bg, L0_ffn_We):
    f32 = lambda a: np.ascontiguousarray(np.asarray(a, dtype=np.float32))
    perm_k = np.array([h * 128 + hd for h in range(H) for hd in range(HD)])
    perm_v = perm_k + HD
    Wup = f32(L0_ffn_Wup)
    bup = f32(L0_ffn_bup)
    common = {
        "winT": f32(np.asarray(L0_attn_W_in).T),
        "wxpT": f32(np.asarray(L0_attn_W_xp).T),
        "wdtT": f32(np.asarray(L0_attn_W_dt).T),
        "woutT": f32(np.asarray(L0_attn_W_out).T),
        "wq": f32(L0_ffn_Wq),
        "wdown": f32(L0_ffn_Wdown),
        "wupk": np.ascontiguousarray(Wup[:, perm_k]),
        "wupv": np.ascontiguousarray(Wup[:, perm_v]),
        "wo": f32(L0_ffn_Wo),
        "wg": f32(L0_ffn_Wg),
        "we": f32(L0_ffn_We),
        "alog": f32(L0_attn_A_log),
        "dvec": f32(L0_attn_D),
        "convw": f32(L0_attn_conv_w),
        "convb": f32(L0_attn_conv_b),
        "bdt": f32(L0_attn_b_dt),
        "lng": f32(L0_attn_ln_g),
        "lnb": f32(L0_attn_ln_b),
        "g1": f32(L0_ffn_gamma1),
        "g2": f32(L0_ffn_gamma2),
        "bq": f32(L0_ffn_bq),
        "bdown": f32(L0_ffn_bdown),
        "bupk": np.ascontiguousarray(bup[perm_k]),
        "bupv": np.ascontiguousarray(bup[perm_v]),
        "bo": f32(L0_ffn_bo),
        "bg": f32(L0_ffn_bg),
        "eye32": np.eye(128, dtype=np.float32),
        "eyebf": np.eye(128, dtype=ml_dtypes.bfloat16),
    }
    return common


def kernel(L0_attn_x, **kw):
    x = np.ascontiguousarray(np.asarray(L0_attn_x, dtype=np.float32))
    common = host_common(**kw)
    in_maps = []
    for c in range(8):
        m = dict(common)
        m["x"] = np.ascontiguousarray(x[2 * c:2 * c + 2])
        in_maps.append(m)
    nc = _get_nc()
    res = run_bass_kernel_spmd(nc, in_maps, core_ids=list(range(8)))
    return np.concatenate([res.results[c]["out"] for c in range(8)], axis=0)


# version 17 (final)
# speedup vs baseline: 1.5770x; 1.0031x over previous
import sys
sys.path.insert(0, '/opt/trn_rl_repo')
from contextlib import ExitStack
import numpy as np
import ml_dtypes
import concourse.bass as bass
import concourse.mybir as mybir
from concourse.bass_utils import run_bass_kernel_spmd

F32 = mybir.dt.float32
BF16 = mybir.dt.bfloat16
AF = mybir.ActivationFunctionType
ALU = mybir.AluOpType
AX = mybir.AxisListType

DM, DI, DS, DC, DTR = 1024, 2048, 16, 4, 64
T, NSEQ = 2048, 2
H, HD, LR, E = 16, 64, 64, 8
TCM, TCF = 256, 512
KD = DM // 128
ID = DI // 128

DEBUG_TAPS = False
TRUNC = None
STRICT_WAITS = False
ELIDE_ENGS = ()


class Prog:
    ENGS = ("sync", "tensor", "vector", "scalar", "gpsimd")
    NDMA = 32

    def __init__(self):
        self.ops = []
        self._barrier = {}
        self._dma_rr = 0
        self._dma_cum = [0] * self.NDMA

    def add(self, eng, fn, after=(), is_dma=False):
        deps = [d for d in after if d is not None]
        deps += [b for b in self._barrier.values() if b is not None]
        o = {"eng": eng, "fn": fn, "after": deps, "id": len(self.ops),
             "is_dma": is_dma, "inc": False, "cum": None, "dsem": None, "dcum": None}
        if is_dma:
            s = self._dma_rr % self.NDMA
            self._dma_rr += 1
            self._dma_cum[s] += 16
            o["dsem"] = s
            o["dcum"] = self._dma_cum[s]
        for d in deps:
            assert d["id"] < o["id"]
        self.ops.append(o)
        return o

    def dma(self, out, in_, after=()):
        return self.add("sync", lambda e, o=out, i=in_: e.dma_start(out=o, in_=i),
                        after=after, is_dma=True)

    def barrier(self):
        last = {}
        for o in self.ops:
            last[o["eng"]] = o
        self._barrier = last

    def emit(self, block, sems, dsems):
        if TRUNC is not None:
            self.ops = self.ops[:TRUNC]
            keep = set(id(o) for o in self.ops)
            for o in self.ops:
                o["after"] = [d for d in o["after"] if id(d) in keep]
        for o in self.ops:
            for d in o["after"]:
                if (not STRICT_WAITS and not d["is_dma"] and d["eng"] == o["eng"]
                        and o["eng"] in ELIDE_ENGS):
                    continue
                d["inc"] = True
        cum = {e: 0 for e in self.ENGS}
        for o in self.ops:
            if o["inc"] and not o["is_dma"]:
                cum[o["eng"]] += 1
                o["cum"] = cum[o["eng"]]

        def stream(eng_name, eng):
            waited = {e: 0 for e in self.ENGS}
            dwaited = [0] * self.NDMA
            for o in self.ops:
                if o["eng"] != eng_name:
                    continue
                for d in o["after"]:
                    if d["is_dma"]:
                        if d["dcum"] > dwaited[d["dsem"]]:
                            eng.wait_ge(dsems[d["dsem"]], d["dcum"])
                            dwaited[d["dsem"]] = d["dcum"]
                    else:
                        if (not STRICT_WAITS and d["eng"] == eng_name
                                and eng_name in ELIDE_ENGS):
                            continue
                        if d["cum"] is not None and d["cum"] > waited[d["eng"]]:
                            eng.wait_ge(sems[d["eng"]], d["cum"])
                            waited[d["eng"]] = d["cum"]
                if o["is_dma"]:
                    if o["dcum"] > 16 and o["dcum"] - 16 > dwaited[o["dsem"]]:
                        eng.wait_ge(dsems[o["dsem"]], o["dcum"] - 16)
                        dwaited[o["dsem"]] = o["dcum"] - 16
                    instr = o["fn"](eng)
                    instr.then_inc(dsems[o["dsem"]], 16)
                else:
                    instr = o["fn"](eng)
                    if o["inc"]:
                        instr.then_inc(sems[eng_name], 1)

        @block.sync
        def _(e):
            stream("sync", e)

        @block.tensor
        def _(e):
            stream("tensor", e)

        @block.vector
        def _(e):
            stream("vector", e)

        @block.scalar
        def _(e):
            stream("scalar", e)

        @block.gpsimd
        def _(e):
            stream("gpsimd", e)


def bcast_view(row_ap, width):
    return row_ap.unsqueeze(1).broadcast_to([1, 128, width])


def build_nc():
    nc = bass.Bass("TRN2", target_bir_lowering=False, debug=False,
                   detect_race_conditions=STRICT_WAITS)
    P = Prog()

    din = lambda n, sh, dt=F32: nc.dram_tensor(n, sh, dt, kind="ExternalInput").ap()
    x_h = din("x", [NSEQ, T, DM])
    winT_h = din("winT", [DM, 2 * DI])
    wxpT_h = din("wxpT", [DI, 96])
    wdtT_h = din("wdtT", [DTR, DI])
    woutT_h = din("woutT", [DI, DM])
    wq_h = din("wq", [DM, DM])
    wdown_h = din("wdown", [DM, LR])
    wupk_h = din("wupk", [LR, DM])
    wupv_h = din("wupv", [LR, DM])
    wo_h = din("wo", [DM, DM])
    wg_h = din("wg", [DM, E])
    we_h = din("we", [E, DM, DM])
    alog_h = din("alog", [DI, DS])
    dvec_h = din("dvec", [DI])
    convw_h = din("convw", [DI, DC])
    convb_h = din("convb", [DI])
    bdt_h = din("bdt", [DI])
    lng_h = din("lng", [DM])
    lnb_h = din("lnb", [DM])
    g1_h = din("g1", [DM])
    g2_h = din("g2", [DM])
    bq_h = din("bq", [DM])
    bdown_h = din("bdown", [LR])
    bupk_h = din("bupk", [DM])
    bupv_h = din("bupv", [DM])
    bo_h = din("bo", [DM])
    bg_h = din("bg", [E])
    eye32_h = din("eye32", [128, 128])
    eyebf_h = din("eyebf", [128, 128], BF16)

    out_h = nc.dram_tensor("out", [NSEQ, T, DM], F32, kind="ExternalOutput").ap()
    skind = "ExternalOutput" if DEBUG_TAPS else "Internal"
    xn_scr = nc.dram_tensor("xn_scr", [NSEQ, T, DM], F32, kind=skind).ap()
    s1_scr = nc.dram_tensor("s1_scr", [NSEQ, T, DM], F32, kind=skind).ap()
    dbg_h = {}
    if DEBUG_TAPS:
        dbg_h["o2sl"] = nc.dram_tensor("dbg_o2sl", [64, H, TCF], BF16, kind="ExternalOutput").ap()
        dbg_h["x1"] = nc.dram_tensor("dbg_x1", [128, 4, DM], F32, kind="ExternalOutput").ap()
        dbg_h["q2"] = nc.dram_tensor("dbg_q2", [128, H, 256], BF16, kind="ExternalOutput").ap()
        dbg_h["k2"] = nc.dram_tensor("dbg_k2", [128, H, 256], BF16, kind="ExternalOutput").ap()
        dbg_h["attn"] = nc.dram_tensor("dbg_attn", [128, 512], BF16, kind="ExternalOutput").ap()
        dbg_h["vg"] = nc.dram_tensor("dbg_vg", [16, 64, 128], BF16, kind="ExternalOutput").ap()
        dbg_h["attT"] = nc.dram_tensor("dbg_attT", [16, 4096], BF16, kind="ExternalOutput").ap()
    winT_scr = nc.dram_tensor("winT_b", [KD, 128, 2 * DI], BF16, kind="Internal").ap()
    wq_scr = nc.dram_tensor("wq_b", [KD, 128, DM], BF16, kind="Internal").ap()
    wo2_scr = nc.dram_tensor("wo2_b", [H, 64, DM], BF16, kind="Internal").ap()
    we_scr = nc.dram_tensor("we_b", [E, 2, 128, KD, 512], BF16, kind="Internal").ap()

    out_stores = []

    with ExitStack() as top:
        sems = {e: top.enter_context(nc.semaphore("sem_" + e)) for e in Prog.ENGS}
        dsems = [top.enter_context(nc.semaphore(f"dma{i}")) for i in range(Prog.NDMA)]
        sb = lambda n, sh, dt=F32: top.enter_context(nc.sbuf_tensor(n, sh, dt))

        eye32 = sb("s_eye32", [128, 128])
        eyebf = sb("s_eyebf", [128, 128], BF16)
        consts_sb = sb("consts_sb", [128, 4])
        cxz_sb = sb("cxz_sb", [128, 32])
        bq2_sb = sb("bq2_sb", [128, H])
        bdown_sb = sb("bdown_sb", [64, 1])
        bupk2_sb = sb("bupk2_sb", [128, H])
        wupv3_sb = sb("wupv3_sb", [65, 64 * H], BF16)
        bo_sb = sb("bo_sb", [128, KD])
        bg_sb = sb("bg_sb", [E, 1])
        wdown_sb = sb("wdown_sb", [128, KD * LR], BF16)
        wupk_sb = sb("wupk_sb", [64, DM], BF16)
        wg_sb = sb("wg_sb", [128, KD * E], BF16)

        pb = [top.enter_context(nc.psum_tensor(f"pb{i}", [128, 512], F32)) for i in range(6)]
        pbBF = top.enter_context(nc.psum_tensor("pbBF", [128, 1024], BF16))
        pbG = top.enter_context(nc.psum_tensor("pbG", [128, 512], F32))

        mres = ExitStack()
        msb0 = lambda n, sh, dt=F32: mres.enter_context(nc.sbuf_tensor(n, sh, dt))
        A_sb = msb0("A_sb", [128, ID * DS])
        D_sb = msb0("D_sb", [128, ID])
        convw_sb = msb0("convw_sb", [128, ID * DC])
        convb_sb = msb0("convb_sb", [128, ID])
        bdt_sb = msb0("bdt_sb", [128, ID])
        wxpT_sb = msb0("wxpT_sb", [128, ID * 96], BF16)
        wdtT_sb = msb0("wdtT_sb", [64, DI], BF16)
        woutT_sb = msb0("woutT_sb", [128, ID * DM], BF16)

        with ExitStack() as prep:
            psb = lambda n, sh, dt=F32: prep.enter_context(nc.sbuf_tensor(n, sh, dt))
            slab = psb("p_slab", [128, DM])
            slab_bf = psb("p_slab_bf", [128, DM], BF16)
            vcol = psb("p_vcol", [128, 16])
            lnb_bf = psb("p_lnb_bf", [128, KD], BF16)
            g18_sb = psb("p_g18", [128, KD])
            g1c_sb = psb("p_g1c", [128, KD])
            g2c_sb = psb("p_g2c", [128, KD])
            lng_c = psb("p_lng", [128, KD])

            P.add("vector", lambda e: e.memset(consts_sb[:, 0:1], 1e-5))
            P.add("vector", lambda e: e.memset(consts_sb[:, 1:2], 1e-6))
            P.add("vector", lambda e: e.memset(consts_sb[:, 2:3], 1.0))
            P.dma(eye32[:], eye32_h)
            P.dma(eyebf[:], eyebf_h)
            ld_a = P.dma(A_sb[:], alog_h.rearrange("(i p) n -> p i n", p=128))
            e_a = P.add("scalar", lambda e: e.activation(A_sb[:], A_sb[:], AF.Exp, bias=0.0, scale=1.0), after=[ld_a])
            P.add("vector", lambda e: e.tensor_scalar(A_sb[:], A_sb[:], -1.0, None, ALU.mult, ALU.bypass), after=[e_a])
            P.dma(D_sb[:], dvec_h.rearrange("(i p) -> p i", p=128))
            P.dma(convw_sb[:], convw_h.rearrange("(i p) c -> p i c", p=128))
            P.dma(convb_sb[:], convb_h.rearrange("(i p) -> p i", p=128))
            P.dma(bdt_sb[:], bdt_h.rearrange("(i p) -> p i", p=128))
            ld_g1 = P.dma(g1c_sb[:], g1_h.rearrange("(k p) -> p k", p=128))
            ld_g2 = P.dma(g2c_sb[:], g2_h.rearrange("(k p) -> p k", p=128))
            ld_lng = P.dma(lng_c[:], lng_h.rearrange("(k p) -> p k", p=128))
            ld_lnb = P.dma(vcol[:, 0:KD], lnb_h.rearrange("(k p) -> p k", p=128))
            cv_lnb = P.add("vector", lambda e: e.tensor_copy(lnb_bf[:], vcol[:, 0:KD]), after=[ld_lnb])
            lb1 = P.dma(bq2_sb[0:64, :], bq_h.rearrange("(h d) -> d h", d=HD))
            lb2 = P.dma(bq2_sb[64:128, :], bq_h.rearrange("(h d) -> d h", d=HD))
            P.add("vector", lambda e: e.tensor_scalar(bq2_sb[:], bq2_sb[:], 0.125, None, ALU.mult, ALU.bypass), after=[lb1, lb2])
            P.dma(bdown_sb[:], bdown_h.rearrange("(a p) -> p a", p=64))
            P.dma(bupk2_sb[0:64, :], bupk_h.rearrange("(h d) -> d h", d=HD))
            P.dma(bupk2_sb[64:128, :], bupk_h.rearrange("(h d) -> d h", d=HD))
            P.dma(bo_sb[:], bo_h.rearrange("(k p) -> p k", p=128))
            P.dma(bg_sb[:], bg_h.rearrange("(a p) -> p a", p=E))
            g18 = P.add("vector", lambda e: e.tensor_scalar(g18_sb[:], g1c_sb[:], 0.125, None, ALU.mult, ALU.bypass), after=[ld_g1])

            prev_st = None
            cmm = None
            for k in range(KD):
                for hh in range(4):
                    ldw = P.dma(slab[:], winT_h[k * 128:(k + 1) * 128, hh * 1024:(hh + 1) * 1024],
                                after=[prev_st])
                    cst = P.add("vector", lambda e: e.tensor_copy(slab_bf[:], slab[:]), after=[ldw, cmm])
                    for m8 in range(8):
                        m = hh * 8 + m8
                        cmm = P.add("tensor", lambda e, m=m, m8=m8, k=k, hh=hh: e.matmul(
                            pbG[:, m:m + 1], slab_bf[:, m8 * 128:(m8 + 1) * 128],
                            lnb_bf[:, k:k + 1],
                            start=(k == 0 and hh == 0 and m8 == 0),
                            stop=(k == KD - 1 and hh == 3 and m8 == 7)),
                            after=[cst, cv_lnb])
                    scl = P.add("scalar", lambda e, k=k: e.activation(
                        slab_bf[:], slab[:], AF.Copy, bias=0.0, scale=lng_c[:, k:k + 1]),
                        after=[cst, cmm, ld_lng])
                    prev_st = P.dma(winT_scr[k, :, hh * 1024:(hh + 1) * 1024], slab_bf[:], after=[scl])
            P.add("vector", lambda e: e.tensor_copy(cxz_sb[:], pbG[:, 0:32]), after=[cmm])

            for k in range(KD):
                ldw = P.dma(slab[:], wq_h[k * 128:(k + 1) * 128, :], after=[prev_st])
                scl = P.add("scalar", lambda e, k=k: e.activation(
                    slab_bf[:], slab[:], AF.Copy, bias=0.0, scale=g18_sb[:, k:k + 1]),
                    after=[ldw, g18])
                prev_st = P.dma(wq_scr[k], slab_bf[:], after=[scl])
            for k in range(KD):
                ldw = P.dma(slab[:], wo_h[k * 128:(k + 1) * 128, :], after=[prev_st])
                scl = P.add("scalar", lambda e: e.activation(
                    slab_bf[:], slab[:], AF.Copy, bias=0.0, scale=1.0), after=[ldw])
                st1 = P.dma(wo2_scr[2 * k], slab_bf[0:64, :], after=[scl])
                prev_st = P.dma(wo2_scr[2 * k + 1], slab_bf[64:128, :], after=[scl, st1])
            for ei in range(E):
                for k in range(KD):
                    ldw = P.dma(slab[:], we_h[ei, k * 128:(k + 1) * 128, :], after=[prev_st])
                    scl = P.add("scalar", lambda e, k=k: e.activation(
                        slab_bf[:], slab[:], AF.Copy, bias=0.0, scale=g2c_sb[:, k:k + 1]),
                        after=[ldw, ld_g2])
                    st1 = P.dma(we_scr[ei, 0, :, k, :], slab_bf[:, 0:512], after=[scl])
                    prev_st = P.dma(we_scr[ei, 1, :, k, :], slab_bf[:, 512:1024], after=[scl, st1])
            prev_wd = None
            prev_wg = None
            for k in range(KD):
                ldw = P.dma(slab[:, 0:LR], wdown_h[k * 128:(k + 1) * 128, :],
                            after=[prev_st, prev_wd])
                prev_st = None
                prev_wd = P.add("scalar", lambda e, k=k: e.activation(
                    wdown_sb[:, k * LR:(k + 1) * LR], slab[:, 0:LR], AF.Copy, bias=0.0,
                    scale=g1c_sb[:, k:k + 1]), after=[ldw, ld_g1])
                ldg = P.dma(slab[:, 512:512 + E], wg_h[k * 128:(k + 1) * 128, :],
                            after=[ldw, prev_wg])
                prev_wg = P.add("scalar", lambda e, k=k: e.activation(
                    wg_sb[:, k * E:(k + 1) * E], slab[:, 512:512 + E], AF.Copy, bias=0.0,
                    scale=g2c_sb[:, k:k + 1]), after=[ldg, ld_g2])
            l1 = P.dma(slab[0:64, :], wupk_h, after=[prev_wd, prev_wg])
            c1 = P.add("vector", lambda e: e.tensor_copy(wupk_sb[:], slab[0:64, :]), after=[l1])
            l2 = P.dma(slab[0:64, :], wupv_h, after=[l1, c1])
            c2 = P.add("vector", lambda e: e.tensor_copy(
                wupv3_sb[0:64, :].rearrange("p (d g) -> p d g", d=HD),
                slab[0:64, :].rearrange("p (g d) -> p g d", g=H).transpose([0, 2, 1])),
                after=[l2])
            l2b = P.dma(slab[64:65, 0:DM], bupv_h.unsqueeze(0), after=[l2])
            c2b = P.add("vector", lambda e: e.tensor_copy(
                wupv3_sb[64:65, :].rearrange("p (d g) -> p d g", d=HD),
                slab[64:65, 0:DM].rearrange("p (g d) -> p g d", g=H).transpose([0, 2, 1])),
                after=[l2b])
            prev_cp = c2b
            for i in range(ID):
                ld3 = P.dma(slab[:, 0:96], wxpT_h[i * 128:(i + 1) * 128, :], after=[prev_cp])
                prev_cp = P.add("vector", lambda e, i=i: e.tensor_copy(
                    wxpT_sb[:, i * 96:(i + 1) * 96], slab[:, 0:96]), after=[ld3])
            l4 = P.dma(slab[0:64, 0:DI // 2], wdtT_h[:, 0:DI // 2], after=[prev_cp])
            c4 = P.add("vector", lambda e: e.tensor_copy(wdtT_sb[:, 0:DI // 2], slab[0:64, 0:DI // 2]), after=[l4])
            l5 = P.dma(slab[0:64, 0:DI // 2], wdtT_h[:, DI // 2:], after=[l4, c4])
            prev_cp = P.add("vector", lambda e: e.tensor_copy(wdtT_sb[:, DI // 2:], slab[0:64, 0:DI // 2]), after=[l5])
            for i in range(ID):
                ldw = P.dma(slab[:], woutT_h[i * 128:(i + 1) * 128, :], after=[prev_cp])
                prev_cp = P.add("vector", lambda e, i=i: e.tensor_copy(
                    woutT_sb[:, i * DM:(i + 1) * DM], slab[:]), after=[ldw])
            P.barrier()

        with ExitStack() as pm:
            msb = lambda n, sh, dt=F32: pm.enter_context(nc.sbuf_tensor(n, sh, dt))
            x_tm = msb("x_tm", [128, 2, DM])
            xn_tm = msb("xn_tm", [128, 2, DM])
            sx_tm = msb("sx_tm", [128, 2, DM])
            xnA_fm = msb("xnA_fm", [128, KD, TCM], BF16)
            stat = msb("stat", [128, 32])
            xs_scr = [msb(f"xs_scr{i}", [128, DC - 1 + TCM]) for i in range(2)]
            ct_a = msb("ct_a", [128, TCM])
            ct_b = msb("ct_b", [128, TCM])
            halo_sb = msb("halo_sb", [128, ID * (DC - 1)])
            xc_bf = msb("xc_bf", [128, ID, TCM], BF16)
            zsil = msb("zsil", [128, ID, TCM], BF16)
            dt_fm = msb("dt_fm", [128, ID, TCM])
            udt_fm = msb("udt_fm", [128, ID, TCM], BF16)
            dblBC_bf = msb("dblBC_bf", [96, TCM], BF16)
            dtlo_bf = msb("dtlo_bf", [64, TCM], BF16)
            e1_scr = msb("e1_scr", [128, TCM])
            Bbc = msb("Bbc", [128, DS, TCM], BF16)
            Cbc = msb("Cbc", [128, DS, TCM], BF16)
            dA_s = [msb(f"dA{i}", [128, TCM]) for i in range(2)]
            U_all = msb("U_all", [128, DS, TCM], BF16)
            h_i = msb("h_i", [128, DS, TCM], BF16)
            pr_s = msb("pr_s", [128, TCM])
            prod_s = msb("prod_s", [128, DS, TCM], BF16)
            red_a = msb("red_a", [128, 8, TCM])
            red_b = msb("red_b", [128, 4, TCM])
            red_c = msb("red_c", [128, 2, TCM])
            y_s = msb("y_s", [128, TCM])
            hstate = msb("hstate", [128, ID * DS])
            ytot_bf = msb("ytot_bf", [128, ID, TCM], BF16)
            outA_sb = [msb(f"outA{i}", [128, TCM]) for i in range(2)]
            S1_tm = msb("S1_tm", [128, 2, DM])
            winbuf = [msb(f"winbuf{i}", [128, KD, 256], BF16) for i in range(2)]

            NCH = T // TCM
            pend_w = [None]
            st_s1_h = [None]
            wout_guard = [None]
            pb4_guard = [None]

            def emit_wout():
                pw = pend_w[0]
                if pw is None:
                    return None
                pend_w[0] = None
                fin_l, dt_last, ws, wt0 = pw
                s1_parts = []
                prev_ads = [None, None]
                wout_free = [None, None]
                last_mm = None
                ads_last = None
                for mo in range(KD):
                    bank = pb[1] if mo % 2 == 0 else pb[5]
                    mm = None
                    for i in range(ID):
                        mm = P.add("tensor", lambda e, mo=mo, i=i, b=bank: e.matmul(
                            b[:, 0:TCM], woutT_sb[:, i * DM + mo * 128:i * DM + (mo + 1) * 128],
                            ytot_bf[:, i, :], start=(i == 0), stop=(i == ID - 1)),
                            after=[fin_l[i], wout_free[mo % 2],
                                   dt_last if mo == 0 else None])
                    last_mm = mm
                    ev = P.add("scalar", lambda e, mo=mo, b=bank: e.activation(
                        outA_sb[mo % 2][:], b[:, 0:TCM], AF.Copy, bias=0.0, scale=1.0),
                        after=[mm])
                    wout_free[mo % 2] = ev
                    gbank = pbG if mo % 2 == 0 else pb[4]
                    tts_ = []
                    for j in range(2):
                        tts_.append(P.add("tensor", lambda e, mo=mo, j=j, gb=gbank: e.transpose(
                            gb[:, j * 128:(j + 1) * 128],
                            outA_sb[mo % 2][:, j * 128:(j + 1) * 128], eye32[:]),
                            after=[ev, prev_ads[mo % 2]]))
                    wout_free[mo % 2] = tts_[-1]
                    ad = P.add("vector", lambda e, mo=mo, gb=gbank: e.tensor_tensor(
                        S1_tm[:, 0:2, mo * 128:(mo + 1) * 128],
                        gb[:, 0:256].rearrange("p (a b) -> p a b", a=2),
                        sx_tm[:, 0:2, mo * 128:(mo + 1) * 128], ALU.add),
                        after=tts_ + ([st_s1_h[0]] if mo == 0 else []))
                    prev_ads[mo % 2] = ad
                    ads_last = ad
                    s1_parts.append(ad)
                st_s1_h[0] = P.dma(
                    s1_scr[ws, wt0:wt0 + TCM, :].rearrange("(a p) d -> p a d", p=128),
                    S1_tm[:], after=s1_parts)
                wout_guard[0] = last_mm
                pb4_guard[0] = ads_last
                return ads_last

            for s in range(NSEQ):
                P.barrier()
                P.add("vector", lambda e: e.memset(hstate[:], 0.0))
                P.add("vector", lambda e: e.memset(halo_sb[:], 0.0))
                for ci in range(NCH):
                    P.barrier()
                    t0 = ci * TCM
                    ld_x = P.dma(x_tm[:], x_h[s, t0:t0 + TCM, :].rearrange("(a p) d -> p a d", p=128))

                    sq_v = red_a[:, 0:4, :].rearrange("p a b -> p (a b)")
                    sq_prev = [None]

                    def stats_multi(entries, eps):
                        a1, a2, m1, m2, vr, sd, res = {}, {}, {}, {}, {}, {}, []
                        for k, (src, c0, deps) in enumerate(entries):
                            a1[k] = P.add("scalar", lambda e, src=src, c0=c0: e.activation(
                                sq_v, src, AF.Copy, bias=0.0, scale=1.0,
                                accum_out=stat[:, c0:c0 + 1]), after=deps + [sq_prev[0]])
                            sq_prev[0] = a1[k]
                        for k, (src, c0, deps) in enumerate(entries):
                            a2[k] = P.add("scalar", lambda e, src=src, c0=c0: e.activation(
                                sq_v, src, AF.Square, bias=0.0, scale=1.0,
                                accum_out=stat[:, c0 + 1:c0 + 2]), after=deps + [sq_prev[0]])
                            sq_prev[0] = a2[k]
                        for k, (src, c0, deps) in enumerate(entries):
                            m1[k] = P.add("vector", lambda e, c0=c0: e.tensor_scalar(
                                stat[:, c0 + 2:c0 + 3], stat[:, c0:c0 + 1], 1.0 / DM, None,
                                ALU.mult, ALU.bypass), after=[a1[k]])
                        for k, (src, c0, deps) in enumerate(entries):
                            m2[k] = P.add("vector", lambda e, c0=c0: e.tensor_scalar(
                                stat[:, c0 + 3:c0 + 4], stat[:, c0 + 2:c0 + 3],
                                stat[:, c0 + 2:c0 + 3], None, ALU.mult, ALU.bypass),
                                after=[m1[k]])
                        for k, (src, c0, deps) in enumerate(entries):
                            vr[k] = P.add("vector", lambda e, c0=c0: e.scalar_tensor_tensor(
                                stat[:, c0 + 4:c0 + 5], stat[:, c0 + 1:c0 + 2], 1.0 / DM,
                                stat[:, c0 + 3:c0 + 4], ALU.mult, ALU.subtract),
                                after=[a2[k], m2[k]])
                        for k, (src, c0, deps) in enumerate(entries):
                            sd[k] = P.add("scalar", lambda e, c0=c0: e.activation(
                                stat[:, c0 + 5:c0 + 6], stat[:, c0 + 4:c0 + 5], AF.Sqrt,
                                bias=eps, scale=1.0), after=[vr[k]])
                        for k, (src, c0, deps) in enumerate(entries):
                            res.append(P.add("vector", lambda e, c0=c0: e.reciprocal(
                                stat[:, c0 + 6:c0 + 7], stat[:, c0 + 5:c0 + 6]),
                                after=[sd[k]]))
                        return res

                    ivs = stats_multi([(x_tm[:, j, :], 16 * j, [ld_x]) for j in range(2)],
                                      consts_sb[:, 0:1])
                    xn_ops, xnA_ops, sx_ops = [], [], []
                    for j in range(2):
                        xn_ops.append(P.add("vector", lambda e, j=j, a=16 * j + 2, b=16 * j + 6:
                                            e.tensor_scalar(
                            xn_tm[:, j, :], x_tm[:, j, :], stat[:, a:a + 1], stat[:, b:b + 1],
                            ALU.subtract, ALU.mult), after=[ivs[j]]))
                    wads = emit_wout()
                    for j in range(2):
                        sx_ops.append(P.add("vector", lambda e, j=j: e.tensor_tensor(
                            sx_tm[:, j, :], x_tm[:, j, :], xn_tm[:, j, :], ALU.add),
                            after=[xn_ops[j], wads if j == 0 else None]))
                    ivs2 = stats_multi([(xn_tm[:, j, :], 16 * j + 8, [xn_ops[j]])
                                        for j in range(2)], consts_sb[:, 0:1])
                    for j in range(2):
                        xnA_ops.append(P.add("vector", lambda e, j=j, a=16 * j + 10,
                                             b=16 * j + 14: e.tensor_scalar(
                            x_tm[:, j, :], xn_tm[:, j, :], stat[:, a:a + 1], stat[:, b:b + 1],
                            ALU.subtract, ALU.mult), after=[ivs2[j], sx_ops[j]]))
                    st_xn = P.dma(xn_scr[s, t0:t0 + TCM, :].rearrange("(a p) d -> p a d", p=128),
                                  xn_tm[:], after=xn_ops)

                    fm_ev = []
                    for k in range(KD):
                        bank = pb[2 + (k % 2)]
                        tj = []
                        for j in range(2):
                            tj.append(P.add("tensor", lambda e, j=j, k=k, b=bank: e.transpose(
                                b[:, j * 128:(j + 1) * 128],
                                x_tm[:, j, k * 128:(k + 1) * 128], eye32[:]),
                                after=[xnA_ops[j], fm_ev[k - 2] if k >= 2 else None]))
                        fm_ev.append(P.add("scalar", lambda e, k=k, b=bank: e.activation(
                            xnA_fm[:, k, :], b[:, 0:TCM], AF.Copy, bias=0.0, scale=1.0), after=tj))

                    conv_done = [None] * ID
                    xcbf_done = [None] * ID
                    zs_done = [None] * ID
                    ev_m = [None] * 32
                    mm_last_of_g = [None] * 16
                    for g in range(16):
                        wld = []
                        war = [mm_last_of_g[g - 2]] if g >= 2 else []
                        for k in range(KD):
                            wld.append(P.dma(winbuf[g % 2][:, k, :],
                                             winT_scr[k, :, g * 256:(g + 1) * 256], after=war))
                        for m in range(g * 2, g * 2 + 2):
                            bank = pb[m % 2]
                            mm = None
                            for k in range(KD):
                                mm = P.add("tensor", lambda e, m=m, k=k, g=g, b=bank: e.matmul(
                                    b[:, 0:TCM], winbuf[g % 2][:, k, (m % 2) * 128:(m % 2 + 1) * 128],
                                    xnA_fm[:, k, :], start=(k == 0), stop=(k == KD - 1)),
                                    after=[wld[k], fm_ev[k], ev_m[m - 2] if m >= 2 else None])
                            mm_last_of_g[g] = mm
                            if m < 16:
                                i = m
                                ev = P.add("scalar", lambda e, m=m, i=i, b=bank: e.activation(
                                    xs_scr[i % 2][:, DC - 1:], b[:, 0:TCM], AF.Identity,
                                    bias=cxz_sb[:, m:m + 1], scale=1.0),
                                    after=[mm, conv_done[i - 2] if i >= 2 else None])
                                ev_m[m] = ev
                                hl = P.add("vector", lambda e, i=i: e.tensor_copy(
                                    xs_scr[i % 2][:, 0:DC - 1], halo_sb[:, i * 3:(i + 1) * 3]),
                                    after=[ev])
                                c1 = P.add("vector", lambda e, i=i: e.tensor_scalar(
                                    ct_a[:], xs_scr[i % 2][:, 0:TCM], convw_sb[:, 4 * i:4 * i + 1],
                                    None, ALU.mult, ALU.bypass), after=[ev, hl])
                                c2 = P.add("vector", lambda e, i=i: e.scalar_tensor_tensor(
                                    ct_b[:], xs_scr[i % 2][:, 1:1 + TCM], convw_sb[:, 4 * i + 1:4 * i + 2],
                                    ct_a[:], ALU.mult, ALU.add), after=[c1])
                                c3 = P.add("vector", lambda e, i=i: e.scalar_tensor_tensor(
                                    ct_a[:], xs_scr[i % 2][:, 2:2 + TCM], convw_sb[:, 4 * i + 2:4 * i + 3],
                                    ct_b[:], ALU.mult, ALU.add), after=[c2])
                                c4 = P.add("vector", lambda e, i=i: e.scalar_tensor_tensor(
                                    ct_b[:], xs_scr[i % 2][:, 3:3 + TCM], convw_sb[:, 4 * i + 3:4 * i + 4],
                                    ct_a[:], ALU.mult, ALU.add), after=[c3])
                                sv = P.add("vector", lambda e, i=i: e.tensor_copy(
                                    halo_sb[:, i * 3:(i + 1) * 3], xs_scr[i % 2][:, TCM:TCM + 3]),
                                    after=[ev])
                                sil = P.add("scalar", lambda e, i=i: e.activation(
                                    xc_bf[:, i, :], ct_b[:], AF.Silu, bias=convb_sb[:, i:i + 1],
                                    scale=1.0), after=[c4])
                                conv_done[i] = P.add("vector", lambda e: e.nop(), after=[c4, sv, sil])
                                xcbf_done[i] = sil
                            else:
                                i = m - 16
                                ev = P.add("scalar", lambda e, m=m, i=i, b=bank: e.activation(
                                    zsil[:, i, :], b[:, 0:TCM], AF.Silu, bias=cxz_sb[:, m:m + 1],
                                    scale=1.0), after=[mm])
                                ev_m[m] = ev
                                zs_done[i] = ev

                    mm = None
                    for i in range(ID):
                        mm = P.add("tensor", lambda e, i=i: e.matmul(
                            pb[0][0:96, 0:TCM], wxpT_sb[:, i * 96:(i + 1) * 96],
                            xc_bf[:, i, :], start=(i == 0), stop=(i == ID - 1)),
                            after=[xcbf_done[i], ev_m[30]])
                    ev_dtlo = P.add("scalar", lambda e: e.activation(
                        dtlo_bf[:], pb[0][0:64, 0:TCM], AF.Copy, bias=0.0, scale=1.0), after=[mm])
                    ev_bc = P.add("vector", lambda e: e.tensor_copy(
                        dblBC_bf[64:96, :], pb[0][64:96, 0:TCM]), after=[mm])
                    bc_ops = []
                    for n in range(DS):
                        bc_ops.append(P.dma(Bbc[:, n, :], bcast_view(dblBC_bf[64 + n:65 + n, :], TCM),
                                            after=[ev_bc]))
                        bc_ops.append(P.dma(Cbc[:, n, :], bcast_view(dblBC_bf[80 + n:81 + n, :], TCM),
                                            after=[ev_bc]))

                    dt_done = [None] * ID
                    udt_done = [None] * ID
                    ev_prev = None
                    for i in range(ID):
                        bank = pb[1] if i % 2 == 0 else pb[4]
                        mmdt = P.add("tensor", lambda e, i=i, b=bank: e.matmul(
                            b[:, 0:TCM], wdtT_sb[:, i * 128:(i + 1) * 128], dtlo_bf[:],
                            start=True, stop=True),
                            after=[ev_dtlo, ev_m[31] if i == 0 else None,
                                   dt_done[i - 2] if i >= 2 else None,
                                   pb4_guard[0] if i == 1 else None])
                        ex1 = P.add("scalar", lambda e, i=i, b=bank: e.activation(
                            e1_scr[:], b[:, 0:TCM], AF.Exp, bias=bdt_sb[:, i:i + 1], scale=1.0),
                            after=[mmdt, ev_prev])
                        ex2 = P.add("scalar", lambda e, i=i: e.activation(
                            dt_fm[:, i, :], e1_scr[:], AF.Ln, bias=consts_sb[:, 2:3], scale=1.0), after=[ex1])
                        ev_prev = ex2
                        dt_done[i] = ex2
                        udt_done[i] = P.add("vector", lambda e, i=i: e.tensor_tensor(
                            udt_fm[:, i, :], dt_fm[:, i, :], xc_bf[:, i, :], ALU.mult), after=[ex2])

                    fin = [None] * ID
                    tts_hist = {}
                    prev_f1 = None
                    hsave_prev = None
                    prev_reda = None
                    pend = None

                    def emit_f1(iq, yf):
                        nonlocal prev_f1
                        f1 = P.add("vector", lambda e, i=iq: e.scalar_tensor_tensor(
                            pr_s[:], xc_bf[:, i, :], D_sb[:, i:i + 1], y_s[:],
                            ALU.mult, ALU.add), after=[yf, fin[iq - 1] if iq > 0 else None])
                        prev_f1 = f1
                        return f1

                    def emit_fin(iq, f1):
                        fin[iq] = P.add("vector", lambda e, i=iq: e.tensor_tensor(
                            ytot_bf[:, i, :], pr_s[:], zsil[:, i, :], ALU.mult),
                            after=[f1, zs_done[iq]]
                            + ([wout_guard[0]] if iq == 0 else []))
                    for i in range(ID):
                        uall = P.add("vector", lambda e, i=i: e.tensor_tensor(
                            U_all[:, 0:DS, :], Bbc[:, 0:DS, :],
                            udt_fm[:, i, :].unsqueeze(1).broadcast_to([128, DS, TCM]),
                            ALU.mult),
                            after=[udt_done[i]] + ([bc_ops[2 * n] for n in range(DS)]
                                                   if i == 0 else []))
                        for n in range(DS):
                            idx = i * DS + n
                            ex = P.add("scalar", lambda e, i=i, n=n, sl=idx % 2: e.activation(
                                dA_s[sl][:], dt_fm[:, i, :], AF.Exp, bias=0.0,
                                scale=A_sb[:, i * DS + n:i * DS + n + 1]),
                                after=[dt_done[i], tts_hist.get(idx - 2)])
                            tts = P.add("vector", lambda e, i=i, n=n, idx=idx: e.tensor_tensor_scan(
                                h_i[:, n, :], dA_s[idx % 2][:],
                                U_all[:, n, :],
                                hstate[:, i * DS + n:i * DS + n + 1], ALU.mult, ALU.add),
                                after=[ex, uall, hsave_prev if n == 0 else None])
                            tts_hist[idx] = tts
                        pf1 = emit_f1(*pend) if pend is not None else None
                        hsave_prev = P.add("vector", lambda e, i=i: e.tensor_copy(
                            hstate[:, i * DS:(i + 1) * DS],
                            h_i[:, 0:DS, TCM - 1:TCM].squeeze(2)),
                            after=[tts_hist[i * DS + DS - 1]])
                        prodall = P.add("vector", lambda e, i=i: e.tensor_tensor(
                            prod_s[:, 0:DS, :], h_i[:, 0:DS, :], Cbc[:, 0:DS, :], ALU.mult),
                            after=[tts_hist[i * DS + DS - 1], prev_reda]
                            + ([bc_ops[2 * n + 1] for n in range(DS)] if i == 0 else []))
                        reda = P.add("gpsimd", lambda e: e.tensor_tensor(
                            red_a[:, 0:8, :].rearrange("p a b -> p (a b)"),
                            prod_s[:, 0:8, :].rearrange("p a b -> p (a b)"),
                            prod_s[:, 8:16, :].rearrange("p a b -> p (a b)"), ALU.add),
                            after=[prodall])
                        prev_reda = reda
                        redb = P.add("gpsimd", lambda e: e.tensor_tensor(
                            red_b[:, 0:4, :].rearrange("p a b -> p (a b)"),
                            red_a[:, 0:4, :].rearrange("p a b -> p (a b)"),
                            red_a[:, 4:8, :].rearrange("p a b -> p (a b)"), ALU.add),
                            after=[reda])
                        redc = P.add("gpsimd", lambda e: e.tensor_tensor(
                            red_c[:, 0:2, :].rearrange("p a b -> p (a b)"),
                            red_b[:, 0:2, :].rearrange("p a b -> p (a b)"),
                            red_b[:, 2:4, :].rearrange("p a b -> p (a b)"), ALU.add),
                            after=[redb])
                        if pend is not None:
                            emit_fin(pend[0], pf1)
                            pend = None
                        yfin = P.add("gpsimd", lambda e: e.tensor_tensor(
                            y_s[:], red_c[:, 0, :], red_c[:, 1, :], ALU.add),
                            after=[redc, prev_f1])
                        pend = (i, yfin)
                    pf1 = emit_f1(*pend)
                    emit_fin(pend[0], pf1)
                    pend = None

                    pend_w[0] = (list(fin), dt_done[ID - 1], s, t0)
                emit_wout()
            P.barrier()
        mres.close()

        with ExitStack() as pf:
            fsb = lambda n, sh, dt=F32: pf.enter_context(nc.sbuf_tensor(n, sh, dt))
            fxn_tm = fsb("f_xn", [128, 4, DM])
            x1_tm = fsb("f_x1", [128, 4, DM])
            out_tm = fsb("f_out", [128, 4, DM])
            h_scr = fsb("f_hs", [128, DM])
            hsB = fsb("f_hsB", [128, DM])
            hs2 = [h_scr, hsB]
            pbj = [(pb[2], pb[3]), (pb[4], pb[5])]
            fsq_scr = fsb("f_sq", [128, DM])
            fstat = fsb("f_stat", [128, 80])
            h1_fm = fsb("f_h1fm", [128, KD, TCF], BF16)
            h2_fm = fsb("f_h2fm", [128, KD, TCF], BF16)
            h2e_fm = [fsb(f"f_h2e{i}", [128, KD, TCF], BF16) for i in range(2)]
            q2ARR = fsb("f_q2", [128, H, 256], BF16)
            k2ARR = fsb("f_k2", [128, H, 256], BF16)
            qb1_scr = fsb("f_qb1", [64, H, 64], BF16)
            kb1_scr = fsb("f_kb1", [64, H, 64], BF16)
            vg_sb = fsb("f_vg", [16, 64, 128], BF16)
            r1_bf = fsb("f_rbf", [65, TCF], BF16)
            attn_b = fsb("f_attnb", [128, 512], BF16)
            attT_sb = fsb("f_attT", [16, 4096], BF16)
            out2slabs = fsb("f_o2sl", [64, H, TCF], BF16)
            aproj_sb = [fsb(f"f_ap{i}", [128, TCF]) for i in range(2)]
            g_tm = fsb("f_gtm", [128, 4 * E])
            gn_tm = fsb("f_gntm", [128, 4 * E])
            gn_bf = fsb("f_gnbf", [8, TCF], BF16)
            gbc = [fsb(f"f_gbc{i}", [128, TCF], BF16) for i in range(4)]
            moe_sb = [fsb(f"f_moe{i}", [128, TCF]) for i in range(2)]
            webuf = [fsb(f"f_we{i}", [128, KD, 512], BF16) for i in range(2)]
            wo2_sb = fsb("f_wo2", [64, H * 512], BF16)

            def wq_slice(k, h):
                base = (k % 4) * 2
                off = h * HD
                return webuf[k // 4][:, base + off // 512, off % 512:off % 512 + HD]

            def wo_load(half, war):
                return [P.dma(wo2_sb[:, h * 512:(h + 1) * 512],
                              wo2_scr[h, :, half * 512:(half + 1) * 512], after=war)
                        for h in range(H)]

            def rms_to_fm(src_tm, dst_fm, scol, eps, dep0):
                evs = []
                sqs, sds, ivr, hhs = {}, {}, {}, {}
                prev_sq = None
                for j in range(4):
                    sqs[j] = P.add("scalar", lambda e, j=j: e.activation(
                        fsq_scr[:], src_tm[:, j, :], AF.Square, bias=0.0, scale=1.0,
                        accum_out=fstat[:, scol + j:scol + j + 1]), after=dep0 + [prev_sq])
                    prev_sq = sqs[j]
                    sds[j] = P.add("scalar", lambda e, j=j: e.activation(
                        fstat[:, scol + 8 + j:scol + 9 + j], fstat[:, scol + j:scol + j + 1],
                        AF.Sqrt, bias=eps, scale=1.0 / DM), after=[sqs[j]])
                for j in range(4):
                    ivr[j] = P.add("vector", lambda e, j=j: e.reciprocal(
                        fstat[:, scol + 16 + j:scol + 17 + j],
                        fstat[:, scol + 8 + j:scol + 9 + j]), after=[sds[j]])
                for j in range(4):
                    war = [evs[2 * (j - 2)], evs[2 * (j - 2) + 1]] if j >= 2 else []
                    hhs[j] = P.add("vector", lambda e, j=j: e.tensor_scalar(
                        hs2[j % 2][:], src_tm[:, j, :], fstat[:, scol + 16 + j:scol + 17 + j],
                        None, ALU.mult, ALU.bypass), after=[ivr[j]] + war)
                    tts_ = []
                    for k in range(KD):
                        bnk = pbj[j % 2][k // 4]
                        tts_.append(P.add("tensor", lambda e, j=j, k=k, b=bnk: e.transpose(
                            b[:, (k % 4) * 128:(k % 4 + 1) * 128],
                            hs2[j % 2][:, k * 128:(k + 1) * 128],
                            eye32[:]), after=[hhs[j]] + war))
                    e1 = P.add("scalar", lambda e, j=j: e.activation(
                        dst_fm[:, 0:4, j * 128:(j + 1) * 128], pbj[j % 2][0][:, 0:512].rearrange(
                            "p (a b) -> p a b", a=4), AF.Copy, bias=0.0, scale=1.0),
                        after=tts_[0:4])
                    e2 = P.add("scalar", lambda e, j=j: e.activation(
                        dst_fm[:, 4:8, j * 128:(j + 1) * 128], pbj[j % 2][1][:, 0:512].rearrange(
                            "p (a b) -> p a b", a=4), AF.Copy, bias=0.0, scale=1.0),
                        after=tts_[4:8])
                    evs += [e1, e2]
                return evs

            NCF = T // TCF
            for s in range(NSEQ):
                for ci in range(NCF):
                    P.barrier()
                    t0 = ci * TCF
                    prev_wo_free = None
                    prev_r_user = None
                    wql = [P.dma(webuf[k // 4][:, (k % 4) * 2:(k % 4) * 2 + 2, :]
                                 .rearrange("p a b -> p (a b)"), wq_scr[k])
                           for k in range(KD)]
                    wolA = wo_load(0, [])
                    ld_xn = P.dma(fxn_tm[:], xn_scr[s, t0:t0 + TCF, :].rearrange("(a p) d -> p a d", p=128))
                    ld_s1 = P.dma(out_tm[:], s1_scr[s, t0:t0 + TCF, :].rearrange("(a p) d -> p a d", p=128))

                    h1_evs = rms_to_fm(fxn_tm, h1_fm, 0, consts_sb[:, 1:2], [ld_xn])

                    mm = None
                    for k in range(KD):
                        mm = P.add("tensor", lambda e, k=k: e.matmul(
                            pb[0][0:64, 0:TCF], wdown_sb[:, k * LR:(k + 1) * LR], h1_fm[:, k, :],
                            start=(k == 0), stop=(k == KD - 1)),
                            after=[h1_evs[-1], prev_wo_free])
                    ev_r = P.add("scalar", lambda e: e.activation(
                        r1_bf[0:64, :], pb[0][0:64, 0:TCF], AF.Identity, bias=bdown_sb[:], scale=1.0),
                        after=[mm, prev_r_user])
                    ones_r = P.add("vector", lambda e: e.memset(r1_bf[64:65, :], 1.0),
                                   after=[prev_r_user])

                    ev2prev = [ev_r, prev_wo_free]
                    q2_ev = []
                    for h in range(H):
                        bank = pb[h % 2]
                        mm = None
                        for b in range(2):
                            for k in range(KD):
                                mm = P.add("tensor", lambda e, h=h, k=k, b=b, bk=bank: e.matmul(
                                    bk[64 * b:64 * b + 64, 0:256],
                                    wq_slice(k, h),
                                    h1_fm[:, k, :].rearrange("p (t two) -> p two t", two=2)[:, b, :],
                                    start=(k == 0), stop=(k == KD - 1),
                                    tile_position=(0, 64 * b)),
                                    after=[h1_evs[-1], ev2prev[h % 2]] + (wql if h == 0 else []))
                        ev = P.add("scalar", lambda e, h=h, bk=bank: e.activation(
                            q2ARR[:, h, :], bk[:, 0:256], AF.Identity, bias=bq2_sb[:, h:h + 1],
                            scale=1.0), after=[mm])
                        ev2prev[h % 2] = ev
                        q2_ev.append(ev)
                    k2_ev = []
                    for g in range(H):
                        bank = pb[g % 2]
                        mm = None
                        for b in range(2):
                            mm = P.add("tensor", lambda e, g=g, b=b, bk=bank: e.matmul(
                                bk[64 * b:64 * b + 64, 0:256],
                                wupk_sb[:, g * HD:(g + 1) * HD],
                                r1_bf[0:64, :].rearrange("p (t two) -> p two t", two=2)[:, b, :],
                                start=True, stop=True,
                                tile_position=(0, 64 * b)),
                                after=[ev_r, ev2prev[g % 2]])
                        ev = P.add("scalar", lambda e, g=g, bk=bank: e.activation(
                            k2ARR[:, g, :], bk[:, 0:256], AF.Identity, bias=bupk2_sb[:, g:g + 1],
                            scale=1.0), after=[mm])
                        ev2prev[g % 2] = ev
                        k2_ev.append(ev)

                    slab_evs = []
                    prev_vg_user = None
                    prev_qk_free = [k2_ev[-2], k2_ev[-1]]
                    prev_av_free = [None, None]
                    prev_attn_user = None
                    prev_attT_user = None
                    prev_b1_user = None
                    for bi in range(4):
                        vg_evs = []
                        for hq in range(16):
                            bank = pb[hq % 2]
                            mms = []
                            for hh in range(4):
                                hd = hq * 4 + hh
                                mms.append(P.add("tensor", lambda e, hd=hd, hh=hh, bk=bank, bi=bi: e.matmul(
                                    bk[0:16, hh * 128:(hh + 1) * 128],
                                    wupv3_sb[:, hd * H:(hd + 1) * H],
                                    r1_bf[:, bi * 128:(bi + 1) * 128],
                                    start=True, stop=True),
                                    after=[ev_r, ones_r, ev2prev[hq % 2] if hq < 2 else None,
                                           vg_evs[hq - 2] if hq >= 2 else None,
                                           prev_vg_user if hq < 2 else None]))
                            vg_evs.append(P.add("vector", lambda e, hq=hq, bk=bank: e.tensor_copy(
                                vg_sb[:, hq * 4:hq * 4 + 4, :].rearrange("p a b -> p (a b)"),
                                bk[0:16, 0:512]), after=mms))
                        qbank = pb[2 + (bi % 2)]
                        cpq = P.add("vector", lambda e, bi=bi: e.tensor_copy(
                            qb1_scr[:], q2ARR[64:128, 0:H, bi * 64:(bi + 1) * 64]),
                            after=[q2_ev[-1], prev_b1_user])
                        cpk = P.add("vector", lambda e, bi=bi: e.tensor_copy(
                            kb1_scr[:], k2ARR[64:128, 0:H, bi * 64:(bi + 1) * 64]),
                            after=[k2_ev[-1], prev_b1_user])
                        qk_last = None
                        for q in range(64):
                            pidx = bi * 64 + q
                            j, slot = q % 4, q // 4
                            for b in range(2):
                                if b == 0:
                                    sq_ = q2ARR[0:64, 0:H, pidx:pidx + 1].rearrange("p a b -> p (a b)")
                                    sk_ = k2ARR[0:64, 0:H, pidx:pidx + 1].rearrange("p a b -> p (a b)")
                                else:
                                    sq_ = qb1_scr[:, 0:H, q:q + 1].rearrange("p a b -> p (a b)")
                                    sk_ = kb1_scr[:, 0:H, q:q + 1].rearrange("p a b -> p (a b)")
                                qk_last = P.add("tensor", lambda e, sq_=sq_, sk_=sk_, j=j, slot=slot, b=b, bk=qbank: e.matmul(
                                    bk[32 * j:32 * j + 16, slot * 32 + b * 16:slot * 32 + b * 16 + 16],
                                    sq_, sk_,
                                    start=True, stop=True,
                                    tile_position=(0, 32 * j), skip_group_check=True),
                                    after=[prev_qk_free[bi % 2], q2_ev[-1], k2_ev[-1], cpq, cpk] if q == 0 and b == 0 else [])
                        prev_b1_user = qk_last
                        nm = P.add("vector", lambda e, bk=qbank: e.tensor_reduce(
                            fstat[:, 0:32], bk[:, 0:512].rearrange("p (a b) -> p a b", a=32),
                            AX.X, ALU.max, negate=True), after=[qk_last])
                        sub = P.add("vector", lambda e, bk=qbank: e.tensor_tensor(
                            h_scr[:, 0:512].rearrange("p (a b) -> p a b", a=32),
                            bk[:, 0:512].rearrange("p (a b) -> p a b", a=32),
                            fstat[:, 0:32].unsqueeze(2).broadcast_to([128, 32, 16]), ALU.add),
                            after=[nm, h1_evs[-1]])
                        exp_ = P.add("scalar", lambda e: e.activation(
                            fsq_scr[:, 0:512], h_scr[:, 0:512], AF.Exp, bias=0.0, scale=1.0),
                            after=[sub])
                        smr = P.add("vector", lambda e: e.tensor_reduce(
                            fstat[:, 32:64], fsq_scr[:, 0:512].rearrange("p (a b) -> p a b", a=32),
                            AX.X, ALU.add, negate=False), after=[exp_])
                        rc = P.add("vector", lambda e: e.reciprocal(
                            fstat[:, 0:32], fstat[:, 32:64]), after=[smr])
                        nrm = P.add("vector", lambda e: e.tensor_tensor(
                            attn_b[:].rearrange("p (a b) -> p a b", a=32),
                            fsq_scr[:, 0:512].rearrange("p (a b) -> p a b", a=32),
                            fstat[:, 0:32].unsqueeze(2).broadcast_to([128, 32, 16]), ALU.mult),
                            after=[rc, prev_attn_user])
                        at_evs = []
                        tts_ = []
                        for gi in range(32):
                            slot, b = gi // 2, gi % 2
                            tts_.append(P.add("tensor", lambda e, gi=gi, slot=slot, b=b: e.transpose(
                                pbBF[0:16, (gi % 8) * 128:(gi % 8 + 1) * 128],
                                attn_b[:, slot * 32 + b * 16:slot * 32 + b * 16 + 16], eyebf[:]),
                                after=[nrm, at_evs[-1] if gi % 8 == 0 and at_evs else None,
                                       prev_attT_user if gi == 0 else None]))
                            if gi % 8 == 7:
                                at_evs.append(P.add("vector", lambda e, gi=gi: e.tensor_copy(
                                    attT_sb[:, (gi // 8) * 1024:(gi // 8 + 1) * 1024],
                                    pbBF[0:16, 0:1024]), after=tts_[-8:]))
                        for a in range(4):
                            abank = pb[4 + (a % 2)]
                            av_last = None
                            for ti in range(32):
                                tt = a * 32 + ti
                                t = bi * 128 + tt
                                qq = t // 2
                                j, slot, b = qq % 4, (qq // 4) % 16, t % 2
                                gi = slot * 2 + b
                                av_last = P.add("tensor", lambda e, tt=tt, ti=ti, j=j, gi=gi, bk=abank: e.matmul(
                                    bk[0:64, ti * 16:(ti + 1) * 16],
                                    vg_sb[:, 0:64, tt:tt + 1].rearrange("p a b -> p (a b)"),
                                    attT_sb[:, gi * 128 + 32 * j:gi * 128 + 32 * j + 16],
                                    start=True, stop=True),
                                    after=[at_evs[-1], vg_evs[-1], prev_av_free[a % 2],
                                           h1_evs[-1]] if ti == 0 else [])
                            sev = P.add("vector", lambda e, bi=bi, a=a, bk=abank: e.tensor_copy(
                                out2slabs[:, :, bi * 128 + a * 32:bi * 128 + (a + 1) * 32].transpose([0, 2, 1]),
                                bk[0:64, 0:512].rearrange("p (a b) -> p a b", a=32)),
                                after=[av_last])
                            prev_av_free[a % 2] = sev
                            slab_evs.append(sev)
                        prev_qk_free[bi % 2] = sub
                        prev_attn_user = tts_[-1]
                        prev_attT_user = av_last
                        prev_vg_user = av_last
                        if DEBUG_TAPS and s == 0 and ci == 0 and bi == 0:
                            prev_attn_user = P.dma(dbg_h["attn"], attn_b[:], after=[tts_[-1]])
                            prev_attT_user = P.dma(dbg_h["attT"], attT_sb[:], after=[av_last])
                            prev_vg_user = P.dma(dbg_h["vg"], vg_sb[:], after=[av_last])
                    prev_r_user = av_last

                    if DEBUG_TAPS and s == 0 and ci == 0:
                        P.dma(dbg_h["o2sl"], out2slabs[:], after=slab_evs)
                        P.dma(dbg_h["q2"], q2ARR[:], after=[q2_ev[-1]])
                        P.dma(dbg_h["k2"], k2ARR[:], after=[k2_ev[-1]])
                    x1_ops = []
                    ap_free = [vg_evs[-2], vg_evs[-1]]
                    prev_tt2 = [None, None]
                    wolB = None
                    for mo in range(KD):
                        bank = pb[mo % 2]
                        mm = None
                        for h in range(H):
                            mm = P.add("tensor", lambda e, mo=mo, h=h, b=bank: e.matmul(
                                b[:, 0:TCF],
                                wo2_sb[:, h * 512 + (mo % 4) * 128:h * 512 + (mo % 4) * 128 + 128],
                                out2slabs[:, h, :], start=(h == 0), stop=(h == H - 1)),
                                after=[slab_evs[-1], ap_free[mo % 2]]
                                + (wolA if mo == 0 else [])
                                + (wolB if mo == 4 else []))
                        if mo == 3:
                            wolB = wo_load(1, [mm])
                        ev = P.add("scalar", lambda e, mo=mo, b=bank: e.activation(
                            aproj_sb[mo % 2][:], b[:, 0:TCF], AF.Identity, bias=bo_sb[:, mo:mo + 1],
                            scale=1.0), after=[mm])
                        ap_free[mo % 2] = ev
                        gbank = pbG if mo % 2 == 0 else pb[2]
                        tts_ = []
                        for j in range(4):
                            tts_.append(P.add("tensor", lambda e, mo=mo, j=j, gb=gbank: e.transpose(
                                gb[:, j * 128:(j + 1) * 128],
                                aproj_sb[mo % 2][:, j * 128:(j + 1) * 128], eye32[:]),
                                after=[ev, prev_tt2[mo % 2] if j == 0 else None]))
                        ad = P.add("vector", lambda e, mo=mo, gb=gbank: e.tensor_tensor(
                            x1_tm[:, 0:4, mo * 128:(mo + 1) * 128],
                            gb[:, 0:512].rearrange("p (a b) -> p a b", a=4),
                            fxn_tm[:, 0:4, mo * 128:(mo + 1) * 128], ALU.add), after=tts_)
                        prev_tt2[mo % 2] = ad
                        ap_free[mo % 2] = ad
                        x1_ops.append(ad)

                    if DEBUG_TAPS and s == 0 and ci == 0:
                        P.dma(dbg_h["x1"], x1_tm[:], after=[x1_ops[-1]])
                    h2_evs = rms_to_fm(x1_tm, h2_fm, 24, consts_sb[:, 1:2], [x1_ops[-1]])

                    mm = None
                    for k in range(KD):
                        mm = P.add("tensor", lambda e, k=k: e.matmul(
                            pb[0][0:E, 0:TCF], wg_sb[:, k * E:(k + 1) * E], h2_fm[:, k, :],
                            start=(k == 0), stop=(k == KD - 1)),
                            after=[h2_evs[-1], ap_free[0]])
                    gev = P.add("scalar", lambda e: e.activation(
                        h_scr[0:E, 0:TCF], pb[0][0:E, 0:TCF], AF.Identity, bias=bg_sb[:], scale=1.0),
                        after=[mm, x1_ops[-1]])
                    gt = []
                    for j in range(4):
                        gt.append(P.add("tensor", lambda e, j=j: e.transpose(
                            pbG[:, j * E:(j + 1) * E], h_scr[0:E, j * 128:(j + 1) * 128],
                            eye32[0:E, 0:E]),
                            after=[gev] + ([prev_tt2[0]] if j == 0 else [])))
                    gcp = P.add("vector", lambda e: e.tensor_copy(g_tm[:], pbG[:, 0:4 * E]), after=gt)
                    gsm = []
                    for j in range(4):
                        nm = P.add("vector", lambda e, j=j: e.tensor_reduce(
                            fstat[:, 40 + j:41 + j], g_tm[:, j * E:(j + 1) * E], AX.X, ALU.max,
                            negate=True), after=[gcp])
                        sb_ = P.add("vector", lambda e, j=j: e.tensor_tensor(
                            gn_tm[:, j * E:(j + 1) * E], g_tm[:, j * E:(j + 1) * E],
                            fstat[:, 40 + j:41 + j].broadcast_to([128, E]), ALU.add), after=[nm])
                        ex_ = P.add("scalar", lambda e, j=j: e.activation(
                            g_tm[:, j * E:(j + 1) * E], gn_tm[:, j * E:(j + 1) * E], AF.Exp,
                            bias=0.0, scale=1.0), after=[sb_])
                        sm_ = P.add("vector", lambda e, j=j: e.tensor_reduce(
                            fstat[:, 44 + j:45 + j], g_tm[:, j * E:(j + 1) * E], AX.X, ALU.add,
                            negate=False), after=[ex_])
                        rc_ = P.add("vector", lambda e, j=j: e.reciprocal(
                            fstat[:, 40 + j:41 + j], fstat[:, 44 + j:45 + j]), after=[sm_])
                        mu_ = P.add("vector", lambda e, j=j: e.tensor_scalar(
                            gn_tm[:, j * E:(j + 1) * E], g_tm[:, j * E:(j + 1) * E],
                            fstat[:, 40 + j:41 + j], None, ALU.mult, ALU.bypass), after=[rc_])
                        gsm.append(mu_)
                    gbt = []
                    for j in range(4):
                        gbt.append(P.add("tensor", lambda e, j=j: e.transpose(
                            pbG[0:E, j * 128:(j + 1) * 128], gn_tm[:, j * E:(j + 1) * E], eye32[:]),
                            after=[gsm[j]] + ([gcp] if j == 0 else [])))
                    gnf = P.add("vector", lambda e: e.tensor_copy(gn_bf[:], pbG[0:E, 0:TCF]), after=gbt)

                    wel = {}
                    moe_ads = []
                    for half in range(2):
                        mm_last = None
                        h2e_done = [None, None]
                        mm_of_e = [None] * E
                        for ei in range(E):
                            war = [mm_of_e[ei - 2]] if ei >= 2 else ([h2_evs[-1]] if half == 0 else [moe_ads[-1]])
                            wel[(half, ei)] = P.dma(webuf[ei % 2][:],
                                                    we_scr[ei, half], after=war)
                            gb = P.dma(gbc[ei % 4][:], bcast_view(gn_bf[ei:ei + 1, :], TCF),
                                       after=[gnf] + ([mm_of_e[ei - 4]] if ei >= 4 else []))
                            h2e = P.add("vector", lambda e, ei=ei: e.tensor_tensor(
                                h2e_fm[ei % 2][:], h2_fm[:],
                                gbc[ei % 4][:].unsqueeze(1).broadcast_to([128, KD, TCF]),
                                ALU.mult), after=[gb, h2_evs[-1], mm_of_e[ei - 2] if ei >= 2 else None])
                            mm = None
                            for mo4 in range(4):
                                for k in range(KD):
                                    mm = P.add("tensor", lambda e, ei=ei, mo4=mo4, k=k, half=half: e.matmul(
                                        pb[2 + mo4][:, 0:TCF],
                                        webuf[ei % 2][:, k, mo4 * 128:(mo4 + 1) * 128],
                                        h2e_fm[ei % 2][:, k, :],
                                        start=(ei == 0 and k == 0), stop=(ei == E - 1 and k == KD - 1)),
                                        after=[wel[(half, ei)], h2e,
                                               (h2_evs[-1] if half == 0 else moe_ads[-1]) if ei == 0 and mo4 == 0 and k == 0 else None])
                            mm_of_e[ei] = mm
                            mm_last = mm
                        for mo4 in range(4):
                            mo = half * 4 + mo4
                            ev = P.add("scalar", lambda e, mo4=mo4: e.activation(
                                moe_sb[mo4 % 2][:], pb[2 + mo4][:, 0:TCF], AF.Copy, bias=0.0,
                                scale=1.0), after=[mm_last, moe_ads[-2] if len(moe_ads) >= 2 else None])
                            tts_ = []
                            for j in range(4):
                                tts_.append(P.add("tensor", lambda e, j=j, mo4=mo4: e.transpose(
                                    pbG[:, j * 128:(j + 1) * 128],
                                    moe_sb[mo4 % 2][:, j * 128:(j + 1) * 128], eye32[:]),
                                    after=[ev, moe_ads[-1] if moe_ads else gnf]))
                            if half == 0 and mo4 == 0:
                                sx1 = []
                                for j in range(4):
                                    sx1.append(P.add("vector", lambda e, j=j: e.tensor_tensor(
                                        fxn_tm[:, j, :], out_tm[:, j, :], x1_tm[:, j, :], ALU.add),
                                        after=[ld_s1, x1_ops[-1], h2_evs[-1]]))
                                sx1_done = sx1[-1]
                            ad = P.add("vector", lambda e, mo=mo: e.tensor_tensor(
                                out_tm[:, 0:4, mo * 128:(mo + 1) * 128],
                                pbG[:, 0:512].rearrange("p (a b) -> p a b", a=4),
                                fxn_tm[:, 0:4, mo * 128:(mo + 1) * 128], ALU.add),
                                after=tts_ + [sx1_done])
                            moe_ads.append(ad)
                    st_out = P.dma(out_h[s, t0:t0 + TCF, :].rearrange("(a p) d -> p a d", p=128),
                                   out_tm[:], after=moe_ads)
                    out_stores.append(st_out)

        P.barrier()
        P.add("sync", lambda e: e.nop(), after=out_stores[-1:])

        blk = top.enter_context(nc.Block())
        with nc.allow_non_contiguous_dma(reason="small one-time vector layout loads"):
            P.emit(blk, sems, dsems)
    return nc


_NC_CACHE = None


def _get_nc():
    global _NC_CACHE
    if _NC_CACHE is None:
        _NC_CACHE = build_nc()
    return _NC_CACHE


def host_common(L0_attn_ln_g, L0_attn_ln_b, L0_attn_W_in, L0_attn_conv_w,
                L0_attn_conv_b, L0_attn_W_xp, L0_attn_W_dt, L0_attn_b_dt, L0_attn_A_log,
                L0_attn_D, L0_attn_W_out, L0_ffn_gamma1, L0_ffn_Wq, L0_ffn_bq,
                L0_ffn_Wdown, L0_ffn_bdown, L0_ffn_Wup, L0_ffn_bup, L0_ffn_Wo,
                L0_ffn_bo, L0_ffn_gamma2, L0_ffn_Wg, L0_ffn_bg, L0_ffn_We):
    f32 = lambda a: np.ascontiguousarray(np.asarray(a, dtype=np.float32))
    perm_k = np.array([h * 128 + hd for h in range(H) for hd in range(HD)])
    perm_v = perm_k + HD
    Wup = f32(L0_ffn_Wup)
    bup = f32(L0_ffn_bup)
    common = {
        "winT": f32(np.asarray(L0_attn_W_in).T),
        "wxpT": f32(np.asarray(L0_attn_W_xp).T),
        "wdtT": f32(np.asarray(L0_attn_W_dt).T),
        "woutT": f32(np.asarray(L0_attn_W_out).T),
        "wq": f32(L0_ffn_Wq),
        "wdown": f32(L0_ffn_Wdown),
        "wupk": np.ascontiguousarray(Wup[:, perm_k]),
        "wupv": np.ascontiguousarray(Wup[:, perm_v]),
        "wo": f32(L0_ffn_Wo),
        "wg": f32(L0_ffn_Wg),
        "we": f32(L0_ffn_We),
        "alog": f32(L0_attn_A_log),
        "dvec": f32(L0_attn_D),
        "convw": f32(L0_attn_conv_w),
        "convb": f32(L0_attn_conv_b),
        "bdt": f32(L0_attn_b_dt),
        "lng": f32(L0_attn_ln_g),
        "lnb": f32(L0_attn_ln_b),
        "g1": f32(L0_ffn_gamma1),
        "g2": f32(L0_ffn_gamma2),
        "bq": f32(L0_ffn_bq),
        "bdown": f32(L0_ffn_bdown),
        "bupk": np.ascontiguousarray(bup[perm_k]),
        "bupv": np.ascontiguousarray(bup[perm_v]),
        "bo": f32(L0_ffn_bo),
        "bg": f32(L0_ffn_bg),
        "eye32": np.eye(128, dtype=np.float32),
        "eyebf": np.eye(128, dtype=ml_dtypes.bfloat16),
    }
    return common


def kernel(L0_attn_x, **kw):
    x = np.ascontiguousarray(np.asarray(L0_attn_x, dtype=np.float32))
    common = host_common(**kw)
    in_maps = []
    for c in range(8):
        m = dict(common)
        m["x"] = np.ascontiguousarray(x[2 * c:2 * c + 2])
        in_maps.append(m)
    nc = _get_nc()
    res = run_bass_kernel_spmd(nc, in_maps, core_ids=list(range(8)))
    return np.concatenate([res.results[c]["out"] for c in range(8)], axis=0)
